# version 09
# baseline (speedup 1.0000x reference)
import numpy as np
from contextlib import ExitStack
import concourse.bass as bass
import concourse.mybir as mybir
from concourse.bass_utils import run_bass_kernel_spmd
from concourse.bass import IndirectOffsetOnAxis

F32 = mybir.dt.float32
BF16 = mybir.dt.bfloat16
U32 = mybir.dt.uint32
AF = mybir.ActivationFunctionType
ALU = mybir.AluOpType
AX = mybir.AxisListType

NCORES = 8
S = 2048
D = 1024
BLOC = 2
T = BLOC * S
DC = D // 128
NTT = T // 128
NTG = T // 512
H_FFN = 4096
HC_FFN = H_FFN // 128
EPS = 1e-5
LN_EPS = 1e-5

E_MOE = 16
K_MOE = 2
CAP_GLOBAL = 5120
CL = 1024
NSLOT = E_MOE * CL

NH = 16
HD = 64
GATE_CAP = 15.0
CHUNK = 128
NCH = S // CHUNK


class _CntView:

    def __init__(self, prog):
        self.p = prog

    def __getitem__(self, name):
        self.p.sem(name)
        return self.p._physcnt[self.p._name2phys[name]]

    def get(self, name, default=0):
        return self[name]

    def __setitem__(self, name, v):
        self.p._physcnt[self.p._name2phys[name]] = v


class Prog:
    ENGINES = ("sync", "scalar", "vector", "tensor", "gpsimd")
    NPHYS = 96

    def __init__(self, nc):
        self.nc = nc
        self.es = ExitStack()
        self.q = {e: [] for e in self.ENGINES}
        self._phys = [self.es.enter_context(nc.semaphore(f"ps{i}"))
                      for i in range(self.NPHYS)]
        self._physcnt = [0] * self.NPHYS
        self._name2phys = {"bar": 0}
        self._cursor = 1
        self.cnt = _CntView(self)

    def epoch(self):
        self._name2phys = {"bar": 0}
        self._cursor = 1

    def sem(self, name):
        if name not in self._name2phys:
            if self._cursor >= self.NPHYS:
                raise RuntimeError("out of physical semaphores in this epoch")
            self._name2phys[name] = self._cursor
            self._cursor += 1
        return self._phys[self._name2phys[name]]

    def sbuf(self, name, shape, dtype=F32, stack=None):
        return (stack or self.es).enter_context(self.nc.sbuf_tensor(name, shape, dtype))

    def psum(self, name, shape, dtype=F32, stack=None):
        return (stack or self.es).enter_context(self.nc.psum_tensor(name, shape, dtype))

    def do(self, eng, fn, inc=None, n=1):
        if inc is not None:
            s = self.sem(inc)
            idx = self._name2phys[inc]
            self._physcnt[idx] += n
            def wrapped(e, fn=fn, s=s, n=n):
                fn(e).then_inc(s, n)
            self.q[eng].append(wrapped)
            return self._physcnt[idx]
        self.q[eng].append(fn)
        return None

    def dma(self, eng, out, in_, inc):
        return self.do(eng, lambda e, o=out, i=in_: e.dma_start(out=o, in_=i),
                       inc=inc, n=16)

    def wait(self, eng, sem, val=None):
        s = self.sem(sem)
        if val is None:
            val = self.cnt[sem]
        if val <= 0:
            return
        self.q[eng].append(lambda e, s=s, v=val: e.wait_ge(s, v))

    def mark(self, sem):
        self.sem(sem)
        return self.cnt[sem]

    def barrier_epoch(self):
        self.barrier()
        self.epoch()

    def barrier(self, engines=None):
        engines = list(engines or self.ENGINES)
        for eng in engines:
            if eng in ("vector", "scalar", "tensor", "gpsimd"):
                self.q[eng].append(lambda e: e.drain())
            self.do(eng, lambda e: e.nop(), inc="bar")
        tgt = self.cnt["bar"]
        for eng in engines:
            self.wait(eng, "bar", tgt)

    def build(self):
        nc = self.nc
        with nc.Block() as block:
            for engname in self.ENGINES:
                if not self.q[engname]:
                    continue
                def body(eng, engname=engname):
                    for fn in self.q[engname]:
                        fn(eng)
                getattr(block, engname)(body)
        self.es.close()
        return nc


CONST_COL_VALS = (LN_EPS, 1e-6, 1.0, -12.0, 0.5, 2147483648.0, float(NSLOT), 1.0 / 128.0)


class Consts:
    def __init__(self, p: Prog):
        self.p = p
        self.cpool = p.sbuf("c_pool", [128, len(CONST_COL_VALS)], F32)
        self.ones_bf = p.sbuf("c_ones_bf", [128, 128], BF16)
        self.ones_f = p.sbuf("c_ones_f", [128, 128], F32)
        self.ident_bf = p.sbuf("c_id_bf", [128, 128], BF16)
        self.ident_f = p.sbuf("c_id_f", [128, 128], F32)
        self.iota_col = p.sbuf("c_iotac", [128, 1], F32)
        self.iota_row16 = p.sbuf("c_iotar16", [128, 16], F32)
        self.triu_excl_bf = p.sbuf("c_triux", [128, 128], BF16)
        self.triu_excl_f = p.sbuf("c_triuxf", [128, 128], F32)
        self.triu_incl_f = p.sbuf("c_triuif", [128, 128], F32)
        self.iota_row32 = p.sbuf("c_iotar32", [128, 32], F32)
        self.mask_sq = p.sbuf("c_masksq", [128, 128], F32)

    def col(self, v, parts=128):
        i = CONST_COL_VALS.index(v)
        return self.cpool[0:parts, i:i + 1]

    def emit(self):
        p = self.p
        es = ExitStack()
        for i, v in enumerate(CONST_COL_VALS):
            p.do("vector", lambda e, i=i, v=v:
                 e.memset(self.cpool[:, i:i + 1], float(v)), inc="cst")
        s = p.sbuf("c_scr", [128, 128], F32, stack=es)
        s2 = p.sbuf("c_scr2", [128, 128], F32, stack=es)
        p.do("gpsimd", lambda e: e.iota(self.iota_col[:, :], pattern=[[0, 1]],
                                        base=0, channel_multiplier=1,
                                        allow_small_or_imprecise_dtypes=True), inc="cst")
        p.do("gpsimd", lambda e: e.iota(self.iota_row16[:, :], pattern=[[1, 16]],
                                        base=0, channel_multiplier=0,
                                        allow_small_or_imprecise_dtypes=True), inc="cst")
        p.do("gpsimd", lambda e: e.iota(self.iota_row32[:, :], pattern=[[1, 32]],
                                        base=0, channel_multiplier=0,
                                        allow_small_or_imprecise_dtypes=True), inc="cst")
        p.do("gpsimd", lambda e: e.iota(s[:, :], pattern=[[1, 128]],
                                        base=0, channel_multiplier=0,
                                        allow_small_or_imprecise_dtypes=True), inc="cst")
        p.wait("vector", "cst")
        p.do("vector", lambda e: e.memset(self.ones_bf[:, :], 1.0), inc="cst")
        p.do("vector", lambda e: e.memset(self.ones_f[:, :], 1.0), inc="cst")
        p.do("vector", lambda e: e.tensor_scalar(self.ident_f[:, :], s[:, :],
                                                 self.iota_col[:, 0:1], None,
                                                 op0=ALU.is_equal), inc="cst")
        p.wait("vector", "cst")
        p.do("vector", lambda e: e.tensor_copy(self.ident_bf[:, :], self.ident_f[:, :]),
             inc="cst")
        p.do("vector", lambda e: e.tensor_scalar(s2[:, :], s[:, :],
                                                 self.iota_col[:, 0:1], None,
                                                 op0=ALU.is_gt), inc="cst")
        p.wait("vector", "cst")
        p.do("vector", lambda e: e.tensor_copy(self.triu_excl_bf[:, :], s2[:, :]),
             inc="cst")
        p.do("vector", lambda e: e.tensor_copy(self.triu_excl_f[:, :], s2[:, :]),
             inc="cst")
        p.wait("vector", "cst")
        p.do("vector", lambda e: e.tensor_scalar(s2[:, :], s[:, :],
                                                 self.iota_col[:, 0:1], None,
                                                 op0=ALU.is_ge), inc="cst")
        p.wait("vector", "cst")
        p.do("vector", lambda e: e.tensor_copy(self.triu_incl_f[:, :], s2[:, :]),
             inc="cst")
        p.wait("vector", "cst")
        p.do("vector", lambda e: e.tensor_scalar(s2[:, :], s[:, :],
                                                 self.iota_col[:, 0:1], None,
                                                 op0=ALU.is_lt), inc="cst")
        p.wait("vector", "cst")
        p.do("vector", lambda e: e.tensor_scalar(self.mask_sq[:, :], s2[:, :],
                                                 -1e30, None, op0=ALU.mult), inc="cst")
        p.wait("vector", "cst")
        p.barrier()
        es.close()


class Stream:
    def __init__(self, p: Prog, c: Consts, cols=T, stack=None):
        self.p = p
        self.c = c
        self.cols = cols
        Stream._n = getattr(Stream, "_n", 0) + 1
        n = Stream._n
        self.resid = [p.sbuf(f"resid{n}_{dc}", [128, cols], F32, stack=stack)
                      for dc in range(DC)]
        self.nln = 0

    def alloc_lnx(self, stack):
        self.nln += 1
        return [self.p.sbuf(f"lnx{self.nln}_{dc}", [128, T], BF16, stack=stack)
                for dc in range(DC)]


def emit_load_resid(p: Prog, c: Consts, st: Stream, x_ap, t0=0, t1=None,
                    psums=None):
    t1 = T if t1 is None else t1
    es = ExitStack()
    Stream._ld = getattr(Stream, "_ld", 0) + 1
    pre = f"ld{Stream._ld}"
    xtm = [p.sbuf(f"{pre}_xtm{i}", [128, D], F32, stack=es) for i in range(2)]
    pst = (psums if psums is not None else
           [p.psum(f"{pre}_pst{i}", [128, 128], F32, stack=es) for i in range(4)])
    tp_marks = []
    ev_marks = []
    dma_marks = []
    ntt = (t1 - t0) // 128
    for tt in range(ntt):
        buf = xtm[tt % 2]
        if tt >= 2:
            p.wait("gpsimd", f"{pre}_tp", tp_marks[(tt - 2) * DC + DC - 1])
        dma_marks.append(p.dma("gpsimd", buf[:, :],
                               x_ap[t0 + tt * 128:t0 + (tt + 1) * 128, :],
                               inc=f"{pre}_x"))
        for dc in range(DC):
            n = tt * DC + dc
            bank = n % 4
            p.wait("tensor", f"{pre}_x", dma_marks[tt])
            if n >= 4:
                p.wait("tensor", f"{pre}_ev", ev_marks[n - 4])
            tp_marks.append(p.do("tensor", lambda e, b=buf, dc=dc, bank=bank:
                 e.transpose(pst[bank][:, :],
                             b[:, dc * 128:(dc + 1) * 128], c.ident_f[:, :]),
                 inc=f"{pre}_tp"))
            p.wait("vector", f"{pre}_tp", tp_marks[-1])
            ev_marks.append(p.do("vector", lambda e, dc=dc, tt=tt, bank=bank:
                 e.tensor_copy(st.resid[dc][:, tt * 128:(tt + 1) * 128],
                               pst[bank][:, :]),
                 inc=f"{pre}_ev"))
    p.barrier()
    es.close()


def emit_store_out(p: Prog, c: Consts, st: Stream, out_ap, t0=0, t1=None,
                   psums=None):
    t1 = T if t1 is None else t1
    es = ExitStack()
    Stream._st = getattr(Stream, "_st", 0) + 1
    pre = f"st{Stream._st}"
    otm = [p.sbuf(f"{pre}_otm{i}", [128, D], F32, stack=es) for i in range(2)]
    psf = (psums if psums is not None else
           [p.psum(f"{pre}_ps{i}", [128, 128], F32, stack=es) for i in range(2)])
    out_marks = []
    ntt = (t1 - t0) // 128
    for tt in range(ntt):
        buf = otm[tt % 2]
        if tt >= 2:
            p.wait("vector", f"{pre}_out", out_marks[tt - 2])
        for dc in range(DC):
            bank = dc % 2
            p.wait("tensor", f"{pre}_st")
            p.do("tensor", lambda e, dc=dc, tt=tt, bank=bank:
                 e.transpose(psf[bank][:, :],
                             st.resid[dc][:, tt * 128:(tt + 1) * 128],
                             c.ident_f[:, :]), inc=f"{pre}_tp")
            p.wait("vector", f"{pre}_tp")
            p.do("vector", lambda e, b=buf, dc=dc, bank=bank:
                 e.tensor_copy(b[:, dc * 128:(dc + 1) * 128],
                               psf[bank][:, :]), inc=f"{pre}_st")
        p.wait("sync", f"{pre}_st")
        out_marks.append(p.dma("sync", out_ap[t0 + tt * 128:t0 + (tt + 1) * 128, :],
                               buf[:, :], inc=f"{pre}_out"))
    p.wait("sync", f"{pre}_out")
    p.barrier()
    es.close()


def emit_ln(p: Prog, c: Consts, st: Stream, lnx, rms2=False, name="ln",
            router=None, lgall=None, tg0=0, tg1=NTG, col0=0, psums=None,
            r2out=None, rcol0=None):
    es = ExitStack()
    sq = p.sbuf(f"{name}_sq", [128, 512], BF16, stack=es)
    xbf = p.sbuf(f"{name}_xbf", [128, 512], BF16, stack=es)
    rowm = p.sbuf(f"{name}_rm", [1, 512], F32, stack=es)
    rowv = p.sbuf(f"{name}_rv", [1, 512], F32, stack=es)
    rowr = p.sbuf(f"{name}_rr", [1, 512], F32, stack=es)
    rowt = p.sbuf(f"{name}_rt", [1, 512], F32, stack=es)
    rowu = p.sbuf(f"{name}_ru", [1, 512], F32, stack=es)
    tmp = p.sbuf(f"{name}_tmp", [128, 512], F32, stack=es)
    tmpf = (p.sbuf(f"{name}_tmpf", [128, 512], F32, stack=es)
            if router is not None else None)
    if psums is None:
        ps_s = p.psum(f"{name}_ps", [128, 512], F32, stack=es)
        ps_q = p.psum(f"{name}_pq", [128, 512], F32, stack=es)
        ps_br = p.psum(f"{name}_br", [128, 512], F32, stack=es)
        ps_bm = p.psum(f"{name}_bm", [128, 512], F32, stack=es)
    else:
        ps_s, ps_q, ps_br, ps_bm = psums
    ps_lg = [p.psum(f"{name}_lg{i}", [128, 16], F32, stack=es)
             for i in range(4)] if router is not None else None
    if rcol0 is None:
        rcol0 = col0
    sqmm = 0
    for g in range(tg0, tg1):
        sl = slice(g * 512, (g + 1) * 512)
        sls = slice(g * 512 - col0, (g + 1) * 512 - col0)
        rsl = slice(g * 512 - rcol0, (g + 1) * 512 - rcol0)
        for dc in range(DC):
            p.wait("vector", f"{name}_bc")
            p.wait("vector", f"{name}_mm", sqmm)
            p.do("vector", lambda e, dc=dc, rsl=rsl:
                 e.tensor_tensor(sq[:, :], st.resid[dc][:, rsl], st.resid[dc][:, rsl],
                                 ALU.mult), inc=f"{name}_sqd")
            p.do("vector", lambda e, dc=dc, rsl=rsl:
                 e.tensor_copy(xbf[:, :], st.resid[dc][:, rsl]), inc=f"{name}_sqd")
            p.wait("tensor", f"{name}_sqd")
            p.do("tensor", lambda e, dc=dc:
                 e.matmul(ps_s[0:1, :], c.ones_bf[:, 0:1], xbf[:, :],
                          start=(dc == 0), stop=(dc == DC - 1)), inc=f"{name}_mm")
            sqmm = p.do("tensor", lambda e, dc=dc:
                 e.matmul(ps_q[0:1, :], c.ones_bf[:, 0:1], sq[:, :],
                          start=(dc == 0), stop=(dc == DC - 1)), inc=f"{name}_mm")
        p.wait("vector", f"{name}_mm")
        p.do("vector", lambda e: e.tensor_scalar(rowm[:, :], ps_s[0:1, :],
                                                 1.0 / D, None, op0=ALU.mult),
             inc=f"{name}_r")
        p.do("vector", lambda e: e.tensor_scalar(rowv[:, :], ps_q[0:1, :],
                                                 1.0 / D, None, op0=ALU.mult),
             inc=f"{name}_r")
        p.wait("vector", f"{name}_r")
        p.do("vector", lambda e: e.tensor_tensor(rowt[:, :], rowm[:, :], rowm[:, :],
                                                 ALU.mult), inc=f"{name}_r")
        p.wait("vector", f"{name}_r")
        p.do("vector", lambda e: e.tensor_tensor(rowv[:, :], rowv[:, :], rowt[:, :],
                                                 ALU.subtract), inc=f"{name}_r")
        p.wait("scalar", f"{name}_r")
        p.do("scalar", lambda e: e.activation(rowt[:, :], rowv[:, :], AF.Ln,
                                              bias=c.col(LN_EPS, 1)), inc=f"{name}_a")
        p.wait("scalar", f"{name}_a")
        p.do("scalar", lambda e: e.activation(rowr[:, :], rowt[:, :], AF.Exp,
                                              scale=-0.5), inc=f"{name}_a")
        if rms2:
            p.wait("vector", f"{name}_a")
            p.do("vector", lambda e: e.tensor_scalar(rowt[:, :], rowv[:, :],
                                                     LN_EPS, None, op0=ALU.add),
                 inc=f"{name}_r")
            p.wait("vector", f"{name}_r")
            p.do("vector", lambda e: e.reciprocal(rowu[:, :], rowt[:, :]),
                 inc=f"{name}_r")
            p.wait("vector", f"{name}_r")
            p.do("vector", lambda e: e.tensor_tensor(rowu[:, :], rowu[:, :],
                                                     rowv[:, :], ALU.mult),
                 inc=f"{name}_r")
            p.wait("scalar", f"{name}_r")
            p.do("scalar", lambda e: e.activation(rowt[:, :], rowu[:, :], AF.Ln,
                                                  bias=c.col(LN_EPS, 1)), inc=f"{name}_a")
            p.wait("scalar", f"{name}_a")
            p.do("scalar", lambda e: e.activation(rowu[:, :], rowt[:, :], AF.Exp,
                                                  scale=-0.5), inc=f"{name}_a")
            p.wait("vector", f"{name}_a")
            p.do("vector", lambda e, sls=sls: e.tensor_copy(r2out[0:1, sls],
                                                            rowu[:, :]),
                 inc=f"{name}_r")
        p.wait("vector", f"{name}_a")
        p.wait("vector", f"{name}_r")
        p.do("vector", lambda e: e.tensor_tensor(rowm[:, :], rowm[:, :], rowr[:, :],
                                                 ALU.mult), inc=f"{name}_r")
        p.wait("tensor", f"{name}_r")
        p.do("tensor", lambda e: e.matmul(ps_br[:, :], c.ones_f[0:1, :], rowr[:, :],
                                          start=True, stop=True), inc=f"{name}_bcm")
        p.do("tensor", lambda e: e.matmul(ps_bm[:, :], c.ones_f[0:1, :], rowm[:, :],
                                          start=True, stop=True), inc=f"{name}_bcm")
        p.wait("vector", f"{name}_bcm")
        for dc in range(DC):
            p.do("vector", lambda e, dc=dc, rsl=rsl:
                 e.tensor_tensor(tmp[:, :], st.resid[dc][:, rsl], ps_br[:, :],
                                 ALU.mult), inc=f"{name}_bc")
            p.wait("vector", f"{name}_bc")
            if router is not None:
                p.wait("vector", f"{name}_lgm")
                p.do("vector", lambda e, dc=dc, sl=sl:
                     e.tensor_tensor(tmpf[:, :], tmp[:, :], ps_bm[:, :],
                                     ALU.subtract), inc=f"{name}_bc")
                p.wait("vector", f"{name}_bc")
                p.do("vector", lambda e, dc=dc, sls=sls:
                     e.tensor_copy(lnx[dc][:, sls], tmpf[:, :]), inc=f"{name}_bc")
                p.wait("tensor", f"{name}_bc")
                for ts4 in range(4):
                    tile = g * 4 + ts4
                    p.do("tensor", lambda e, dc=dc, ts4=ts4, tile=tile:
                         e.matmul(ps_lg[ts4][:, :],
                                  tmpf[:, ts4 * 128:(ts4 + 1) * 128],
                                  router[:, dc, :],
                                  start=(dc == 0),
                                  stop=(dc == DC - 1)),
                         inc=f"{name}_lgm")
            else:
                p.do("vector", lambda e, dc=dc, sls=sls:
                     e.tensor_tensor(lnx[dc][:, sls], tmp[:, :], ps_bm[:, :],
                                     ALU.subtract), inc=f"{name}_bc")
        if router is not None:
            p.wait("vector", f"{name}_lgm")
            for ts4 in range(4):
                tile = g * 4 + ts4
                p.do("vector", lambda e, ts4=ts4, tile=tile:
                     e.tensor_copy(lgall[:, tile * 16:(tile + 1) * 16],
                                   ps_lg[ts4][:, :]),
                     inc=f"{name}_bc")
            p.wait("tensor", f"{name}_bc")
    p.barrier()
    es.close()


def emit_quant(p: Prog, c: Consts, w_ap, q_ap, nrow, ncol, name):
    es = ExitStack()
    nrt = nrow // 128
    wt = [p.sbuf(f"{name}_w{i}", [128, ncol], F32, stack=es) for i in range(2)]
    ab = p.sbuf(f"{name}_ab", [128, ncol], F32, stack=es)
    sg = p.sbuf(f"{name}_sg", [128, ncol], F32, stack=es)
    qo = p.sbuf(f"{name}_qo", [128, ncol], BF16, stack=es)
    acc = p.sbuf(f"{name}_acc", [128, nrt], F32, stack=es)
    red = p.sbuf(f"{name}_red", [128, 1], F32, stack=es)
    thr = p.sbuf(f"{name}_thr", [128, 1], F32, stack=es)
    pst = p.psum(f"{name}_ps", [128, 512], F32, stack=es)
    a_marks = []
    for i in range(nrt):
        if i >= 2:
            p.wait("sync", f"{name}_a", a_marks[i - 2])
        m = p.dma("sync", wt[i % 2][:, :], w_ap[i * 128:(i + 1) * 128, :],
                  inc=f"{name}_l")
        p.wait("scalar", f"{name}_l", m)
        a_marks.append(p.do("scalar", lambda e, i=i:
             e.activation(ab[:, :], wt[i % 2][:, :], AF.Abs,
                          accum_out=acc[:, i:i + 1]), inc=f"{name}_a"))
    p.wait("vector", f"{name}_a")
    p.do("vector", lambda e: e.tensor_reduce(red[:, :], acc[:, 0:nrt], AX.X, ALU.add),
         inc=f"{name}_r")
    p.wait("tensor", f"{name}_r")
    p.do("tensor", lambda e: e.matmul(pst[0:1, 0:1], c.ones_f[:, 0:1], red[:, :],
                                      start=True, stop=True), inc=f"{name}_mm")
    p.wait("vector", f"{name}_mm")
    p.do("vector", lambda e: e.tensor_scalar(red[0:1, 0:1], pst[0:1, 0:1],
                                             0.5 / (nrow * ncol), 0.5e-6,
                                             op0=ALU.mult, op1=ALU.add), inc=f"{name}_r")
    p.wait("tensor", f"{name}_r")
    p.do("tensor", lambda e: e.matmul(pst[:, 2:3], c.ones_f[0:1, :], red[0:1, 0:1],
                                      start=True, stop=True), inc=f"{name}_mm")
    p.wait("vector", f"{name}_mm")
    p.do("vector", lambda e: e.tensor_copy(thr[:, :], pst[:, 2:3]), inc=f"{name}_r")
    p.wait("scalar", f"{name}_r")
    s_marks = []
    for i in range(nrt):
        if i >= 2:
            p.wait("sync", f"{name}_r2", s_marks[i - 2][0])
        m = p.dma("sync", wt[i % 2][:, :], w_ap[i * 128:(i + 1) * 128, :],
                  inc=f"{name}_l")
        p.wait("scalar", f"{name}_l", m)
        if i >= 1:
            p.wait("scalar", f"{name}_r2", s_marks[i - 1][0])
        p.do("scalar", lambda e, i=i: e.activation(ab[:, :], wt[i % 2][:, :], AF.Abs),
             inc=f"{name}_a2")
        p.do("scalar", lambda e, i=i: e.activation(sg[:, :], wt[i % 2][:, :], AF.Sign),
             inc=f"{name}_a2")
        p.wait("vector", f"{name}_a2")
        if i >= 1:
            p.wait("vector", f"{name}_sc", s_marks[i - 1][1])
        p.do("vector", lambda e: e.tensor_scalar(ab[:, :], ab[:, :], thr[:, 0:1],
                                                 None, op0=ALU.is_gt), inc=f"{name}_r2")
        p.wait("vector", f"{name}_r2")
        mr = p.do("vector", lambda e: e.tensor_tensor(qo[:, :], ab[:, :], sg[:, :],
                                                      ALU.mult), inc=f"{name}_r2")
        p.wait("gpsimd", f"{name}_r2", mr)
        src = qo[:, :].rearrange("p (mc m) -> p mc m", m=128)
        dst = q_ap[:, i, :, :].rearrange("mc p m -> p mc m")
        ms = p.dma("gpsimd", dst, src, inc=f"{name}_sc")
        s_marks.append((mr, ms))
    p.wait("gpsimd", f"{name}_sc")
    p.barrier()
    es.close()


def emit_l1ffn(p: Prog, c: Consts, st: Stream, lnx, qfc_ap, qpr_ap, lnw_ap, lnb_ap, tag,
               tg0=0, tg1=NTG, col0=0):
    es = ExitStack()
    wfc = [p.sbuf(f"{tag}_wfc{i}", [128, DC, 128], BF16, stack=es) for i in range(2)]
    wpr = [p.sbuf(f"{tag}_wpr{i}", [128, DC, 128], BF16, stack=es) for i in range(2)]
    hsb = [p.sbuf(f"{tag}_h{i}", [128, 512], F32, stack=es) for i in range(HC_FFN)]
    hbf = [p.sbuf(f"{tag}_hb{i}", [128, 512], BF16, stack=es) for i in range(2)]
    lnw = p.sbuf(f"{tag}_lnw", [128, HC_FFN], F32, stack=es)
    lnb = p.sbuf(f"{tag}_lnb", [128, HC_FFN], F32, stack=es)
    sq = p.sbuf(f"{tag}_sq", [128, 512], BF16, stack=es)
    hcast = p.sbuf(f"{tag}_hcast", [128, 512], BF16, stack=es)
    tmp = p.sbuf(f"{tag}_tmp", [128, 512], F32, stack=es)
    rowm = p.sbuf(f"{tag}_rm", [1, 512], F32, stack=es)
    rowv = p.sbuf(f"{tag}_rv", [1, 512], F32, stack=es)
    rowr = p.sbuf(f"{tag}_rr", [1, 512], F32, stack=es)
    rowt = p.sbuf(f"{tag}_rt", [1, 512], F32, stack=es)
    brc = p.sbuf(f"{tag}_brc", [128, 512], F32, stack=es)
    bmc = p.sbuf(f"{tag}_bmc", [128, 512], F32, stack=es)
    pj = [p.psum(f"{tag}_pj{i}", [128, 512], F32, stack=es) for i in range(8)]
    p.dma("sync", lnw[:, :], lnw_ap[:, :], inc=f"{tag}_lw")
    p.dma("sync", lnb[:, :], lnb_ap[:, :], inc=f"{tag}_lw")
    p.wait("vector", f"{tag}_lw")

    fcmm_marks = {}
    ge_marks = {}
    for g in range(tg0, tg1):
        sl = slice(g * 512 - col0, (g + 1) * 512 - col0)
        p.barrier()
        fcw_marks = [0, 0]
        ge_marks = [0, 0]
        for hc in range(HC_FFN):
            wb = wfc[hc % 2]
            p.wait("sync", f"{tag}_fcmm")
            wlm = p.dma("sync", wb[:, :, :].rearrange("p kc m -> p kc m"),
                  qfc_ap[hc, :, :, :].rearrange("kc p m -> p kc m"), inc=f"{tag}_wl")
            p.wait("tensor", f"{tag}_wl", wlm)
            p.wait("tensor", f"{tag}_ge", ge_marks[hc % 2])
            pb = pj[hc % 2]
            for kc in range(DC):
                fcw_marks[hc % 2] = p.do("tensor",
                     lambda e, wb=wb, kc=kc, sl=sl, pb=pb:
                     e.matmul(pb[:, :], wb[:, kc, :], lnx[kc][:, sl],
                              start=(kc == 0), stop=(kc == DC - 1)),
                     inc=f"{tag}_fcmm")
            p.wait("scalar", f"{tag}_fcmm", fcw_marks[hc % 2])
            ge_marks[hc % 2] = p.do("scalar", lambda e, hc=hc, pb=pb:
                 e.activation(hsb[hc][:, :], pb[:, :], AF.Gelu), inc=f"{tag}_ge")
            p.wait("vector", f"{tag}_ge")
            p.wait("vector", f"{tag}_st", p.cnt.get(f"{tag}_st", 0))
            p.do("vector", lambda e, hc=hc:
                 e.tensor_tensor(sq[:, :], hsb[hc][:, :], hsb[hc][:, :], ALU.mult),
                 inc=f"{tag}_sqd")
            p.do("vector", lambda e, hc=hc:
                 e.tensor_copy(hcast[:, :], hsb[hc][:, :]), inc=f"{tag}_sqd")
            p.wait("tensor", f"{tag}_sqd")
            p.do("tensor", lambda e, hc=hc:
                 e.matmul(pj[2][0:1, :], c.ones_bf[:, 0:1], hcast[:, :],
                          start=(hc == 0), stop=(hc == HC_FFN - 1)), inc=f"{tag}_st")
            p.do("tensor", lambda e, hc=hc:
                 e.matmul(pj[3][0:1, :], c.ones_bf[:, 0:1], sq[:, :],
                          start=(hc == 0), stop=(hc == HC_FFN - 1)), inc=f"{tag}_st")
        p.barrier()
        p.wait("vector", f"{tag}_st")
        p.do("vector", lambda e: e.tensor_scalar(rowm[:, :], pj[2][0:1, :],
                                                 1.0 / H_FFN, None, op0=ALU.mult),
             inc=f"{tag}_r")
        p.do("vector", lambda e: e.tensor_scalar(rowv[:, :], pj[3][0:1, :],
                                                 1.0 / H_FFN, None, op0=ALU.mult),
             inc=f"{tag}_r")
        p.wait("vector", f"{tag}_r")
        p.do("vector", lambda e: e.tensor_tensor(rowt[:, :], rowm[:, :], rowm[:, :],
                                                 ALU.mult), inc=f"{tag}_r")
        p.wait("vector", f"{tag}_r")
        p.do("vector", lambda e: e.tensor_tensor(rowv[:, :], rowv[:, :], rowt[:, :],
                                                 ALU.subtract), inc=f"{tag}_r")
        p.wait("scalar", f"{tag}_r")
        p.do("scalar", lambda e: e.activation(rowt[:, :], rowv[:, :], AF.Ln,
                                              bias=c.col(LN_EPS, 1)), inc=f"{tag}_a")
        p.wait("scalar", f"{tag}_a")
        p.do("scalar", lambda e: e.activation(rowr[:, :], rowt[:, :], AF.Exp,
                                              scale=-0.5), inc=f"{tag}_a")
        p.wait("vector", f"{tag}_a")
        p.do("vector", lambda e: e.tensor_tensor(rowm[:, :], rowm[:, :], rowr[:, :],
                                                 ALU.mult), inc=f"{tag}_r")
        p.wait("tensor", f"{tag}_r")
        p.do("tensor", lambda e: e.matmul(pj[2][:, :], c.ones_f[0:1, :], rowr[:, :],
                                          start=True, stop=True), inc=f"{tag}_bcm")
        p.wait("vector", f"{tag}_bcm")
        p.do("vector", lambda e: e.tensor_copy(brc[:, :], pj[2][:, :]), inc=f"{tag}_r")
        p.wait("tensor", f"{tag}_r")
        p.do("tensor", lambda e: e.matmul(pj[2][:, :], c.ones_f[0:1, :], rowm[:, :],
                                          start=True, stop=True), inc=f"{tag}_bcm")
        p.wait("vector", f"{tag}_bcm")
        p.do("vector", lambda e: e.tensor_copy(bmc[:, :], pj[2][:, :]), inc=f"{tag}_r")
        p.wait("vector", f"{tag}_r")
        for hc in range(HC_FFN):
            p.do("vector", lambda e, hc=hc:
                 e.tensor_tensor(tmp[:, :], hsb[hc][:, :], brc[:, :], ALU.mult),
                 inc=f"{tag}_ap")
            p.wait("vector", f"{tag}_ap")
            p.do("vector", lambda e, hc=hc:
                 e.tensor_tensor(tmp[:, :], tmp[:, :], bmc[:, :], ALU.subtract),
                 inc=f"{tag}_ap")
            p.wait("vector", f"{tag}_ap")
            p.do("vector", lambda e, hc=hc:
                 e.tensor_scalar(hsb[hc][:, :], tmp[:, :], lnw[:, hc:hc + 1],
                                 lnb[:, hc:hc + 1], op0=ALU.mult, op1=ALU.add),
                 inc=f"{tag}_ap")
            p.wait("vector", f"{tag}_ap")
            p.wait("scalar", f"{tag}_ap")
        p.barrier()
        p.wait("tensor", f"{tag}_ap")
        hb_marks = [0, 0]
        for hc in range(HC_FFN):
            wb = wpr[hc % 2]
            p.wait("sync", f"{tag}_pjmm")
            wpm = p.dma("sync", wb[:, :, :],
                  qpr_ap[:, hc, :, :].rearrange("dc p m -> p dc m"), inc=f"{tag}_wp")
            hb = hbf[hc % 2]
            p.wait("vector", f"{tag}_pjmm", hb_marks[hc % 2])
            hbm = p.do("vector", lambda e, hb=hb, hc=hc:
                 e.tensor_copy(hb[:, :], hsb[hc][:, :]), inc=f"{tag}_hbc")
            p.wait("tensor", f"{tag}_wp", wpm)
            p.wait("tensor", f"{tag}_hbc", hbm)
            for dc in range(DC):
                hb_marks[hc % 2] = p.do("tensor", lambda e, wb=wb, dc=dc, hc=hc, hb=hb:
                     e.matmul(pj[dc][:, :], wb[:, dc, :], hb[:, :],
                              start=(hc == 0), stop=(hc == HC_FFN - 1)),
                     inc=f"{tag}_pjmm")
        p.wait("vector", f"{tag}_pjmm")
        for dc in range(DC):
            p.do("vector", lambda e, dc=dc, sl=sl:
                 e.scalar_tensor_tensor(st.resid[dc][:, sl], pj[dc][:, :], 0.5,
                                        st.resid[dc][:, sl], op0=ALU.mult, op1=ALU.add),
                 inc=f"{tag}_ev")
        p.wait("tensor", f"{tag}_ev")
    p.barrier()
    es.close()




def emit_l0ffn_route(p: Prog, c: Consts, lnx, lgall, aps, tag,
                     srcst, wst):
    CAPF = float(CAP_GLOBAL)
    CLF = float(CL)
    es = ExitStack()
    ohcat = p.sbuf(f"{tag}_oh", [128, NTT * 32], F32, stack=es)
    Ast = p.sbuf(f"{tag}_A", [128, NTT * 16], F32, stack=es)
    i12f = p.sbuf(f"{tag}_i12", [128, NTT * 2], F32, stack=es)
    keepst = p.sbuf(f"{tag}_kp", [128, NTT * 2], F32, stack=es)
    destst = p.sbuf(f"{tag}_dst", [128, NTT * 2], U32, stack=es)
    ix8 = p.sbuf(f"{tag}_ix8", [128, 8], U32, stack=es)
    junk = p.sbuf(f"{tag}_junk", [128, 16], F32, stack=es)
    scr = p.sbuf(f"{tag}_scr", [128, 16], F32, stack=es)
    col = p.sbuf(f"{tag}_col", [128, 12], F32, stack=es)
    sm32 = p.sbuf(f"{tag}_s32", [32, 32], F32, stack=es)
    sm32b = p.sbuf(f"{tag}_s32b", [32, 32], F32, stack=es)
    sm32c = p.sbuf(f"{tag}_s32c", [32, 32], F32, stack=es)
    trsc = p.sbuf(f"{tag}_tr", [32, 32], F32, stack=es)
    trsc2 = p.sbuf(f"{tag}_tr2", [32, 16], F32, stack=es)
    corow = p.sbuf(f"{tag}_cor", [1, 32], F32, stack=es)
    ccsb = p.sbuf(f"{tag}_ccs", [8, 32], F32, stack=es)
    cmsk = p.sbuf(f"{tag}_cm", [8, 1], F32, stack=es)
    eselb = [p.sbuf(f"{tag}_es{i}", [128, 32], F32, stack=es) for i in range(2)]
    eib = [p.sbuf(f"{tag}_ei{i}", [32, 128], F32, stack=es) for i in range(2)]
    p.dma("sync", cmsk[:, :], aps["cmask"][:, :], inc=f"{tag}_cml")
    rbc = p.nc.gpsimd.alloc_register(name=f"{tag}_rbc")
    p.do("gpsimd", lambda e: e.reg_mov(rbc, NSLOT - 1))

    for i in range(NTT):
        lg = lgall[:, i * 16:(i + 1) * 16]
        mx8 = scr[:, 0:8]
        p.wait("vector", f"{tag}_p1s")
        p.do("vector", lambda e, lg=lg, mx8=mx8: e.max(out=mx8, in_=lg),
             inc=f"{tag}_p1v")
        p.wait("vector", f"{tag}_p1v")
        p.do("vector", lambda e, lg=lg, mx8=mx8:
             e.max_index(out=ix8[:, :], in_max=mx8, in_values=lg), inc=f"{tag}_p1v")
        p.wait("vector", f"{tag}_p1v")
        p.do("vector", lambda e, i=i:
             e.tensor_copy(i12f[:, i * 2:i * 2 + 2], ix8[:, 0:2]), inc=f"{tag}_p1v")
        p.do("vector", lambda e: e.tensor_scalar(col[:, 0:1], scr[:, 0:1], -1.0,
                                                 None, op0=ALU.mult), inc=f"{tag}_p1v")
        p.wait("scalar", f"{tag}_p1v")
        p.do("scalar", lambda e, lg=lg:
             e.activation(junk[:, :], lg, AF.Exp, bias=col[:, 0:1],
                          accum_out=col[:, 1:2]), inc=f"{tag}_p1s")
        p.wait("vector", f"{tag}_p1s")
        p.do("vector", lambda e: e.reciprocal(col[:, 2:3], col[:, 1:2]),
             inc=f"{tag}_p1v")
        p.do("vector", lambda e, mx8=mx8:
             e.tensor_tensor(col[:, 3:4], mx8[:, 1:2], mx8[:, 0:1], ALU.subtract),
             inc=f"{tag}_p1v")
        p.wait("scalar", f"{tag}_p1v")
        p.do("scalar", lambda e: e.activation(col[:, 4:5], col[:, 3:4], AF.Exp),
             inc=f"{tag}_p1s")
        p.wait("vector", f"{tag}_p1s")
        p.do("vector", lambda e: e.tensor_tensor(col[:, 5:6], col[:, 4:5],
                                                 col[:, 2:3], ALU.mult),
             inc=f"{tag}_p1v")
        p.wait("vector", f"{tag}_p1v")
        p.do("vector", lambda e, i=i: e.tensor_scalar(wst[:, i * 2:i * 2 + 1],
             col[:, 2:3], 0.5, None, op0=ALU.mult), inc=f"{tag}_p1v")
        p.do("vector", lambda e, i=i: e.tensor_scalar(wst[:, i * 2 + 1:i * 2 + 2],
             col[:, 5:6], 0.5, None, op0=ALU.mult), inc=f"{tag}_p1v")
        p.do("vector", lambda e, i=i: e.tensor_scalar(ohcat[:, i * 32:i * 32 + 16],
             c.iota_row16[:, :], i12f[:, i * 2:i * 2 + 1], None, op0=ALU.is_equal),
             inc=f"{tag}_p1v")
        p.do("vector", lambda e, i=i: e.tensor_scalar(ohcat[:, i * 32 + 16:i * 32 + 32],
             c.iota_row16[:, :], i12f[:, i * 2 + 1:i * 2 + 2], None, op0=ALU.is_equal),
             inc=f"{tag}_p1v")
        p.wait("vector", f"{tag}_p1v")
    p.barrier()

    ps_t = p.psum(f"{tag}_pt", [32, 32], F32, stack=es)
    esel_marks = [0, 0]
    for i in range(NTT):
        eb = eselb[i % 2]
        p.wait("vector", f"{tag}_p2m", esel_marks[i % 2])
        p.do("vector", lambda e, i=i, eb=eb:
             e.tensor_scalar(eb[:, :], c.iota_row32[:, :], float(i), None,
                             op0=ALU.is_equal), inc=f"{tag}_p2v")
        p.wait("tensor", f"{tag}_p2v")
        esel_marks[i % 2] = p.do("tensor", lambda e, i=i, eb=eb:
             e.matmul(ps_t[:, :], eb[:, :], ohcat[:, i * 32:(i + 1) * 32],
                      start=(i == 0), stop=(i == NTT - 1)), inc=f"{tag}_p2m")
    p.wait("vector", f"{tag}_p2m")
    p.do("vector", lambda e: e.tensor_copy(sm32[:, :], ps_t[:, :]), inc=f"{tag}_p2v")
    p.wait("vector", f"{tag}_p2v")
    p.do("vector", lambda e: e.transpose(sm32b[:, :], sm32[:, :]), inc=f"{tag}_p2v")
    p.wait("vector", f"{tag}_p2v")
    p.do("vector", lambda e: e.tensor_tensor_scan(sm32c[:, :], c.ones_f[0:32, 0:32],
         sm32b[:, :], 0.0, ALU.mult, ALU.add), inc=f"{tag}_p2v")
    p.wait("vector", f"{tag}_p2v")
    p.do("vector", lambda e: e.tensor_tensor(sm32b[:, :], sm32c[:, :], sm32b[:, :],
                                             ALU.subtract), inc=f"{tag}_p2v")
    p.wait("vector", f"{tag}_p2v")
    p.do("vector", lambda e: e.transpose(trsc[:, :], sm32b[:, :]), inc=f"{tag}_p2v")
    p.wait("vector", f"{tag}_p2v")
    p.do("vector", lambda e: e.transpose(sm32[:, :], sm32c[:, :]), inc=f"{tag}_p2v")
    p.wait("gpsimd", f"{tag}_p2v")
    p.dma("gpsimd", aps["cc_in"][:, :], sm32[31:32, :], inc=f"{tag}_cc")
    p.wait("gpsimd", f"{tag}_cc")
    p.do("gpsimd", lambda e: e.collective_compute(
        "AllGather", ALU.bypass, replica_groups=[list(range(NCORES))],
        ins=[aps["cc_in"][:, :].opt()], outs=[aps["cc_out"][:, :].opt()]),
        inc=f"{tag}_cc2")
    p.wait("gpsimd", f"{tag}_cc2", 1)
    p.dma("gpsimd", ccsb[:, :], aps["cc_out"][:, :], inc=f"{tag}_cc")
    p.wait("tensor", f"{tag}_cc")
    p.wait("tensor", f"{tag}_cml")
    p.do("tensor", lambda e: e.matmul(ps_t[0:1, :], cmsk[:, 0:1],
                                      ccsb[:, :], start=True, stop=True),
         inc=f"{tag}_p2m")
    p.wait("vector", f"{tag}_p2m")
    p.do("vector", lambda e: e.tensor_copy(corow[:, :], ps_t[0:1, :]), inc=f"{tag}_p2v")
    p.wait("vector", f"{tag}_p2v")
    p.barrier()

    ps_r = [p.psum(f"{tag}_pr{i}", [128, 32], F32, stack=es) for i in range(2)]
    rk_marks = [0, 0]
    ei_marks = [0, 0]
    for i in range(NTT):
        pb = ps_r[i % 2]
        eb = eib[i % 2]
        p.wait("vector", f"{tag}_p3m", ei_marks[i % 2])
        p.do("vector", lambda e, i=i, eb=eb:
             e.tensor_scalar(eb[:, :], c.iota_col[0:32, 0:1].to_broadcast([32, 128]),
                             float(i), None, op0=ALU.is_equal), inc=f"{tag}_p3v")
        p.wait("tensor", f"{tag}_p3v")
        p.wait("tensor", f"{tag}_p3r", rk_marks[i % 2])
        p.do("tensor", lambda e, i=i, pb=pb:
             e.matmul(pb[:, :], c.triu_excl_f[:, :], ohcat[:, i * 32:(i + 1) * 32],
                      start=True, stop=False), inc=f"{tag}_p3m")
        ei_marks[i % 2] = p.do("tensor", lambda e, pb=pb, eb=eb:
             e.matmul(pb[:, :], eb[:, :], trsc[:, :], start=False, stop=False),
             inc=f"{tag}_p3m")
        mm = p.do("tensor", lambda e, pb=pb:
             e.matmul(pb[:, :], c.ones_f[0:1, :], corow[:, :], start=False,
                      stop=True), inc=f"{tag}_p3m")
        p.wait("vector", f"{tag}_p3m", mm)
        for k in range(2):
            p.do("vector", lambda e, i=i, k=k, pb=pb:
                 e.tensor_tensor(scr[:, :], pb[:, k * 16:(k + 1) * 16],
                                 ohcat[:, i * 32 + k * 16:i * 32 + (k + 1) * 16],
                                 ALU.mult), inc=f"{tag}_p3v")
            p.wait("vector", f"{tag}_p3v")
            p.do("vector", lambda e:
                 e.tensor_reduce(col[:, 6:7], scr[:, :], AX.X, ALU.add),
                 inc=f"{tag}_p3v")
            p.wait("vector", f"{tag}_p3v")
            p.do("vector", lambda e, i=i, k=k:
                 e.tensor_scalar(keepst[:, i * 2 + k:i * 2 + k + 1], col[:, 6:7],
                                 CAPF, None, op0=ALU.is_lt), inc=f"{tag}_p3v")
        p.wait("vector", f"{tag}_p3v")
        p.do("vector", lambda e, i=i:
             e.tensor_scalar(Ast[:, i * 16:(i + 1) * 16],
                             ohcat[:, i * 32:i * 32 + 16],
                             keepst[:, i * 2:i * 2 + 1], None, op0=ALU.mult),
             inc=f"{tag}_p3r")
        p.wait("vector", f"{tag}_p3r")
        rk_marks[i % 2] = p.do("vector", lambda e, i=i:
             e.scalar_tensor_tensor(Ast[:, i * 16:(i + 1) * 16],
                                    ohcat[:, i * 32 + 16:i * 32 + 32],
                                    keepst[:, i * 2 + 1:i * 2 + 2],
                                    Ast[:, i * 16:(i + 1) * 16],
                                    op0=ALU.mult, op1=ALU.add), inc=f"{tag}_p3r")
    p.barrier()

    esel_marks = [0, 0]
    for i in range(NTT):
        eb = eselb[i % 2]
        p.wait("vector", f"{tag}_p4m", esel_marks[i % 2])
        p.do("vector", lambda e, i=i, eb=eb:
             e.tensor_scalar(eb[:, :], c.iota_row32[:, :], float(i), None,
                             op0=ALU.is_equal), inc=f"{tag}_p4v")
        p.wait("tensor", f"{tag}_p4v")
        esel_marks[i % 2] = p.do("tensor", lambda e, i=i, eb=eb:
             e.matmul(ps_t[:, 0:16], eb[:, :], Ast[:, i * 16:(i + 1) * 16],
                      start=(i == 0), stop=(i == NTT - 1)), inc=f"{tag}_p4m")
    p.wait("vector", f"{tag}_p4m")
    p.do("vector", lambda e: e.tensor_copy(sm32[:, 0:16], ps_t[:, 0:16]),
         inc=f"{tag}_p4v")
    p.wait("vector", f"{tag}_p4v")
    p.do("vector", lambda e: e.transpose(sm32b[:, :], sm32[:, :]), inc=f"{tag}_p4v")
    p.wait("vector", f"{tag}_p4v")
    p.do("vector", lambda e: e.tensor_tensor_scan(sm32c[0:16, :], c.ones_f[0:16, 0:32],
         sm32b[0:16, :], 0.0, ALU.mult, ALU.add), inc=f"{tag}_p4v")
    p.wait("vector", f"{tag}_p4v")
    p.do("vector", lambda e: e.tensor_tensor(sm32b[0:16, :], sm32c[0:16, :],
                                             sm32b[0:16, :], ALU.subtract),
         inc=f"{tag}_p4v")
    p.wait("vector", f"{tag}_p4v")
    p.do("vector", lambda e: e.transpose(sm32[:, :], sm32b[:, :]), inc=f"{tag}_p4v")
    p.wait("vector", f"{tag}_p4v")
    p.do("vector", lambda e: e.tensor_copy(trsc2[:, :], sm32[:, 0:16]),
         inc=f"{tag}_p4v")
    p.wait("vector", f"{tag}_p4v")
    p.barrier()

    ps_a = [p.psum(f"{tag}_pa{i}", [128, 16], F32, stack=es) for i in range(2)]
    ps_x = [p.psum(f"{tag}_px{i}", [128, 128], BF16, stack=es) for i in range(2)]
    xtm = [p.sbuf(f"{tag}_xtm{i}", [128, D], BF16, stack=es) for i in range(2)]
    sc_marks = [0, 0]
    a_marks = [0, 0]
    ei_marks = [0, 0]
    for i in range(NTT):
        pb = ps_a[i % 2]
        eb = eib[i % 2]
        p.wait("vector", f"{tag}_p5m", ei_marks[i % 2])
        p.do("vector", lambda e, i=i, eb=eb:
             e.tensor_scalar(eb[:, :], c.iota_col[0:32, 0:1].to_broadcast([32, 128]),
                             float(i), None, op0=ALU.is_equal), inc=f"{tag}_p5v")
        p.wait("tensor", f"{tag}_p5v")
        p.wait("tensor", f"{tag}_p5r", a_marks[i % 2])
        p.do("tensor", lambda e, i=i, pb=pb:
             e.matmul(pb[:, :], c.triu_excl_f[:, :], Ast[:, i * 16:(i + 1) * 16],
                      start=True, stop=False), inc=f"{tag}_p5m")
        ei_marks[i % 2] = p.do("tensor", lambda e, pb=pb, eb=eb:
             e.matmul(pb[:, :], eb[:, :], trsc2[:, :], start=False, stop=True),
             inc=f"{tag}_p5m")
        p.wait("vector", f"{tag}_p5m", ei_marks[i % 2])
        for k in range(2):
            p.do("vector", lambda e, i=i, k=k, pb=pb:
                 e.tensor_tensor(scr[:, :], pb[:, :],
                                 ohcat[:, i * 32 + k * 16:i * 32 + (k + 1) * 16],
                                 ALU.mult), inc=f"{tag}_p5v")
            p.wait("vector", f"{tag}_p5v")
            a_marks[i % 2] = p.do("vector", lambda e:
                 e.tensor_reduce(col[:, 7:8], scr[:, :], AX.X, ALU.add),
                 inc=f"{tag}_p5r")
            p.wait("vector", f"{tag}_p5r")
            p.do("vector", lambda e, i=i, k=k:
                 e.scalar_tensor_tensor(col[:, 8:9], i12f[:, i * 2 + k:i * 2 + k + 1],
                                        CLF, col[:, 7:8], op0=ALU.mult, op1=ALU.add),
                 inc=f"{tag}_p5v")
            p.do("vector", lambda e:
                 e.tensor_scalar(col[:, 9:10], col[:, 7:8], CLF, None,
                                 op0=ALU.is_lt), inc=f"{tag}_p5v")
            p.wait("vector", f"{tag}_p5v")
            p.do("vector", lambda e, i=i, k=k:
                 e.tensor_tensor(col[:, 10:11], col[:, 9:10],
                                 keepst[:, i * 2 + k:i * 2 + k + 1], ALU.mult),
                 inc=f"{tag}_p5v")
            p.wait("vector", f"{tag}_p5v")
            p.do("vector", lambda e:
                 e.tensor_tensor(col[:, 11:12], col[:, 8:9], col[:, 10:11],
                                 ALU.mult), inc=f"{tag}_p5v")
            BIGF = 2147483648.0
            p.do("vector", lambda e, BIGF=BIGF:
                 e.tensor_scalar(col[:, 6:7], col[:, 10:11], -BIGF, BIGF,
                                 op0=ALU.mult, op1=ALU.add), inc=f"{tag}_p5v")
            p.do("vector", lambda e:
                 e.tensor_scalar(col[:, 5:6], col[:, 10:11], -float(NSLOT),
                                 float(NSLOT), op0=ALU.mult, op1=ALU.add),
                 inc=f"{tag}_p5v")
            p.wait("vector", f"{tag}_p5v")
            p.do("vector", lambda e:
                 e.tensor_tensor(col[:, 4:5], col[:, 11:12], col[:, 6:7], ALU.add),
                 inc=f"{tag}_p5v")
            p.do("vector", lambda e:
                 e.tensor_tensor(col[:, 3:4], col[:, 11:12], col[:, 5:6], ALU.add),
                 inc=f"{tag}_p5v")
            p.wait("vector", f"{tag}_p5v")
            p.do("vector", lambda e, i=i, k=k:
                 e.tensor_copy(destst[:, i * 2 + k:i * 2 + k + 1], col[:, 4:5]),
                 inc=f"{tag}_p5v")
            p.do("vector", lambda e, i=i, k=k:
                 e.tensor_copy(srcst[:, i * 2 + k:i * 2 + k + 1], col[:, 3:4]),
                 inc=f"{tag}_p5v")
        buf = xtm[i % 2]
        p.wait("vector", f"{tag}_sc", sc_marks[i % 2])
        for dc in range(DC):
            pxb = ps_x[dc % 2]
            p.wait("tensor", f"{tag}_p5e")
            p.do("tensor", lambda e, i=i, dc=dc, pxb=pxb:
                 e.transpose(pxb[:, :], lnx[dc][:, i * 128:(i + 1) * 128],
                             c.ident_bf[:, :]), inc=f"{tag}_p5t")
            p.wait("vector", f"{tag}_p5t")
            p.do("vector", lambda e, dc=dc, pxb=pxb, buf=buf:
                 e.tensor_copy(buf[:, dc * 128:(dc + 1) * 128], pxb[:, :]),
                 inc=f"{tag}_p5e")
        p.wait("gpsimd", f"{tag}_p5e")
        p.wait("gpsimd", f"{tag}_p5v")
        p.wait("gpsimd", f"{tag}_sc", sc_marks[i % 2])
        for k in range(2):
            sc_marks[i % 2] = p.do("gpsimd", lambda e, i=i, k=k, buf=buf:
                 e.indirect_dma_start(
                     out=aps["xe"][:, :],
                     out_offset=IndirectOffsetOnAxis(
                         ap=destst[:, i * 2 + k:i * 2 + k + 1], axis=0),
                     in_=buf[:, :], in_offset=None,
                     bounds_check=rbc, oob_is_err=False),
                 inc=f"{tag}_sc", n=16)
    p.wait("gpsimd", f"{tag}_sc")
    p.barrier()
    es.close()


def emit_l0ffn_exec(p: Prog, c: Consts, aps, x_ap, y_ap, tag, srcst, wst):
    es2 = ExitStack()
    xeT = [[p.sbuf(f"{tag}_xeT{j}_{dc}", [128, CL], BF16, stack=es2)
            for dc in range(DC)] for j in range(2)]
    wg = [p.sbuf(f"{tag}_wg{j}", [128, DC, DC, 128], BF16, stack=es2) for j in range(2)]
    wu = [p.sbuf(f"{tag}_wu{j}", [128, DC, DC, 128], BF16, stack=es2) for j in range(2)]
    wd = [p.sbuf(f"{tag}_wd{j}", [128, DC, 1024], BF16, stack=es2) for j in range(2)]
    hT = [p.sbuf(f"{tag}_hT{i}", [128, 512], BF16, stack=es2) for i in range(DC)]
    sil = [p.sbuf(f"{tag}_sil{i}", [128, 512], F32, stack=es2) for i in range(2)]
    oub = [p.sbuf(f"{tag}_ou{i}", [128, D], BF16, stack=es2) for i in range(2)]
    zr = p.sbuf(f"{tag}_zr", [1, D], BF16, stack=es2)
    ps_g = [p.psum(f"{tag}_pg{i}", [128, 512], F32, stack=es2) for i in range(2)]
    ps_u = [p.psum(f"{tag}_pu{i}", [128, 512], F32, stack=es2) for i in range(2)]
    ps_d = [p.psum(f"{tag}_pd{i}", [128, 512], F32, stack=es2) for i in range(4)]
    p.do("vector", lambda e: e.memset(zr[:, :], 0.0), inc=f"{tag}_ez")
    p.wait("gpsimd", f"{tag}_ez")
    p.dma("gpsimd", aps["outsc"][NSLOT:NSLOT + 1, :], zr[:, :], inc=f"{tag}_ez2")
    ou_marks = [0, 0]
    mm_done = [0, 0]
    ld_marks = [0, 0]
    wl_marks = [0, 0]
    gu_marks = [[0, 0], [0, 0]]
    dn_marks = [0, 0, 0, 0]

    def load_expert(ex):
        j = ex % 2
        p.wait("sync", f"{tag}_emm", mm_done[j])
        lm = 0
        for dc in range(DC):
            lm = p.do("sync", lambda e, ex=ex, dc=dc, j=j:
                 e.dma_start_transpose(
                     out=xeT[j][dc][:, :],
                     in_=aps["xe"][ex * CL:(ex + 1) * CL, dc * 128:(dc + 1) * 128]),
                 inc=f"{tag}_el", n=16)
        p.wait("gpsimd", f"{tag}_emm", mm_done[j])
        p.dma("gpsimd", wg[j][:, :, :, :], aps["g5"][ex], inc=f"{tag}_ew")
        p.dma("gpsimd", wu[j][:, :, :, :], aps["u5"][ex], inc=f"{tag}_ew")
        wm = p.dma("gpsimd", wd[j][:, :, :], aps["d5"][ex], inc=f"{tag}_ew")
        ld_marks[j] = lm
        wl_marks[j] = wm

    load_expert(0)
    for ex in range(E_MOE):
        j = ex % 2
        if ex + 1 < E_MOE:
            load_expert(ex + 1)
        p.wait("tensor", f"{tag}_el", ld_marks[j])
        p.wait("tensor", f"{tag}_ew", wl_marks[j])
        for sg in range(CL // 512):
            for hc in range(DC):
                hb = hc % 2
                pg, pu = ps_g[hb], ps_u[hb]
                p.wait("tensor", f"{tag}_ev1", gu_marks[hb][0])
                for kc in range(DC):
                    p.do("tensor", lambda e, kc=kc, hc=hc, sg=sg, j=j, pg=pg:
                         e.matmul(pg[:, :], wg[j][:, kc, hc, :],
                                  xeT[j][kc][:, sg * 512:(sg + 1) * 512],
                                  start=(kc == 0), stop=(kc == DC - 1)),
                         inc=f"{tag}_emm")
                for kc in range(DC):
                    mmk = p.do("tensor", lambda e, kc=kc, hc=hc, sg=sg, j=j, pu=pu:
                         e.matmul(pu[:, :], wu[j][:, kc, hc, :],
                                  xeT[j][kc][:, sg * 512:(sg + 1) * 512],
                                  start=(kc == 0), stop=(kc == DC - 1)),
                         inc=f"{tag}_emm")
                p.wait("scalar", f"{tag}_emm", mmk)
                p.wait("scalar", f"{tag}_ev1", gu_marks[hb][1])
                p.do("scalar", lambda e, hb=hb, pg=pg:
                     e.activation(sil[hb][:, :], pg[:, :], AF.Silu), inc=f"{tag}_ea")
                p.wait("vector", f"{tag}_ea")
                m1 = p.do("vector", lambda e, hc=hc, hb=hb, pu=pu:
                     e.tensor_tensor(hT[hc][:, :], sil[hb][:, :], pu[:, :], ALU.mult),
                     inc=f"{tag}_ev1")
                gu_marks[hb] = [m1, m1]
            p.wait("tensor", f"{tag}_ev1")
            for st4 in range(4):
                ob = oub[st4 % 2]
                p.wait("vector", f"{tag}_eo", ou_marks[st4 % 2])
                for dg in range(2):
                    bi = (st4 * 2 + dg) % 4
                    pd = ps_d[bi]
                    p.wait("tensor", f"{tag}_ev2", dn_marks[bi])
                    for hc in range(DC):
                        mmk = p.do("tensor", lambda e, hc=hc, st4=st4, dg=dg, j=j, pd=pd:
                             e.matmul(pd[:, :], hT[hc][:, st4 * 128:(st4 + 1) * 128],
                                      wd[j][:, hc, dg * 512:(dg + 1) * 512],
                                      start=(hc == 0), stop=(hc == DC - 1)),
                             inc=f"{tag}_emm")
                    p.wait("vector", f"{tag}_emm", mmk)
                    dn_marks[bi] = p.do("vector", lambda e, dg=dg, pd=pd, ob=ob:
                         e.tensor_copy(ob[:, dg * 512:(dg + 1) * 512], pd[:, :]),
                         inc=f"{tag}_ev2")
                p.wait("gpsimd", f"{tag}_ev2", dn_marks[(st4 * 2 + 1) % 4])
                row0 = ex * CL + sg * 512 + st4 * 128
                ou_marks[st4 % 2] = p.dma("gpsimd",
                     aps["outsc"][row0:row0 + 128, :], ob[:, :], inc=f"{tag}_eo")
        mm_done[j] = p.mark(f"{tag}_emm")
    p.wait("gpsimd", f"{tag}_eo")
    p.wait("gpsimd", f"{tag}_ez2")
    p.barrier()
    es2.close()

    es3 = ExitStack()
    gth = [p.sbuf(f"{tag}_g{i}", [128, D], BF16, stack=es3) for i in range(2)]
    cmb = p.sbuf(f"{tag}_cmb", [128, D], F32, stack=es3)
    xin = [p.sbuf(f"{tag}_xi{i}", [128, D], F32, stack=es3) for i in range(2)]
    ybf = [p.sbuf(f"{tag}_yb{i}", [128, D], F32, stack=es3) for i in range(2)]
    rbc2 = p.nc.gpsimd.alloc_register(name=f"{tag}_rbc2")
    p.do("gpsimd", lambda e: e.reg_mov(rbc2, NSLOT))
    y_marks = [0, 0]
    x_marks = [0, 0]
    for i in range(NTT):
        p.wait("gpsimd", f"{tag}_cv")
        gm = 0
        for k in range(2):
            gm = p.do("gpsimd", lambda e, i=i, k=k:
                 e.indirect_dma_start(
                     out=gth[k][:, :], out_offset=None,
                     in_=aps["outsc"][:, :],
                     in_offset=IndirectOffsetOnAxis(
                         ap=srcst[:, i * 2 + k:i * 2 + k + 1], axis=0),
                     bounds_check=rbc2, oob_is_err=False),
                 inc=f"{tag}_cg", n=16)
        xb = xin[i % 2]
        p.wait("sync", f"{tag}_cv", x_marks[i % 2])
        xm = p.dma("sync", xb[:, :], x_ap[i * 128:(i + 1) * 128, :], inc=f"{tag}_xl")
        p.wait("vector", f"{tag}_cg", gm)
        p.do("vector", lambda e, i=i:
             e.tensor_scalar(cmb[:, :], gth[0][:, :], wst[:, i * 2:i * 2 + 1],
                             None, op0=ALU.mult), inc=f"{tag}_cv")
        p.wait("vector", f"{tag}_cv")
        p.do("vector", lambda e, i=i:
             e.scalar_tensor_tensor(cmb[:, :], gth[1][:, :],
                                    wst[:, i * 2 + 1:i * 2 + 2], cmb[:, :],
                                    op0=ALU.mult, op1=ALU.add), inc=f"{tag}_cv")
        yb = ybf[i % 2]
        p.wait("vector", f"{tag}_yo", y_marks[i % 2])
        p.wait("vector", f"{tag}_xl", xm)
        p.wait("vector", f"{tag}_cv")
        p.do("vector", lambda e, xb=xb, yb=yb:
             e.tensor_tensor(yb[:, :], xb[:, :], cmb[:, :], ALU.add),
             inc=f"{tag}_cv")
        x_marks[i % 2] = p.mark(f"{tag}_cv")
        p.wait("gpsimd", f"{tag}_cv")
        y_marks[i % 2] = p.dma("gpsimd", y_ap[i * 128:(i + 1) * 128, :], yb[:, :],
                               inc=f"{tag}_yo")
    p.wait("gpsimd", f"{tag}_yo")
    p.barrier()
    es3.close()




def emit_l0attn(p: Prog, c: Consts, waps, x_ap, y_ap, tag, dbg=None):
    es = ExitStack()
    lnx = [p.sbuf(f"{tag}_lnx{dc}", [128, S], BF16, stack=es) for dc in range(DC)]
    r2b = p.sbuf(f"{tag}_r2", [1, S], BF16, stack=es)
    r2bcs = p.sbuf(f"{tag}_r2b", [128, 128], F32, stack=es)
    r2col = p.sbuf(f"{tag}_r2c", [128, 1], F32, stack=es)
    sclc = p.sbuf(f"{tag}_scl", [128, 1], F32, stack=es)
    wq = p.sbuf(f"{tag}_wq", [128, DC, D], BF16, stack=es)
    wk = p.sbuf(f"{tag}_wk", [128, DC, D], BF16, stack=es)
    wv = p.sbuf(f"{tag}_wv", [128, DC, D], BF16, stack=es)
    wio = p.sbuf(f"{tag}_wio", [128, DC, 48], BF16, stack=es)
    wo = p.sbuf(f"{tag}_wo", [128, DC, D], BF16, stack=es)
    lnw = p.sbuf(f"{tag}_lnw", [128, DC], F32, stack=es)
    wtmp = p.sbuf(f"{tag}_wtmp", [128, 512], F32, stack=es)
    qfc = p.sbuf(f"{tag}_qfc", [128, DC, 128], BF16, stack=es)
    kfc = p.sbuf(f"{tag}_kfc", [128, DC, 128], BF16, stack=es)
    ktm = p.sbuf(f"{tag}_ktm", [128, D], BF16, stack=es)
    vtm = p.sbuf(f"{tag}_vtm", [128, D], BF16, stack=es)
    hout = p.sbuf(f"{tag}_hout", [128, DC, 128], BF16, stack=es)
    t48 = p.sbuf(f"{tag}_t48", [128, 48], F32, stack=es)
    g16 = p.sbuf(f"{tag}_g16", [128, 16 * 5], F32, stack=es)
    logi = p.sbuf(f"{tag}_logi", [128, 16], F32, stack=es)
    logf = p.sbuf(f"{tag}_logf", [128, 16], F32, stack=es)
    og = p.sbuf(f"{tag}_og", [128, 16], F32, stack=es)
    Fsb = p.sbuf(f"{tag}_F", [128, 16], F32, stack=es)
    blf = p.sbuf(f"{tag}_blf", [128, 16], F32, stack=es)
    FTsb = p.sbuf(f"{tag}_FT", [16, 128], F32, stack=es)
    EFt = p.sbuf(f"{tag}_EFt", [16, 128], F32, stack=es)
    OTsb = p.sbuf(f"{tag}_OT", [16, 128], F32, stack=es)
    EF2sb = p.sbuf(f"{tag}_EF2", [128, 128], F32, stack=es)
    O2sb = p.sbuf(f"{tag}_O2", [128, 128], F32, stack=es)
    Mexp = p.sbuf(f"{tag}_M", [128, 128], BF16, stack=es)
    SpT = p.sbuf(f"{tag}_SpT", [128, 128], BF16, stack=es)
    qp = p.sbuf(f"{tag}_qp", [64, 128], BF16, stack=es)
    ccol = p.sbuf(f"{tag}_ccol", [128, 1], F32, stack=es)
    kpp = p.sbuf(f"{tag}_kpp", [128, 64], BF16, stack=es)
    wev = p.sbuf(f"{tag}_wev", [128, 128], F32, stack=es)
    Hb = p.sbuf(f"{tag}_H", [64, NH * HD], F32, stack=es)
    Hbf = p.sbuf(f"{tag}_Hbf", [64, NH * HD], BF16, stack=es)
    sel2 = p.sbuf(f"{tag}_sel2", [16, 128], F32, stack=es)
    eh1 = p.sbuf(f"{tag}_eh1", [16, 128], F32, stack=es)
    i16r = p.sbuf(f"{tag}_i16r", [16, 128], F32, stack=es)
    i16c = p.sbuf(f"{tag}_i16c", [16, 1], F32, stack=es)
    hm = p.sbuf(f"{tag}_hm", [16, 128], F32, stack=es)
    PB = [p.psum(f"{tag}_PB{i}", [128, 512], F32, stack=es) for i in range(8)]
    ps_a = PB[0]
    ps_b = [PB[1][:, 0:128], PB[2][:, 0:128]]
    ps_T1 = PB[3][:, 0:128]
    ps_S = PB[4][:, 0:128]
    ps_o = PB[5][:, 0:128]
    ps_h = PB[6][0:64, 0:64]
    ps_F = PB[7][:, 0:128]

    p.dma("sync", lnw[:, :], waps["alnw"], inc=f"{tag}_lnl")
    p.wait("vector", f"{tag}_lnl")
    for name, sb, ncol, fold in (("aq", wq, D, True), ("ak", wk, D, True),
                                 ("av", wv, D, True), ("aio", wio, 48, True),
                                 ("aout", wo, D, False)):
        for kc in range(DC):
            for hf in range(1 if ncol <= 512 else 2):
                w0 = hf * 512
                w1 = min(ncol, w0 + 512)
                p.wait("gpsimd", f"{tag}_wf")
                p.dma("gpsimd", wtmp[:, 0:w1 - w0], waps[name][:, kc, w0:w1],
                      inc=f"{tag}_wl")
                p.wait("vector", f"{tag}_wl")
                if fold:
                    p.do("vector", lambda e, sb=sb, kc=kc, w0=w0, w1=w1:
                         e.tensor_scalar(sb[:, kc, w0:w1], wtmp[:, 0:w1 - w0],
                                         lnw[:, kc:kc + 1], None, op0=ALU.mult),
                         inc=f"{tag}_wf")
                else:
                    p.do("vector", lambda e, sb=sb, kc=kc, w0=w0, w1=w1:
                         e.tensor_copy(sb[:, kc, w0:w1], wtmp[:, 0:w1 - w0]),
                         inc=f"{tag}_wf")
    p.do("gpsimd", lambda e: e.iota(i16r[:, :], pattern=[[1, 128]], base=0,
                                    channel_multiplier=0,
                                    allow_small_or_imprecise_dtypes=True), inc=f"{tag}_ci")
    p.do("gpsimd", lambda e: e.iota(i16c[:, :], pattern=[[0, 1]], base=0,
                                    channel_multiplier=1,
                                    allow_small_or_imprecise_dtypes=True), inc=f"{tag}_ci")
    p.wait("vector", f"{tag}_ci")
    p.do("vector", lambda e: e.tensor_scalar(hm[:, :], i16r[:, :], 63.5, None,
                                             op0=ALU.is_gt), inc=f"{tag}_cs")
    p.wait("vector", f"{tag}_cs")
    p.wait("vector", f"{tag}_cs")
    p.barrier()

    for b in range(BLOC):
        rsub = ExitStack()
        st = Stream(p, c, cols=S, stack=rsub)
        emit_load_resid(p, c, st, x_ap, t0=b * S, t1=(b + 1) * S,
                        psums=[PB[i][:, 0:128] for i in range(4)])
        emit_ln(p, c, st, lnx, rms2=True, name=f"{tag}_ln{b}",
                tg0=b * (S // 512), tg1=(b + 1) * (S // 512), col0=b * S,
                psums=(PB[1], PB[2], PB[3], PB[4]), r2out=r2b)
        p.do("vector", lambda e: e.memset(Hb[:, :], 0.0), inc=f"{tag}_hz")
        p.do("vector", lambda e: e.memset(Hbf[:, :], 0.0), inc=f"{tag}_hz")
        p.wait("tensor", f"{tag}_hz")
        for n in range(NCH):
            sl = slice(n * CHUNK, (n + 1) * CHUNK)
            slg = sl
            p.barrier()
            p.wait("tensor", f"{tag}_r2v")
            p.do("tensor", lambda e, sl=sl:
                 e.matmul(ps_T1[:, :], c.ones_bf[0:1, :], r2b[0:1, sl],
                          start=True, stop=True), inc=f"{tag}_r2m")
            p.wait("vector", f"{tag}_r2m")
            p.do("vector", lambda e: e.tensor_copy(r2bcs[:, :], ps_T1[:, :]),
                 inc=f"{tag}_r2v")
            p.wait("tensor", f"{tag}_r2v")
            p.do("tensor", lambda e:
                 e.matmul(ps_T1[:, 0:1], r2bcs[:, :], c.col(1.0 / 128.0),
                          start=True, stop=True), inc=f"{tag}_r2m")
            p.wait("vector", f"{tag}_r2m")
            p.do("vector", lambda e: e.tensor_copy(r2col[:, 0:1], ps_T1[:, 0:1]),
                 inc=f"{tag}_r2v")
            p.wait("vector", f"{tag}_r2v")
            p.do("vector", lambda e: e.tensor_scalar(sclc[:, 0:1], r2col[:, 0:1],
                 1.0 / GATE_CAP, None, op0=ALU.mult), inc=f"{tag}_r2v")
            p.wait("tensor", f"{tag}_r2v")
            for oc in range(DC):
                pb = ps_b[oc % 2]
                p.wait("tensor", f"{tag}_pe")
                for kc in range(DC):
                    p.do("tensor", lambda e, oc=oc, kc=kc, sl=sl, pb=pb:
                         e.matmul(pb[:, :], wq[:, kc, oc * 128:(oc + 1) * 128],
                                  lnx[kc][:, sl], start=(kc == 0), stop=(kc == DC - 1)),
                         inc=f"{tag}_pm")
                p.wait("vector", f"{tag}_pm")
                p.do("vector", lambda e, oc=oc, pb=pb:
                     e.tensor_tensor(qfc[:, oc, :], pb[:, :], r2bcs[:, :],
                                     ALU.mult), inc=f"{tag}_pe")
                pb = ps_b[(oc + 1) % 2]
                p.wait("tensor", f"{tag}_pe")
                for kc in range(DC):
                    p.do("tensor", lambda e, oc=oc, kc=kc, sl=sl, pb=pb:
                         e.matmul(pb[:, :], wk[:, kc, oc * 128:(oc + 1) * 128],
                                  lnx[kc][:, sl], start=(kc == 0), stop=(kc == DC - 1)),
                         inc=f"{tag}_pm")
                p.wait("vector", f"{tag}_pm")
                p.do("vector", lambda e, oc=oc, pb=pb:
                     e.tensor_tensor(kfc[:, oc, :], pb[:, :], r2bcs[:, :],
                                     ALU.mult), inc=f"{tag}_pe")
            for g in range(2):
                p.wait("tensor", f"{tag}_pe")
                for kc in range(DC):
                    p.do("tensor", lambda e, g=g, kc=kc, sl=sl:
                         e.matmul(ps_a[:, :], lnx[kc][:, sl],
                                  wk[:, kc, g * 512:(g + 1) * 512],
                                  start=(kc == 0), stop=(kc == DC - 1)), inc=f"{tag}_pm")
                p.wait("vector", f"{tag}_pm")
                p.do("vector", lambda e, g=g:
                     e.tensor_scalar(ktm[:, g * 512:(g + 1) * 512], ps_a[:, :],
                                     r2col[:, 0:1], None, op0=ALU.mult),
                     inc=f"{tag}_pe")
                p.wait("tensor", f"{tag}_pe")
                for kc in range(DC):
                    p.do("tensor", lambda e, g=g, kc=kc, sl=sl:
                         e.matmul(ps_a[:, :], lnx[kc][:, sl],
                                  wv[:, kc, g * 512:(g + 1) * 512],
                                  start=(kc == 0), stop=(kc == DC - 1)), inc=f"{tag}_pm")
                p.wait("vector", f"{tag}_pm")
                p.do("vector", lambda e, g=g:
                     e.tensor_scalar(vtm[:, g * 512:(g + 1) * 512], ps_a[:, :],
                                     r2col[:, 0:1], None, op0=ALU.mult),
                     inc=f"{tag}_pe")
            p.wait("tensor", f"{tag}_pe")
            for kc in range(DC):
                p.do("tensor", lambda e, kc=kc, sl=sl:
                     e.matmul(ps_a[:, 0:48], lnx[kc][:, sl], wio[:, kc, :],
                              start=(kc == 0), stop=(kc == DC - 1)), inc=f"{tag}_pm")
            p.wait("scalar", f"{tag}_pm")
            p.wait("scalar", f"{tag}_r2v")
            p.do("scalar", lambda e: e.activation(t48[:, :],
                 ps_a[:, 0:48], AF.Tanh, scale=sclc[:, 0:1]), inc=f"{tag}_ga")
            p.wait("vector", f"{tag}_ga")
            for dst, c0 in ((logi, 0), (logf, 16)):
                p.do("vector", lambda e, dst=dst, c0=c0:
                     e.tensor_scalar(g16[:, 0:16], t48[:, c0:c0 + 16], -GATE_CAP,
                                     None, op0=ALU.mult), inc=f"{tag}_gv")
                p.wait("vector", f"{tag}_gv")
                p.do("vector", lambda e: e.tensor_scalar(g16[:, 16:32], g16[:, 0:16],
                     -1.0, None, op0=ALU.mult), inc=f"{tag}_gv")
                p.do("vector", lambda e: e.tensor_scalar(g16[:, 32:48], g16[:, 0:16],
                     0.0, None, op0=ALU.max), inc=f"{tag}_gv")
                p.wait("vector", f"{tag}_gv")
                p.do("vector", lambda e: e.tensor_tensor(g16[:, 16:32], g16[:, 0:16],
                     g16[:, 16:32], ALU.max), inc=f"{tag}_gv")
                p.wait("scalar", f"{tag}_gv")
                p.do("scalar", lambda e: e.activation(g16[:, 48:64], g16[:, 16:32],
                     AF.Exp, scale=-1.0), inc=f"{tag}_ga")
                p.wait("scalar", f"{tag}_ga")
                p.do("scalar", lambda e: e.activation(g16[:, 64:80], g16[:, 48:64],
                     AF.Ln, bias=c.col(1.0)), inc=f"{tag}_ga")
                p.wait("vector", f"{tag}_ga")
                p.do("vector", lambda e: e.tensor_tensor(g16[:, 48:64], g16[:, 32:48],
                     g16[:, 64:80], ALU.add), inc=f"{tag}_gv")
                p.wait("vector", f"{tag}_gv")
                p.do("vector", lambda e, dst=dst:
                     e.tensor_scalar(dst[:, :], g16[:, 48:64], -1.0, None,
                                     op0=ALU.mult), inc=f"{tag}_gv")
            p.wait("scalar", f"{tag}_gv")
            p.do("scalar", lambda e: e.activation(og[:, :], t48[:, 32:48], AF.Sigmoid,
                                                  scale=GATE_CAP), inc=f"{tag}_ga")
            p.wait("tensor", f"{tag}_ga")
            p.wait("tensor", f"{tag}_gv")
            p.do("tensor", lambda e: e.matmul(ps_F[:, 0:16], c.triu_incl_f[:, :],
                                              logf[:, :], start=True, stop=True),
                 inc=f"{tag}_fm")
            p.wait("vector", f"{tag}_fm")
            p.do("vector", lambda e: e.tensor_copy(Fsb[:, :], ps_F[:, 0:16]),
                 inc=f"{tag}_fv")
            p.wait("vector", f"{tag}_fv")
            p.do("vector", lambda e: e.tensor_tensor(blf[:, :], logi[:, :], Fsb[:, :],
                                                     ALU.subtract), inc=f"{tag}_fv")
            p.wait("tensor", f"{tag}_fv")
            p.do("tensor", lambda e: e.transpose(ps_F[0:16, 0:128],
                 Fsb[:, :], c.ident_f[:, :]), inc=f"{tag}_fm")
            p.wait("vector", f"{tag}_fm")
            p.do("vector", lambda e: e.tensor_copy(FTsb[:, :], ps_F[0:16, 0:128]),
                 inc=f"{tag}_fv")
            p.wait("tensor", f"{tag}_fv")
            p.do("tensor", lambda e: e.transpose(ps_F[0:16, 0:128], og[:, :],
                                                 c.ident_f[:, :]), inc=f"{tag}_fm")
            p.wait("vector", f"{tag}_fm")
            p.do("vector", lambda e: e.tensor_copy(OTsb[:, :], ps_F[0:16, 0:128]),
                 inc=f"{tag}_fv")
            p.wait("scalar", f"{tag}_fv")
            p.do("scalar", lambda e: e.activation(EFt[:, :], FTsb[:, :], AF.Exp),
                 inc=f"{tag}_ga")
            p.wait("tensor", f"{tag}_ga")
            for h in range(NH):
                hp, hr = h // 2, (h % 2) * 64
                if h % 2 == 0:
                    p.wait("vector", f"{tag}_e2m")
                    p.do("vector", lambda e, hp=hp: e.tensor_scalar(
                         sel2[:, :], hm[:, :], 1.0, float(2 * hp),
                         op0=ALU.mult, op1=ALU.add), inc=f"{tag}_slv")
                    p.wait("vector", f"{tag}_slv")
                    p.do("vector", lambda e: e.tensor_scalar(
                         sel2[:, :], sel2[:, :], i16c[:, 0:1], None,
                         op0=ALU.is_equal), inc=f"{tag}_slv")
                    p.wait("tensor", f"{tag}_slv")
                    p.wait("tensor", f"{tag}_e2v")
                    p.do("tensor", lambda e, hp=hp:
                         e.matmul(ps_o[:, :], sel2[:, :], EFt[:, :],
                                  start=True, stop=True), inc=f"{tag}_e2m")
                    p.wait("vector", f"{tag}_e2m")
                    p.do("vector", lambda e: e.tensor_copy(EF2sb[:, :], ps_o[:, :]),
                         inc=f"{tag}_e2v")
                    p.wait("tensor", f"{tag}_e2v")
                    p.do("tensor", lambda e, hp=hp:
                         e.matmul(ps_o[:, :], sel2[:, :], OTsb[:, :],
                                  start=True, stop=True), inc=f"{tag}_e2m")
                    p.wait("vector", f"{tag}_e2m")
                    p.do("vector", lambda e: e.tensor_copy(O2sb[:, :], ps_o[:, :]),
                         inc=f"{tag}_e2v")
                    p.wait("tensor", f"{tag}_e2v")
                p.wait("vector", f"{tag}_t1m")
                p.do("vector", lambda e, h=h: e.tensor_scalar(
                     eh1[:, :], i16c[:, 0:1].to_broadcast([16, 128]), float(h), None,
                     op0=ALU.is_equal), inc=f"{tag}_ehv")
                p.wait("tensor", f"{tag}_ga2")
                p.wait("tensor", f"{tag}_ehv")
                p.do("tensor", lambda e, h=h:
                     e.matmul(ps_T1[:, :], eh1[:, :], FTsb[:, :],
                              start=True, stop=False), inc=f"{tag}_t1m")
                p.do("tensor", lambda e:
                     e.matmul(ps_T1[:, :], c.ident_f[:, :], c.mask_sq[:, :],
                              start=False, stop=True), inc=f"{tag}_t1m")
                p.wait("scalar", f"{tag}_t1m")
                p.wait("scalar", f"{tag}_sv")
                p.do("scalar", lambda e, h=h:
                     e.activation(Mexp[:, :], ps_T1[:, :], AF.Exp,
                                  bias=blf[:, h:h + 1]), inc=f"{tag}_ga2")
                p.do("tensor", lambda e, hp=hp, hr=hr:
                     e.matmul(ps_S[:, :], kfc[hr:hr + 64, hp, :], qfc[hr:hr + 64, hp, :],
                              start=True, stop=True), inc=f"{tag}_sm")
                p.wait("vector", f"{tag}_sm")
                p.wait("vector", f"{tag}_ga2")
                p.do("vector", lambda e: e.tensor_tensor(SpT[:, :], ps_S[:, :],
                     Mexp[:, :], ALU.mult), inc=f"{tag}_sv")
                p.do("vector", lambda e, hp=hp, hr=hr:
                     e.tensor_tensor(qp[:, :], qfc[hr:hr + 64, hp, :],
                                     EF2sb[hr:hr + 64, :], ALU.mult), inc=f"{tag}_sv")
                p.do("vector", lambda e:
                     e.tensor_copy(ccol[:, :], Mexp[:, 127:128]), inc=f"{tag}_sv")
                p.wait("vector", f"{tag}_sv")
                p.do("vector", lambda e, h=h:
                     e.tensor_scalar(kpp[:, :], ktm[:, h * 64:(h + 1) * 64],
                                     ccol[:, 0:1], None, op0=ALU.mult),
                     inc=f"{tag}_sv")
                p.wait("tensor", f"{tag}_sv")
                p.do("tensor", lambda e, h=h:
                     e.matmul(ps_o[0:64, :], vtm[:, h * 64:(h + 1) * 64], SpT[:, :],
                              start=True, stop=False), inc=f"{tag}_om")
                mo = p.do("tensor", lambda e, h=h:
                     e.matmul(ps_o[0:64, :], Hbf[:, h * 64:(h + 1) * 64], qp[:, :],
                              start=False, stop=True), inc=f"{tag}_om")
                p.do("tensor", lambda e, h=h:
                     e.matmul(ps_h[:, :], kpp[:, :], vtm[:, h * 64:(h + 1) * 64],
                              start=True, stop=True), inc=f"{tag}_hm")
                p.wait("vector", f"{tag}_om")
                p.do("vector", lambda e, h=h, hp=hp, hr=hr:
                     e.tensor_tensor(hout[hr:hr + 64, hp, :], ps_o[0:64, :],
                                     O2sb[hr:hr + 64, :], ALU.mult), inc=f"{tag}_ov")
                p.wait("vector", f"{tag}_hm")
                p.do("vector", lambda e, h=h, hr=hr:
                     e.tensor_scalar(Hb[:, h * 64:(h + 1) * 64],
                                     Hb[:, h * 64:(h + 1) * 64],
                                     EF2sb[hr:hr + 64, 127:128], None, op0=ALU.mult),
                     inc=f"{tag}_hv")
                p.wait("vector", f"{tag}_hv")
                p.do("vector", lambda e, h=h:
                     e.tensor_tensor(Hb[:, h * 64:(h + 1) * 64],
                                     Hb[:, h * 64:(h + 1) * 64], ps_h[:, :], ALU.add),
                     inc=f"{tag}_hv")
                p.wait("vector", f"{tag}_hv")
                p.wait("vector", f"{tag}_om", mo)
                p.do("vector", lambda e, h=h:
                     e.tensor_copy(Hbf[:, h * 64:(h + 1) * 64],
                                   Hb[:, h * 64:(h + 1) * 64]), inc=f"{tag}_hv")
                p.wait("tensor", f"{tag}_hv")
                p.wait("tensor", f"{tag}_ov")
            for dc in range(DC):
                pb = ps_b[dc % 2]
                p.wait("tensor", f"{tag}_we")
                for fc in range(DC):
                    p.do("tensor", lambda e, dc=dc, fc=fc, pb=pb:
                         e.matmul(pb[:, :], wo[:, fc, dc * 128:(dc + 1) * 128],
                                  hout[:, fc, :], start=(fc == 0), stop=(fc == DC - 1)),
                         inc=f"{tag}_wm")
                p.wait("vector", f"{tag}_wm")
                p.do("vector", lambda e, dc=dc, sl=sl, pb=pb:
                     e.tensor_tensor(wev[:, :], pb[:, :],
                                     lnx[dc][:, sl], ALU.add), inc=f"{tag}_we")
                p.wait("vector", f"{tag}_we")
                p.do("vector", lambda e, dc=dc, slg=slg:
                     e.tensor_tensor(st.resid[dc][:, slg], st.resid[dc][:, slg],
                                     wev[:, :], ALU.add), inc=f"{tag}_we")
        p.barrier()
        emit_store_out(p, c, st, y_ap, t0=b * S, t1=(b + 1) * S,
                       psums=[PB[i][:, 0:128] for i in range(2)])
        rsub.close()
    p.barrier()
    es.close()


def emit_l1attn(p: Prog, c: Consts, waps, x_ap, y_ap, tag, upto=9, dbga=None, dbgb=None):
    SCALE = 0.125
    NKT = S // 128
    NQG = S // 512
    es = ExitStack()
    lnx = [p.sbuf(f"{tag}_lnx{dc}", [128, S], BF16, stack=es) for dc in range(DC)]
    qf = p.sbuf(f"{tag}_qf", [128, S], BF16, stack=es)
    k1f = p.sbuf(f"{tag}_k1f", [128, S], BF16, stack=es)
    k2f = p.sbuf(f"{tag}_k2f", [128, S], BF16, stack=es)
    vtm = p.sbuf(f"{tag}_vtm", [128, NKT, 128], BF16, stack=es)
    es1 = p.sbuf(f"{tag}_es1", [128, NKT, 512], BF16, stack=es)
    es2_ = p.sbuf(f"{tag}_es2", [128, NKT, 512], BF16, stack=es)
    hout = p.sbuf(f"{tag}_ho", [128, S], BF16, stack=es)
    wpr = [p.sbuf(f"{tag}_wp{i}", [128, DC, 128], BF16, stack=es) for i in range(4)]
    wosl = p.sbuf(f"{tag}_wo", [128, D], BF16, stack=es)
    Ba = p.sbuf(f"{tag}_Ba", [128, 512], F32, stack=es)
    Bb = p.sbuf(f"{tag}_Bb", [128, 512], F32, stack=es)
    Bg = p.sbuf(f"{tag}_Bg", [64, 512], F32, stack=es)
    Nt = p.sbuf(f"{tag}_Nt", [128, 512], F32, stack=es)
    Nsb = p.sbuf(f"{tag}_Nsb", [128, 512], BF16, stack=es)
    Asb = p.sbuf(f"{tag}_Asb", [128, 512], BF16, stack=es)
    lam16 = p.sbuf(f"{tag}_lam", [128, 16], F32, stack=es)
    llrow = p.sbuf(f"{tag}_ll", [1, 16], F32, stack=es)
    PB = [p.psum(f"{tag}_PB{i}", [128, 512], F32, stack=es) for i in range(8)]
    psA, psZ1, psZ2, psD, psO = PB[5], PB[6], PB[7], PB[0], PB[4]
    p.dma("sync", llrow[:, :], waps["dll"], inc=f"{tag}_ll")
    p.wait("tensor", f"{tag}_ll")
    p.do("tensor", lambda e: e.matmul(PB[4][:, 0:16], c.ones_f[0:1, :], llrow[:, :],
                                      start=True, stop=True), inc=f"{tag}_lm")
    p.wait("scalar", f"{tag}_lm")
    p.do("scalar", lambda e: e.activation(lam16[:, :], PB[4][:, 0:16], AF.Sigmoid),
         inc=f"{tag}_la")
    p.wait("tensor", f"{tag}_la")
    p.wait("vector", f"{tag}_la")
    for b in range(BLOC):
        rsub = ExitStack()
        st = Stream(p, c, cols=S, stack=rsub)
        emit_load_resid(p, c, st, x_ap, t0=b * S, t1=(b + 1) * S,
                        psums=[PB[i][:, 0:128] for i in range(4)])
        emit_ln(p, c, st, lnx, rms2=False, name=f"{tag}_ln{b}",
                tg0=b * (S // 512), tg1=(b + 1) * (S // 512), col0=b * S,
                psums=(PB[0], PB[1], PB[2], PB[3]))
        if dbga is not None and b == 1:
            p.wait("gpsimd", f"{tag}_woe")
            p.dma("gpsimd", dbgb[8, 0:128, 0:512], st.resid[0][:, 0:512],
                  inc=f"{tag}_dbg")
            p.wait("gpsimd", f"{tag}_dbg")
        for hp in range(DC):
            if dbga is not None and b == 0:
                p.wait("gpsimd", f"{tag}_woe")
                p.dma("gpsimd", dbgb[hp, 0:128, 0:512], st.resid[0][:, 0:512],
                      inc=f"{tag}_dbg")
                p.wait("gpsimd", f"{tag}_dbg")
            p.wait("gpsimd", f"{tag}_pm")
            p.wait("gpsimd", f"{tag}_wom")
            for i, nm in enumerate(("dq", "dk1", "dk2", "dv")):
                p.dma("gpsimd", wpr[i][:, :, :],
                      waps[nm][:, :, hp * 128:(hp + 1) * 128], inc=f"{tag}_wl")
            p.dma("gpsimd", wosl[:, :], waps["do"][:, hp, :], inc=f"{tag}_wl")
            p.wait("tensor", f"{tag}_wl")
            p.wait("tensor", f"{tag}_woe")
            for dst, wi in ((qf, 0), (k1f, 1), (k2f, 2)):
                for tg in range(NQG):
                    pb = PB[1 + tg % 2]
                    p.wait("tensor", f"{tag}_pe")
                    for kc in range(DC):
                        p.do("tensor", lambda e, wi=wi, kc=kc, tg=tg, b=b, pb=pb:
                             e.matmul(pb[:, :], wpr[wi][:, kc, :],
                                      lnx[kc][:, tg * 512:(tg + 1) * 512],
                                      start=(kc == 0), stop=(kc == DC - 1)),
                             inc=f"{tag}_pm")
                    p.wait("vector", f"{tag}_pm")
                    p.do("vector", lambda e, dst=dst, tg=tg, pb=pb:
                         e.tensor_copy(dst[:, tg * 512:(tg + 1) * 512], pb[:, :]),
                         inc=f"{tag}_pe")
            for kt in range(NKT):
                pb = PB[1 + kt % 2]
                p.wait("tensor", f"{tag}_pe")
                for kc in range(DC):
                    p.do("tensor", lambda e, kc=kc, kt=kt, pb=pb:
                         e.matmul(pb[:, 0:128], lnx[kc][:, kt * 128:(kt + 1) * 128],
                                  wpr[3][:, kc, :], start=(kc == 0),
                                  stop=(kc == DC - 1)), inc=f"{tag}_pm")
                p.wait("vector", f"{tag}_pm")
                p.do("vector", lambda e, kt=kt, pb=pb:
                     e.tensor_copy(vtm[:, kt, :], pb[:, 0:128]), inc=f"{tag}_pe")
            p.wait("tensor", f"{tag}_pe")
            for h2 in range(2 if upto >= 2 else 0):
                h = 2 * hp + h2
                hr = h2 * 64
                for qg in range(NQG):
                    nkt = 4 * (qg + 1)
                    qsl = slice(qg * 512, (qg + 1) * 512)
                    p.wait("scalar", f"{tag}_cv")
                    for mat, (kf, est, psZ) in enumerate(
                            ((k1f, es1, psZ1), (k2f, es2_, psZ2))):
                        for kt in range(nkt):
                            jj = kt - 4 * qg
                            p.wait("tensor", f"{tag}_ea")
                            p.do("tensor", lambda e, kf=kf, kt=kt, hr=hr, qsl=qsl:
                                 e.matmul(psA[:, :],
                                          kf[hr:hr + 64, kt * 128:(kt + 1) * 128],
                                          qf[hr:hr + 64, qsl],
                                          start=True, stop=(kt < 4 * qg)),
                                 inc=f"{tag}_sm")
                            if jj >= 0:
                                p.do("tensor", lambda e, jj=jj:
                                     e.matmul(psA[:, jj * 128:(jj + 1) * 128],
                                              c.ident_f[:, :], c.mask_sq[:, :],
                                              start=False, stop=True),
                                     inc=f"{tag}_sm")
                            p.wait("scalar", f"{tag}_sm")
                            p.do("scalar", lambda e, est=est, kt=kt:
                                 e.activation(est[:, kt, :], psA[:, :], AF.Exp,
                                              scale=SCALE, bias=c.col(-12.0)),
                                 inc=f"{tag}_ea")
                            if jj >= 1:
                                p.wait("vector", f"{tag}_ea")
                                p.do("vector", lambda e, est=est, kt=kt, jj=jj:
                                     e.memset(est[:, kt, 0:jj * 128], 0.0),
                                     inc=f"{tag}_ev")
                            p.wait("tensor", f"{tag}_ea")
                            p.wait("tensor", f"{tag}_ev")
                            p.do("tensor", lambda e, est=est, kt=kt, psZ=psZ:
                                 e.matmul(psZ[:, :], c.ones_bf[:, :], est[:, kt, :],
                                          start=(kt == 0), stop=(kt == nkt - 1)),
                                 inc=f"{tag}_zm")
                    if upto < 3:
                        continue
                    p.wait("vector", f"{tag}_zm")
                    p.do("vector", lambda e: e.reciprocal(Ba[:, :], psZ1[:, :]),
                         inc=f"{tag}_bv")
                    p.do("vector", lambda e: e.reciprocal(Bb[:, :], psZ2[:, :]),
                         inc=f"{tag}_bv")
                    p.wait("vector", f"{tag}_bv")
                    p.do("vector", lambda e, h=h:
                         e.tensor_scalar(Bb[:, :], Bb[:, :], lam16[:, h:h + 1],
                                         None, op0=ALU.mult), inc=f"{tag}_bv")
                    p.wait("vector", f"{tag}_bv")
                    if upto < 4:
                        continue
                    p.wait("tensor", f"{tag}_hv")
                    p.wait("tensor", f"{tag}_gv")
                    for kt in range(nkt):
                        p.wait("vector", f"{tag}_om")
                        p.wait("vector", f"{tag}_dm")
                        p.do("vector", lambda e, kt=kt:
                             e.tensor_tensor(Nt[:, :], es1[:, kt, :], Ba[:, :],
                                             ALU.mult), inc=f"{tag}_cv")
                        p.wait("vector", f"{tag}_cv")
                        p.do("vector", lambda e, kt=kt:
                             e.tensor_tensor(Asb[:, :], es2_[:, kt, :], Bb[:, :],
                                             ALU.mult), inc=f"{tag}_cv")
                        p.wait("vector", f"{tag}_cv")
                        p.do("vector", lambda e:
                             e.tensor_tensor(Nsb[:, :], Nt[:, :], Asb[:, :],
                                             ALU.subtract), inc=f"{tag}_cv")
                        p.wait("vector", f"{tag}_cv")
                        p.do("vector", lambda e:
                             e.tensor_scalar(Nt[:, :], Nsb[:, :], -1.0, None,
                                             op0=ALU.mult), inc=f"{tag}_cv")
                        p.wait("vector", f"{tag}_cv")
                        p.do("vector", lambda e:
                             e.tensor_tensor(Asb[:, :], Nsb[:, :], Nt[:, :],
                                             ALU.max), inc=f"{tag}_cv")
                        p.wait("tensor", f"{tag}_cv")
                        p.do("tensor", lambda e, kt=kt:
                             e.matmul(psD[:, :], c.ones_bf[:, :], Asb[:, :],
                                      start=(kt == 0), stop=(kt == nkt - 1)),
                             inc=f"{tag}_dm")
                        p.do("tensor", lambda e, kt=kt, hr=hr:
                             e.matmul(psO[0:64, :], vtm[:, kt, hr:hr + 64],
                                      Nsb[:, :], start=(kt == 0),
                                      stop=(kt == nkt - 1)), inc=f"{tag}_om")
                    if upto < 5:
                        continue
                    if dbga is not None and b == 0 and hp == 3 and h2 == 0 and qg == 0:
                        p.wait("gpsimd", f"{tag}_cv")
                        p.wait("gpsimd", f"{tag}_bv")
                        p.dma("gpsimd", dbga[0, 0:128, 0:512], es1[:, 0, :], inc=f"{tag}_dbg")
                        p.dma("gpsimd", dbga[1, 0:128, 0:512], es2_[:, 0, :], inc=f"{tag}_dbg")
                        p.dma("gpsimd", dbga[2, 0:128, 0:512], Ba[:, :], inc=f"{tag}_dbg")
                        p.dma("gpsimd", dbga[3, 0:128, 0:512], Bb[:, :], inc=f"{tag}_dbg")
                        p.dma("gpsimd", dbga[4, 0:128, 0:512], Nsb[:, :], inc=f"{tag}_dbg")
                        p.dma("gpsimd", dbga[5, 0:128, 0:512], Asb[:, :], inc=f"{tag}_dbg")
                        p.wait("gpsimd", f"{tag}_dbg")
                    p.wait("vector", f"{tag}_dm")
                    p.do("vector", lambda e: e.tensor_scalar(Bg[:, :], psD[0:64, :],
                         1e-6, None, op0=ALU.max), inc=f"{tag}_gv")
                    p.wait("vector", f"{tag}_gv")
                    p.do("vector", lambda e: e.reciprocal(Bg[:, :], Bg[:, :]),
                         inc=f"{tag}_gv")
                    p.wait("vector", f"{tag}_om")
                    p.wait("vector", f"{tag}_gv")
                    p.do("vector", lambda e, hr=hr, qsl=qsl:
                         e.tensor_tensor(hout[hr:hr + 64, qsl], psO[0:64, :],
                                         Bg[:, :], ALU.mult), inc=f"{tag}_hv")
                    p.wait("tensor", f"{tag}_hv")
                    if dbga is not None and b == 0 and hp == 3 and h2 == 1 and qg == 0:
                        p.wait("gpsimd", f"{tag}_hv")
                        p.wait("gpsimd", f"{tag}_gv")
                        p.dma("gpsimd", dbga[6, 0:64, 0:512], Bg[:, :], inc=f"{tag}_dbg")
                        p.dma("gpsimd", dbga[7, 0:128, 0:512], hout[:, 0:512], inc=f"{tag}_dbg")
                        p.wait("gpsimd", f"{tag}_dbg")
            for tg in range(NQG if upto >= 6 else 0):
                for dc in range(DC):
                    pb = PB[1 + dc % 2]
                    p.wait("tensor", f"{tag}_woe")
                    p.do("tensor", lambda e, dc=dc, tg=tg, pb=pb:
                         e.matmul(pb[:, :], wosl[:, dc * 128:(dc + 1) * 128],
                                  hout[:, tg * 512:(tg + 1) * 512],
                                  start=True, stop=True), inc=f"{tag}_wom")
                    p.wait("vector", f"{tag}_wom")
                    p.do("vector", lambda e, dc=dc, tg=tg, pb=pb:
                         e.tensor_tensor(st.resid[dc][:, tg * 512:(tg + 1) * 512],
                                         st.resid[dc][:, tg * 512:(tg + 1) * 512],
                                         pb[:, :], ALU.add), inc=f"{tag}_woe")
        p.barrier()
        emit_store_out(p, c, st, y_ap, t0=b * S, t1=(b + 1) * S,
                       psums=[PB[i][:, 0:128] for i in range(2)])
        rsub.close()
    p.barrier()
    es.close()


def _prep_common(inputs):
    d = {}
    d["l1_wfcT"] = np.ascontiguousarray(inputs["L1_ffn_W_fc"].T)
    d["l1_wprT"] = np.ascontiguousarray(inputs["L1_ffn_W_proj"].T)
    d["l1_lnw"] = np.ascontiguousarray(
        np.asarray(inputs["L1_ffn_ln_w"]).reshape(HC_FFN, 128).T)
    d["l1_lnb"] = np.ascontiguousarray(
        np.asarray(inputs["L1_ffn_ln_b"]).reshape(HC_FFN, 128).T)
    d["router"] = np.ascontiguousarray(
        np.asarray(inputs["L0_ffn_router_w"], dtype=np.float32)
        .reshape(DC, 128, E_MOE).transpose(1, 0, 2))
    d["g5"] = np.ascontiguousarray(
        np.asarray(inputs["L0_ffn_gate_w"], dtype=np.float32)
        .reshape(E_MOE, DC, 128, DC, 128).transpose(0, 2, 1, 3, 4))
    d["u5"] = np.ascontiguousarray(
        np.asarray(inputs["L0_ffn_up_w"], dtype=np.float32)
        .reshape(E_MOE, DC, 128, DC, 128).transpose(0, 2, 1, 3, 4))
    d["d5"] = np.ascontiguousarray(
        np.asarray(inputs["L0_ffn_down_w"], dtype=np.float32)
        .reshape(E_MOE, DC, 128, D).transpose(0, 2, 1, 3))
    def _wt(w):
        return np.ascontiguousarray(np.asarray(w, dtype=np.float32).T
                                    .reshape(DC, 128, -1).transpose(1, 0, 2))
    d["aq"] = _wt(inputs["L0_attn_Wq"])
    d["ak"] = _wt(inputs["L0_attn_Wk"])
    d["av"] = _wt(inputs["L0_attn_Wv"])
    d["aio"] = np.ascontiguousarray(np.concatenate(
        [np.asarray(inputs["L0_attn_Wi"], dtype=np.float32).T,
         np.asarray(inputs["L0_attn_Wf"], dtype=np.float32).T,
         np.asarray(inputs["L0_attn_Wo"], dtype=np.float32).T], axis=1)
        .reshape(DC, 128, 48).transpose(1, 0, 2))
    d["aout"] = _wt(inputs["L0_attn_Wout"])
    d["alnw"] = np.ascontiguousarray(
        np.asarray(inputs["L0_attn_ln_w"], dtype=np.float32).reshape(DC, 128).T)
    d["dq"] = _wt(inputs["L1_attn_Wq"])
    d["dk1"] = _wt(inputs["L1_attn_Wk1"])
    d["dk2"] = _wt(inputs["L1_attn_Wk2"])
    d["dv"] = _wt(inputs["L1_attn_Wv"])
    d["do"] = _wt(inputs["L1_attn_Wo"])
    d["dll"] = np.ascontiguousarray(
        np.asarray(inputs["L1_attn_lambda_logit"], dtype=np.float32).reshape(1, 16))
    return d


def _cmask(core):
    m = np.zeros((8, 1), np.float32)
    m[:core] = 1.0
    return m


def _x_shard(x, core):
    return np.ascontiguousarray(
        np.asarray(x).reshape(16, S, D)[core * BLOC:(core + 1) * BLOC]
        .reshape(T, D))


def build_l1ffn_launch(first, parts=("quant", "ln", "ffn")):
    nc = bass.Bass("TRN2", target_bir_lowering=False)
    x_in = nc.dram_tensor("x", [T, D], F32, kind="ExternalInput").ap()
    x_out = nc.dram_tensor("y", [T, D], F32, kind="ExternalOutput").ap()
    wfcT = nc.dram_tensor("l1_wfcT", [D, H_FFN], F32, kind="ExternalInput").ap()
    wprT = nc.dram_tensor("l1_wprT", [H_FFN, D], F32, kind="ExternalInput").ap()
    lnw = nc.dram_tensor("l1_lnw", [128, HC_FFN], F32, kind="ExternalInput").ap()
    lnb = nc.dram_tensor("l1_lnb", [128, HC_FFN], F32, kind="ExternalInput").ap()
    if first:
        qfc = nc.dram_tensor("qfc", [HC_FFN, DC, 128, 128], BF16,
                             kind="ExternalOutput").ap()
        qpr = nc.dram_tensor("qpr", [DC, HC_FFN, 128, 128], BF16,
                             kind="ExternalOutput").ap()
    else:
        qfc = nc.dram_tensor("qfc", [HC_FFN, DC, 128, 128], BF16,
                             kind="ExternalInput").ap()
        qpr = nc.dram_tensor("qpr", [DC, HC_FFN, 128, 128], BF16,
                             kind="ExternalInput").ap()
    p = Prog(nc)
    c = Consts(p)
    c.emit()
    if first and "quant" in parts:
        emit_quant(p, c, wfcT, qfc, D, H_FFN, "qf")
        emit_quant(p, c, wprT, qpr, H_FFN, D, "qp")
    emit_l1ffn_sub(p, c, x_in, x_out, qfc, qpr, lnw, lnb, "ff")
    return p.build()


def emit_l1ffn_sub(p, c, x_in, x_out, qfc, qpr, lnw, lnb, tag):
    for b in range(BLOC):
        sub = ExitStack()
        st = Stream(p, c, cols=S, stack=sub)
        lnxt = [p.sbuf(f"{tag}lnx{b}_{dc}", [128, S], BF16, stack=sub)
                for dc in range(DC)]
        emit_load_resid(p, c, st, x_in, t0=b * S, t1=(b + 1) * S)
        emit_ln(p, c, st, lnxt, rms2=False, name=f"{tag}ln{b}",
                tg0=b * (S // 512), tg1=(b + 1) * (S // 512), col0=b * S)
        emit_l1ffn(p, c, st, lnxt, qfc, qpr, lnw, lnb, f"{tag}{b}",
                   tg0=b * (S // 512), tg1=(b + 1) * (S // 512), col0=b * S)
        emit_store_out(p, c, st, x_out, t0=b * S, t1=(b + 1) * S)
        sub.close()
        p.epoch()


def emit_moe_sub(p, c, x_in, x_out, router, aps, tag):
    esk = ExitStack()
    rsb = p.sbuf(f"{tag}_rsb", [128, DC, E_MOE], F32, stack=esk)
    lgall = p.sbuf(f"{tag}_lg", [128, NTT * E_MOE], F32, stack=esk)
    p.dma("sync", rsb[:, :, :], router[:, :, :], inc=f"{tag}_rld")
    srcst = p.sbuf(f"{tag}_src", [128, NTT * 2], U32, stack=esk)
    wstash = p.sbuf(f"{tag}_wsh", [128, NTT * 2], F32, stack=esk)
    lnsub = ExitStack()
    lnxt = [p.sbuf(f"{tag}_lnx{dc}", [128, T], BF16, stack=lnsub)
            for dc in range(DC)]
    p.wait("tensor", f"{tag}_rld")
    for b in range(BLOC):
        rsub = ExitStack()
        rst = Stream(p, c, cols=S, stack=rsub)
        emit_load_resid(p, c, rst, x_in, t0=b * S, t1=(b + 1) * S)
        emit_ln(p, c, rst, lnxt, rms2=False, name=f"{tag}_ln{b}", router=rsb,
                lgall=lgall, tg0=b * (S // 512), tg1=(b + 1) * (S // 512),
                col0=0, rcol0=b * S)
        rsub.close()
    emit_l0ffn_route(p, c, lnxt, lgall, aps, tag, srcst, wstash)
    lnsub.close()
    emit_l0ffn_exec(p, c, aps, x_in, x_out, tag, srcst, wstash)
    esk.close()


def build_moe_launch():
    nc = bass.Bass("TRN2", target_bir_lowering=False)
    x_in = nc.dram_tensor("x", [T, D], F32, kind="ExternalInput").ap()
    x_out = nc.dram_tensor("y", [T, D], F32, kind="ExternalOutput").ap()
    router = nc.dram_tensor("router", [128, DC, E_MOE], F32, kind="ExternalInput").ap()
    g5 = nc.dram_tensor("g5", [E_MOE, 128, DC, DC, 128], F32, kind="ExternalInput").ap()
    u5 = nc.dram_tensor("u5", [E_MOE, 128, DC, DC, 128], F32, kind="ExternalInput").ap()
    d5 = nc.dram_tensor("d5", [E_MOE, 128, DC, D], F32, kind="ExternalInput").ap()
    cmask = nc.dram_tensor("cmask", [8, 1], F32, kind="ExternalInput").ap()
    aps = {
        "xe": nc.dram_tensor("moe_xe", [NSLOT, D], BF16, kind="Internal").ap(),
        "outsc": nc.dram_tensor("moe_out", [NSLOT + 1, D], BF16, kind="Internal").ap(),
        "cc_in": nc.dram_tensor("moe_ccin", [1, 32], F32, kind="Internal").ap(),
        "cc_out": nc.dram_tensor("moe_ccout", [8, 32], F32, kind="Internal").ap(),
        "g5": g5, "u5": u5, "d5": d5, "cmask": cmask,
    }
    p = Prog(nc)
    c = Consts(p)
    c.emit()
    emit_moe_sub(p, c, x_in, x_out, router, aps, "mo")
    return p.build()


def build_l0attn_launch():
    nc = bass.Bass("TRN2", target_bir_lowering=False)
    x_in = nc.dram_tensor("x", [T, D], F32, kind="ExternalInput").ap()
    x_out = nc.dram_tensor("y", [T, D], F32, kind="ExternalOutput").ap()
    waps = {}
    for nm, sh in (("aq", [128, DC, D]), ("ak", [128, DC, D]), ("av", [128, DC, D]),
                   ("aio", [128, DC, 48]), ("aout", [128, DC, D]),
                   ("alnw", [128, DC])):
        waps[nm] = nc.dram_tensor(nm, sh, F32, kind="ExternalInput").ap()
    p = Prog(nc)
    c = Consts(p)
    c.emit()
    emit_l0attn(p, c, waps, x_in, x_out, "at")
    return p.build()


def build_l1attn_launch(upto=9, dbg=False):
    nc = bass.Bass("TRN2", target_bir_lowering=False)
    x_in = nc.dram_tensor("x", [T, D], F32, kind="ExternalInput").ap()
    x_out = nc.dram_tensor("y", [T, D], F32, kind="ExternalOutput").ap()
    dbga = (nc.dram_tensor("dbga", [8, 128, 512], F32, kind="ExternalOutput").ap()
            if dbg else None)
    dbgb = (nc.dram_tensor("dbgb", [9, 128, 512], BF16, kind="ExternalOutput").ap()
            if dbg else None)
    waps = {}
    for nm in ("dq", "dk1", "dk2", "dv", "do"):
        waps[nm] = nc.dram_tensor(nm, [128, DC, D], F32, kind="ExternalInput").ap()
    waps["dll"] = nc.dram_tensor("dll", [1, 16], F32, kind="ExternalInput").ap()
    p = Prog(nc)
    c = Consts(p)
    c.emit()
    emit_l1attn(p, c, waps, x_in, x_out, "da", upto=upto, dbga=dbga, dbgb=dbgb)
    return p.build()


def build_fused(dbg=False):
    nc = bass.Bass("TRN2", target_bir_lowering=False)
    x_in = nc.dram_tensor("x", [T, D], F32, kind="ExternalInput").ap()
    y_out = nc.dram_tensor("y", [T, D], F32, kind="ExternalOutput").ap()
    xi = [nc.dram_tensor(f"xi{i}", [T, D], F32,
                         kind=("ExternalOutput" if dbg else "Internal")).ap()
          for i in range(5)]
    router = nc.dram_tensor("router", [128, DC, E_MOE], F32, kind="ExternalInput").ap()
    g5 = nc.dram_tensor("g5", [E_MOE, 128, DC, DC, 128], F32, kind="ExternalInput").ap()
    u5 = nc.dram_tensor("u5", [E_MOE, 128, DC, DC, 128], F32, kind="ExternalInput").ap()
    d5 = nc.dram_tensor("d5", [E_MOE, 128, DC, D], F32, kind="ExternalInput").ap()
    cmask = nc.dram_tensor("cmask", [8, 1], F32, kind="ExternalInput").ap()
    aps = {
        "xe": nc.dram_tensor("moe_xe", [NSLOT, D], BF16, kind="Internal").ap(),
        "outsc": nc.dram_tensor("moe_out", [NSLOT + 1, D], BF16, kind="Internal").ap(),
        "cc_in": nc.dram_tensor("moe_ccin", [1, 32], F32, kind="Internal").ap(),
        "cc_out": nc.dram_tensor("moe_ccout", [8, 32], F32, kind="Internal").ap(),
        "g5": g5, "u5": u5, "d5": d5, "cmask": cmask,
    }
    aps2 = dict(aps)
    aps2["cc_in"] = nc.dram_tensor("moe_ccin2", [1, 32], F32, kind="Internal").ap()
    aps2["cc_out"] = nc.dram_tensor("moe_ccout2", [8, 32], F32, kind="Internal").ap()
    awaps = {}
    for nm, sh in (("aq", [128, DC, D]), ("ak", [128, DC, D]), ("av", [128, DC, D]),
                   ("aio", [128, DC, 48]), ("aout", [128, DC, D]),
                   ("alnw", [128, DC])):
        awaps[nm] = nc.dram_tensor(nm, sh, F32, kind="ExternalInput").ap()
    dwaps = {}
    for nm in ("dq", "dk1", "dk2", "dv", "do"):
        dwaps[nm] = nc.dram_tensor(nm, [128, DC, D], F32, kind="ExternalInput").ap()
    dwaps["dll"] = nc.dram_tensor("dll", [1, 16], F32, kind="ExternalInput").ap()
    wfcT = nc.dram_tensor("l1_wfcT", [D, H_FFN], F32, kind="ExternalInput").ap()
    wprT = nc.dram_tensor("l1_wprT", [H_FFN, D], F32, kind="ExternalInput").ap()
    lnw = nc.dram_tensor("l1_lnw", [128, HC_FFN], F32, kind="ExternalInput").ap()
    lnb = nc.dram_tensor("l1_lnb", [128, HC_FFN], F32, kind="ExternalInput").ap()
    qfc = nc.dram_tensor("qfc", [HC_FFN, DC, 128, 128], BF16, kind="Internal").ap()
    qpr = nc.dram_tensor("qpr", [DC, HC_FFN, 128, 128], BF16, kind="Internal").ap()

    p = Prog(nc)
    c = Consts(p)
    c.emit()
    emit_quant(p, c, wfcT, qfc, D, H_FFN, "qf")
    p.epoch()
    emit_quant(p, c, wprT, qpr, H_FFN, D, "qp")
    p.epoch()
    emit_moe_sub(p, c, x_in, xi[0], router, aps, "m1")
    p.epoch()
    emit_l0attn(p, c, awaps, xi[0], xi[1], "at")
    p.epoch()
    emit_moe_sub(p, c, xi[1], xi[2], router, aps2, "m2")
    p.epoch()
    emit_l1ffn_sub(p, c, xi[2], xi[3], qfc, qpr, lnw, lnb, "fa")
    p.epoch()
    emit_l1attn(p, c, dwaps, xi[3], xi[4], "da")
    p.epoch()
    emit_l1ffn_sub(p, c, xi[4], y_out, qfc, qpr, lnw, lnb, "fb")
    return p.build()


def kernel(**inputs):
    com = _prep_common(inputs)
    x = np.asarray(inputs["L0_attn_x"], dtype=np.float32)
    nc = build_fused()
    in_maps = []
    for core in range(NCORES):
        m = {"x": _x_shard(x, core), "cmask": _cmask(core)}
        for k in ("router", "g5", "u5", "d5", "aq", "ak", "av", "aio", "aout",
                  "alnw", "dq", "dk1", "dk2", "dv", "do", "dll",
                  "l1_wfcT", "l1_wprT", "l1_lnw", "l1_lnb"):
            m[k] = com[k]
        in_maps.append(m)
    res = run_bass_kernel_spmd(nc, in_maps, core_ids=list(range(NCORES)))
    out = np.concatenate([r["y"].reshape(BLOC, S, D) for r in res.results], axis=0)
    return out


def kernel_unfused(**inputs):
    com = _prep_common(inputs)
    x = np.asarray(inputs["L0_attn_x"], dtype=np.float32)
    shards = [_x_shard(x, core) for core in range(NCORES)]

    nc_moe = build_moe_launch()
    moe_keys = ("router", "g5", "u5", "d5")
    def run_moe(shards):
        in_maps = [{"x": shards[core], **{k: com[k] for k in moe_keys},
                    "cmask": _cmask(core)} for core in range(NCORES)]
        res = run_bass_kernel_spmd(nc_moe, in_maps, core_ids=list(range(NCORES)))
        return [r["y"] for r in res.results]

    shards = run_moe(shards)

    nc_at = build_l0attn_launch()
    at_keys = ("aq", "ak", "av", "aio", "aout", "alnw")
    in_maps = [{"x": shards[core], **{k: com[k] for k in at_keys}}
               for core in range(NCORES)]
    res = run_bass_kernel_spmd(nc_at, in_maps, core_ids=list(range(NCORES)))
    shards = [r["y"] for r in res.results]

    shards = run_moe(shards)

    nc_f1 = build_l1ffn_launch(first=True)
    ff_keys = ("l1_wfcT", "l1_wprT", "l1_lnw", "l1_lnb")
    in_maps = [{"x": shards[core], **{k: com[k] for k in ff_keys}}
               for core in range(NCORES)]
    res = run_bass_kernel_spmd(nc_f1, in_maps, core_ids=list(range(NCORES)))
    shards = [r["y"] for r in res.results]
    qfc_s = [r["qfc"] for r in res.results]
    qpr_s = [r["qpr"] for r in res.results]

    nc_da = build_l1attn_launch()
    da_keys = ("dq", "dk1", "dk2", "dv", "do", "dll")
    in_maps = [{"x": shards[core], **{k: com[k] for k in da_keys}}
               for core in range(NCORES)]
    res = run_bass_kernel_spmd(nc_da, in_maps, core_ids=list(range(NCORES)))
    shards = [r["y"] for r in res.results]

    nc_f2 = build_l1ffn_launch(first=False)
    in_maps = [{"x": shards[core], **{k: com[k] for k in ff_keys},
                "qfc": qfc_s[core], "qpr": qpr_s[core]}
               for core in range(NCORES)]
    res = run_bass_kernel_spmd(nc_f2, in_maps, core_ids=list(range(NCORES)))
    out = np.concatenate([r["y"].reshape(BLOC, S, D) for r in res.results], axis=0)
    return out


# version 10
# speedup vs baseline: 1.0012x; 1.0012x over previous
import numpy as np
from contextlib import ExitStack
import concourse.bass as bass
import concourse.mybir as mybir
from concourse.bass_utils import run_bass_kernel_spmd
from concourse.bass import IndirectOffsetOnAxis

F32 = mybir.dt.float32
BF16 = mybir.dt.bfloat16
U32 = mybir.dt.uint32
AF = mybir.ActivationFunctionType
ALU = mybir.AluOpType
AX = mybir.AxisListType

NCORES = 8
S = 2048
D = 1024
BLOC = 2
T = BLOC * S
DC = D // 128
NTT = T // 128
NTG = T // 512
H_FFN = 4096
HC_FFN = H_FFN // 128
EPS = 1e-5
LN_EPS = 1e-5

E_MOE = 16
K_MOE = 2
CAP_GLOBAL = 5120
CL = 1024
NSLOT = E_MOE * CL

NH = 16
HD = 64
GATE_CAP = 15.0
CHUNK = 128
NCH = S // CHUNK


class _CntView:

    def __init__(self, prog):
        self.p = prog

    def __getitem__(self, name):
        self.p.sem(name)
        return self.p._physcnt[self.p._name2phys[name]]

    def get(self, name, default=0):
        return self[name]

    def __setitem__(self, name, v):
        self.p._physcnt[self.p._name2phys[name]] = v


class Prog:
    ENGINES = ("sync", "scalar", "vector", "tensor", "gpsimd")
    NPHYS = 96

    def __init__(self, nc):
        self.nc = nc
        self.es = ExitStack()
        self.q = {e: [] for e in self.ENGINES}
        self._phys = [self.es.enter_context(nc.semaphore(f"ps{i}"))
                      for i in range(self.NPHYS)]
        self._physcnt = [0] * self.NPHYS
        self._name2phys = {"bar": 0}
        self._cursor = 1
        self.cnt = _CntView(self)

    def epoch(self):
        self._name2phys = {"bar": 0}
        self._cursor = 1

    def sem(self, name):
        if name not in self._name2phys:
            if self._cursor >= self.NPHYS:
                raise RuntimeError("out of physical semaphores in this epoch")
            self._name2phys[name] = self._cursor
            self._cursor += 1
        return self._phys[self._name2phys[name]]

    def sbuf(self, name, shape, dtype=F32, stack=None):
        return (stack or self.es).enter_context(self.nc.sbuf_tensor(name, shape, dtype))

    def psum(self, name, shape, dtype=F32, stack=None):
        return (stack or self.es).enter_context(self.nc.psum_tensor(name, shape, dtype))

    def do(self, eng, fn, inc=None, n=1):
        if inc is not None:
            s = self.sem(inc)
            idx = self._name2phys[inc]
            self._physcnt[idx] += n
            def wrapped(e, fn=fn, s=s, n=n):
                fn(e).then_inc(s, n)
            self.q[eng].append(wrapped)
            return self._physcnt[idx]
        self.q[eng].append(fn)
        return None

    def dma(self, eng, out, in_, inc):
        return self.do(eng, lambda e, o=out, i=in_: e.dma_start(out=o, in_=i),
                       inc=inc, n=16)

    def wait(self, eng, sem, val=None):
        s = self.sem(sem)
        if val is None:
            val = self.cnt[sem]
        if val <= 0:
            return
        self.q[eng].append(lambda e, s=s, v=val: e.wait_ge(s, v))

    def mark(self, sem):
        self.sem(sem)
        return self.cnt[sem]

    def barrier_epoch(self):
        self.barrier()
        self.epoch()

    def barrier(self, engines=None):
        engines = list(engines or self.ENGINES)
        for eng in engines:
            if eng in ("vector", "scalar", "tensor", "gpsimd"):
                self.q[eng].append(lambda e: e.drain())
            self.do(eng, lambda e: e.nop(), inc="bar")
        tgt = self.cnt["bar"]
        for eng in engines:
            self.wait(eng, "bar", tgt)

    def build(self):
        nc = self.nc
        with nc.Block() as block:
            for engname in self.ENGINES:
                if not self.q[engname]:
                    continue
                def body(eng, engname=engname):
                    for fn in self.q[engname]:
                        fn(eng)
                getattr(block, engname)(body)
        self.es.close()
        return nc


CONST_COL_VALS = (LN_EPS, 1e-6, 1.0, -12.0, 0.5, 2147483648.0, float(NSLOT), 1.0 / 128.0)


class Consts:
    def __init__(self, p: Prog):
        self.p = p
        self.cpool = p.sbuf("c_pool", [128, len(CONST_COL_VALS)], F32)
        self.ones_bf = p.sbuf("c_ones_bf", [128, 128], BF16)
        self.ones_f = p.sbuf("c_ones_f", [128, 128], F32)
        self.ident_bf = p.sbuf("c_id_bf", [128, 128], BF16)
        self.ident_f = p.sbuf("c_id_f", [128, 128], F32)
        self.iota_col = p.sbuf("c_iotac", [128, 1], F32)
        self.iota_row16 = p.sbuf("c_iotar16", [128, 16], F32)
        self.triu_excl_bf = p.sbuf("c_triux", [128, 128], BF16)
        self.triu_excl_f = p.sbuf("c_triuxf", [128, 128], F32)
        self.triu_incl_f = p.sbuf("c_triuif", [128, 128], F32)
        self.iota_row32 = p.sbuf("c_iotar32", [128, 32], F32)
        self.mask_sq = p.sbuf("c_masksq", [128, 128], F32)

    def col(self, v, parts=128):
        i = CONST_COL_VALS.index(v)
        return self.cpool[0:parts, i:i + 1]

    def emit(self):
        p = self.p
        es = ExitStack()
        for i, v in enumerate(CONST_COL_VALS):
            p.do("vector", lambda e, i=i, v=v:
                 e.memset(self.cpool[:, i:i + 1], float(v)), inc="cst")
        s = p.sbuf("c_scr", [128, 128], F32, stack=es)
        s2 = p.sbuf("c_scr2", [128, 128], F32, stack=es)
        p.do("gpsimd", lambda e: e.iota(self.iota_col[:, :], pattern=[[0, 1]],
                                        base=0, channel_multiplier=1,
                                        allow_small_or_imprecise_dtypes=True), inc="cst")
        p.do("gpsimd", lambda e: e.iota(self.iota_row16[:, :], pattern=[[1, 16]],
                                        base=0, channel_multiplier=0,
                                        allow_small_or_imprecise_dtypes=True), inc="cst")
        p.do("gpsimd", lambda e: e.iota(self.iota_row32[:, :], pattern=[[1, 32]],
                                        base=0, channel_multiplier=0,
                                        allow_small_or_imprecise_dtypes=True), inc="cst")
        p.do("gpsimd", lambda e: e.iota(s[:, :], pattern=[[1, 128]],
                                        base=0, channel_multiplier=0,
                                        allow_small_or_imprecise_dtypes=True), inc="cst")
        p.wait("vector", "cst")
        p.do("vector", lambda e: e.memset(self.ones_bf[:, :], 1.0), inc="cst")
        p.do("vector", lambda e: e.memset(self.ones_f[:, :], 1.0), inc="cst")
        p.do("vector", lambda e: e.tensor_scalar(self.ident_f[:, :], s[:, :],
                                                 self.iota_col[:, 0:1], None,
                                                 op0=ALU.is_equal), inc="cst")
        p.wait("vector", "cst")
        p.do("vector", lambda e: e.tensor_copy(self.ident_bf[:, :], self.ident_f[:, :]),
             inc="cst")
        p.do("vector", lambda e: e.tensor_scalar(s2[:, :], s[:, :],
                                                 self.iota_col[:, 0:1], None,
                                                 op0=ALU.is_gt), inc="cst")
        p.wait("vector", "cst")
        p.do("vector", lambda e: e.tensor_copy(self.triu_excl_bf[:, :], s2[:, :]),
             inc="cst")
        p.do("vector", lambda e: e.tensor_copy(self.triu_excl_f[:, :], s2[:, :]),
             inc="cst")
        p.wait("vector", "cst")
        p.do("vector", lambda e: e.tensor_scalar(s2[:, :], s[:, :],
                                                 self.iota_col[:, 0:1], None,
                                                 op0=ALU.is_ge), inc="cst")
        p.wait("vector", "cst")
        p.do("vector", lambda e: e.tensor_copy(self.triu_incl_f[:, :], s2[:, :]),
             inc="cst")
        p.wait("vector", "cst")
        p.do("vector", lambda e: e.tensor_scalar(s2[:, :], s[:, :],
                                                 self.iota_col[:, 0:1], None,
                                                 op0=ALU.is_lt), inc="cst")
        p.wait("vector", "cst")
        p.do("vector", lambda e: e.tensor_scalar(self.mask_sq[:, :], s2[:, :],
                                                 -1e30, None, op0=ALU.mult), inc="cst")
        p.wait("vector", "cst")
        p.barrier()
        es.close()


class Stream:
    def __init__(self, p: Prog, c: Consts, cols=T, stack=None):
        self.p = p
        self.c = c
        self.cols = cols
        Stream._n = getattr(Stream, "_n", 0) + 1
        n = Stream._n
        self.resid = [p.sbuf(f"resid{n}_{dc}", [128, cols], F32, stack=stack)
                      for dc in range(DC)]
        self.nln = 0

    def alloc_lnx(self, stack):
        self.nln += 1
        return [self.p.sbuf(f"lnx{self.nln}_{dc}", [128, T], BF16, stack=stack)
                for dc in range(DC)]


def emit_load_resid(p: Prog, c: Consts, st: Stream, x_ap, t0=0, t1=None,
                    psums=None):
    t1 = T if t1 is None else t1
    es = ExitStack()
    Stream._ld = getattr(Stream, "_ld", 0) + 1
    pre = f"ld{Stream._ld}"
    xtm = [p.sbuf(f"{pre}_xtm{i}", [128, D], F32, stack=es) for i in range(2)]
    pst = (psums if psums is not None else
           [p.psum(f"{pre}_pst{i}", [128, 128], F32, stack=es) for i in range(4)])
    tp_marks = []
    ev_marks = []
    dma_marks = []
    ntt = (t1 - t0) // 128
    for tt in range(ntt):
        buf = xtm[tt % 2]
        if tt >= 2:
            p.wait("gpsimd", f"{pre}_tp", tp_marks[(tt - 2) * DC + DC - 1])
        dma_marks.append(p.dma("gpsimd", buf[:, :],
                               x_ap[t0 + tt * 128:t0 + (tt + 1) * 128, :],
                               inc=f"{pre}_x"))
        for dc in range(DC):
            n = tt * DC + dc
            bank = n % 4
            p.wait("tensor", f"{pre}_x", dma_marks[tt])
            if n >= 4:
                p.wait("tensor", f"{pre}_ev", ev_marks[n - 4])
            tp_marks.append(p.do("tensor", lambda e, b=buf, dc=dc, bank=bank:
                 e.transpose(pst[bank][:, :],
                             b[:, dc * 128:(dc + 1) * 128], c.ident_f[:, :]),
                 inc=f"{pre}_tp"))
            p.wait("vector", f"{pre}_tp", tp_marks[-1])
            ev_marks.append(p.do("vector", lambda e, dc=dc, tt=tt, bank=bank:
                 e.tensor_copy(st.resid[dc][:, tt * 128:(tt + 1) * 128],
                               pst[bank][:, :]),
                 inc=f"{pre}_ev"))
    p.barrier()
    es.close()


def emit_store_out(p: Prog, c: Consts, st: Stream, out_ap, t0=0, t1=None,
                   psums=None):
    t1 = T if t1 is None else t1
    es = ExitStack()
    Stream._st = getattr(Stream, "_st", 0) + 1
    pre = f"st{Stream._st}"
    otm = [p.sbuf(f"{pre}_otm{i}", [128, D], F32, stack=es) for i in range(2)]
    psf = (psums if psums is not None else
           [p.psum(f"{pre}_ps{i}", [128, 128], F32, stack=es) for i in range(2)])
    out_marks = []
    ntt = (t1 - t0) // 128
    for tt in range(ntt):
        buf = otm[tt % 2]
        if tt >= 2:
            p.wait("vector", f"{pre}_out", out_marks[tt - 2])
        for dc in range(DC):
            bank = dc % 2
            p.wait("tensor", f"{pre}_st")
            p.do("tensor", lambda e, dc=dc, tt=tt, bank=bank:
                 e.transpose(psf[bank][:, :],
                             st.resid[dc][:, tt * 128:(tt + 1) * 128],
                             c.ident_f[:, :]), inc=f"{pre}_tp")
            p.wait("vector", f"{pre}_tp")
            p.do("vector", lambda e, b=buf, dc=dc, bank=bank:
                 e.tensor_copy(b[:, dc * 128:(dc + 1) * 128],
                               psf[bank][:, :]), inc=f"{pre}_st")
        p.wait("sync", f"{pre}_st")
        out_marks.append(p.dma("sync", out_ap[t0 + tt * 128:t0 + (tt + 1) * 128, :],
                               buf[:, :], inc=f"{pre}_out"))
    p.wait("sync", f"{pre}_out")
    p.barrier()
    es.close()


def emit_ln(p: Prog, c: Consts, st: Stream, lnx, rms2=False, name="ln",
            router=None, lgall=None, tg0=0, tg1=NTG, col0=0, psums=None,
            r2out=None, rcol0=None):
    es = ExitStack()
    sq = p.sbuf(f"{name}_sq", [128, 512], BF16, stack=es)
    xbf = p.sbuf(f"{name}_xbf", [128, 512], BF16, stack=es)
    rowm = p.sbuf(f"{name}_rm", [1, 512], F32, stack=es)
    rowv = p.sbuf(f"{name}_rv", [1, 512], F32, stack=es)
    rowr = p.sbuf(f"{name}_rr", [1, 512], F32, stack=es)
    rowt = p.sbuf(f"{name}_rt", [1, 512], F32, stack=es)
    rowu = p.sbuf(f"{name}_ru", [1, 512], F32, stack=es)
    tmp = p.sbuf(f"{name}_tmp", [128, 512], F32, stack=es)
    tmpf = (p.sbuf(f"{name}_tmpf", [128, 512], F32, stack=es)
            if router is not None else None)
    if psums is None:
        ps_s = p.psum(f"{name}_ps", [128, 512], F32, stack=es)
        ps_q = p.psum(f"{name}_pq", [128, 512], F32, stack=es)
        ps_br = p.psum(f"{name}_br", [128, 512], F32, stack=es)
        ps_bm = p.psum(f"{name}_bm", [128, 512], F32, stack=es)
    else:
        ps_s, ps_q, ps_br, ps_bm = psums
    ps_lg = [p.psum(f"{name}_lg{i}", [128, 16], F32, stack=es)
             for i in range(4)] if router is not None else None
    if rcol0 is None:
        rcol0 = col0
    sqmm = 0
    for g in range(tg0, tg1):
        sl = slice(g * 512, (g + 1) * 512)
        sls = slice(g * 512 - col0, (g + 1) * 512 - col0)
        rsl = slice(g * 512 - rcol0, (g + 1) * 512 - rcol0)
        for dc in range(DC):
            p.wait("vector", f"{name}_bc")
            p.wait("vector", f"{name}_mm", sqmm)
            p.do("vector", lambda e, dc=dc, rsl=rsl:
                 e.tensor_tensor(sq[:, :], st.resid[dc][:, rsl], st.resid[dc][:, rsl],
                                 ALU.mult), inc=f"{name}_sqd")
            p.do("vector", lambda e, dc=dc, rsl=rsl:
                 e.tensor_copy(xbf[:, :], st.resid[dc][:, rsl]), inc=f"{name}_sqd")
            p.wait("tensor", f"{name}_sqd")
            p.do("tensor", lambda e, dc=dc:
                 e.matmul(ps_s[0:1, :], c.ones_bf[:, 0:1], xbf[:, :],
                          start=(dc == 0), stop=(dc == DC - 1)), inc=f"{name}_mm")
            sqmm = p.do("tensor", lambda e, dc=dc:
                 e.matmul(ps_q[0:1, :], c.ones_bf[:, 0:1], sq[:, :],
                          start=(dc == 0), stop=(dc == DC - 1)), inc=f"{name}_mm")
        p.wait("vector", f"{name}_mm")
        p.do("vector", lambda e: e.tensor_scalar(rowm[:, :], ps_s[0:1, :],
                                                 1.0 / D, None, op0=ALU.mult),
             inc=f"{name}_r")
        p.do("vector", lambda e: e.tensor_scalar(rowv[:, :], ps_q[0:1, :],
                                                 1.0 / D, None, op0=ALU.mult),
             inc=f"{name}_r")
        p.wait("vector", f"{name}_r")
        p.do("vector", lambda e: e.tensor_tensor(rowt[:, :], rowm[:, :], rowm[:, :],
                                                 ALU.mult), inc=f"{name}_r")
        p.wait("vector", f"{name}_r")
        p.do("vector", lambda e: e.tensor_tensor(rowv[:, :], rowv[:, :], rowt[:, :],
                                                 ALU.subtract), inc=f"{name}_r")
        p.wait("scalar", f"{name}_r")
        p.do("scalar", lambda e: e.activation(rowt[:, :], rowv[:, :], AF.Ln,
                                              bias=c.col(LN_EPS, 1)), inc=f"{name}_a")
        p.wait("scalar", f"{name}_a")
        p.do("scalar", lambda e: e.activation(rowr[:, :], rowt[:, :], AF.Exp,
                                              scale=-0.5), inc=f"{name}_a")
        if rms2:
            p.wait("vector", f"{name}_a")
            p.do("vector", lambda e: e.tensor_scalar(rowt[:, :], rowv[:, :],
                                                     LN_EPS, None, op0=ALU.add),
                 inc=f"{name}_r")
            p.wait("vector", f"{name}_r")
            p.do("vector", lambda e: e.reciprocal(rowu[:, :], rowt[:, :]),
                 inc=f"{name}_r")
            p.wait("vector", f"{name}_r")
            p.do("vector", lambda e: e.tensor_tensor(rowu[:, :], rowu[:, :],
                                                     rowv[:, :], ALU.mult),
                 inc=f"{name}_r")
            p.wait("scalar", f"{name}_r")
            p.do("scalar", lambda e: e.activation(rowt[:, :], rowu[:, :], AF.Ln,
                                                  bias=c.col(LN_EPS, 1)), inc=f"{name}_a")
            p.wait("scalar", f"{name}_a")
            p.do("scalar", lambda e: e.activation(rowu[:, :], rowt[:, :], AF.Exp,
                                                  scale=-0.5), inc=f"{name}_a")
            p.wait("vector", f"{name}_a")
            p.do("vector", lambda e, sls=sls: e.tensor_copy(r2out[0:1, sls],
                                                            rowu[:, :]),
                 inc=f"{name}_r")
        p.wait("vector", f"{name}_a")
        p.wait("vector", f"{name}_r")
        p.do("vector", lambda e: e.tensor_tensor(rowm[:, :], rowm[:, :], rowr[:, :],
                                                 ALU.mult), inc=f"{name}_r")
        p.wait("tensor", f"{name}_r")
        p.do("tensor", lambda e: e.matmul(ps_br[:, :], c.ones_f[0:1, :], rowr[:, :],
                                          start=True, stop=True), inc=f"{name}_bcm")
        p.do("tensor", lambda e: e.matmul(ps_bm[:, :], c.ones_f[0:1, :], rowm[:, :],
                                          start=True, stop=True), inc=f"{name}_bcm")
        p.wait("vector", f"{name}_bcm")
        for dc in range(DC):
            p.do("vector", lambda e, dc=dc, rsl=rsl:
                 e.tensor_tensor(tmp[:, :], st.resid[dc][:, rsl], ps_br[:, :],
                                 ALU.mult), inc=f"{name}_bc")
            p.wait("vector", f"{name}_bc")
            if router is not None:
                p.wait("vector", f"{name}_lgm")
                p.do("vector", lambda e, dc=dc, sl=sl:
                     e.tensor_tensor(tmpf[:, :], tmp[:, :], ps_bm[:, :],
                                     ALU.subtract), inc=f"{name}_bc")
                p.wait("vector", f"{name}_bc")
                p.do("vector", lambda e, dc=dc, sls=sls:
                     e.tensor_copy(lnx[dc][:, sls], tmpf[:, :]), inc=f"{name}_bc")
                p.wait("tensor", f"{name}_bc")
                for ts4 in range(4):
                    tile = g * 4 + ts4
                    p.do("tensor", lambda e, dc=dc, ts4=ts4, tile=tile:
                         e.matmul(ps_lg[ts4][:, :],
                                  tmpf[:, ts4 * 128:(ts4 + 1) * 128],
                                  router[:, dc, :],
                                  start=(dc == 0),
                                  stop=(dc == DC - 1)),
                         inc=f"{name}_lgm")
            else:
                p.do("vector", lambda e, dc=dc, sls=sls:
                     e.tensor_tensor(lnx[dc][:, sls], tmp[:, :], ps_bm[:, :],
                                     ALU.subtract), inc=f"{name}_bc")
        if router is not None:
            p.wait("vector", f"{name}_lgm")
            for ts4 in range(4):
                tile = g * 4 + ts4
                p.do("vector", lambda e, ts4=ts4, tile=tile:
                     e.tensor_copy(lgall[:, tile * 16:(tile + 1) * 16],
                                   ps_lg[ts4][:, :]),
                     inc=f"{name}_bc")
            p.wait("tensor", f"{name}_bc")
    p.barrier()
    es.close()


def emit_quant(p: Prog, c: Consts, w_ap, q_ap, nrow, ncol, name):
    es = ExitStack()
    nrt = nrow // 128
    wt = [p.sbuf(f"{name}_w{i}", [128, ncol], F32, stack=es) for i in range(2)]
    ab = p.sbuf(f"{name}_ab", [128, ncol], F32, stack=es)
    sg = p.sbuf(f"{name}_sg", [128, ncol], F32, stack=es)
    qo = p.sbuf(f"{name}_qo", [128, ncol], BF16, stack=es)
    acc = p.sbuf(f"{name}_acc", [128, nrt], F32, stack=es)
    red = p.sbuf(f"{name}_red", [128, 1], F32, stack=es)
    thr = p.sbuf(f"{name}_thr", [128, 1], F32, stack=es)
    pst = p.psum(f"{name}_ps", [128, 512], F32, stack=es)
    a_marks = []
    for i in range(nrt):
        if i >= 2:
            p.wait("sync", f"{name}_a", a_marks[i - 2])
        m = p.dma("sync", wt[i % 2][:, :], w_ap[i * 128:(i + 1) * 128, :],
                  inc=f"{name}_l")
        p.wait("scalar", f"{name}_l", m)
        a_marks.append(p.do("scalar", lambda e, i=i:
             e.activation(ab[:, :], wt[i % 2][:, :], AF.Abs,
                          accum_out=acc[:, i:i + 1]), inc=f"{name}_a"))
    p.wait("vector", f"{name}_a")
    p.do("vector", lambda e: e.tensor_reduce(red[:, :], acc[:, 0:nrt], AX.X, ALU.add),
         inc=f"{name}_r")
    p.wait("tensor", f"{name}_r")
    p.do("tensor", lambda e: e.matmul(pst[0:1, 0:1], c.ones_f[:, 0:1], red[:, :],
                                      start=True, stop=True), inc=f"{name}_mm")
    p.wait("vector", f"{name}_mm")
    p.do("vector", lambda e: e.tensor_scalar(red[0:1, 0:1], pst[0:1, 0:1],
                                             0.5 / (nrow * ncol), 0.5e-6,
                                             op0=ALU.mult, op1=ALU.add), inc=f"{name}_r")
    p.wait("tensor", f"{name}_r")
    p.do("tensor", lambda e: e.matmul(pst[:, 2:3], c.ones_f[0:1, :], red[0:1, 0:1],
                                      start=True, stop=True), inc=f"{name}_mm")
    p.wait("vector", f"{name}_mm")
    p.do("vector", lambda e: e.tensor_copy(thr[:, :], pst[:, 2:3]), inc=f"{name}_r")
    p.wait("scalar", f"{name}_r")
    s_marks = []
    for i in range(nrt):
        if i >= 2:
            p.wait("sync", f"{name}_r2", s_marks[i - 2][0])
        m = p.dma("sync", wt[i % 2][:, :], w_ap[i * 128:(i + 1) * 128, :],
                  inc=f"{name}_l")
        p.wait("scalar", f"{name}_l", m)
        if i >= 1:
            p.wait("scalar", f"{name}_r2", s_marks[i - 1][0])
        p.do("scalar", lambda e, i=i: e.activation(ab[:, :], wt[i % 2][:, :], AF.Abs),
             inc=f"{name}_a2")
        p.do("scalar", lambda e, i=i: e.activation(sg[:, :], wt[i % 2][:, :], AF.Sign),
             inc=f"{name}_a2")
        p.wait("vector", f"{name}_a2")
        if i >= 1:
            p.wait("vector", f"{name}_sc", s_marks[i - 1][1])
        p.do("vector", lambda e: e.tensor_scalar(ab[:, :], ab[:, :], thr[:, 0:1],
                                                 None, op0=ALU.is_gt), inc=f"{name}_r2")
        p.wait("vector", f"{name}_r2")
        mr = p.do("vector", lambda e: e.tensor_tensor(qo[:, :], ab[:, :], sg[:, :],
                                                      ALU.mult), inc=f"{name}_r2")
        p.wait("gpsimd", f"{name}_r2", mr)
        src = qo[:, :].rearrange("p (mc m) -> p mc m", m=128)
        dst = q_ap[:, i, :, :].rearrange("mc p m -> p mc m")
        ms = p.dma("gpsimd", dst, src, inc=f"{name}_sc")
        s_marks.append((mr, ms))
    p.wait("gpsimd", f"{name}_sc")
    p.barrier()
    es.close()


def emit_l1ffn(p: Prog, c: Consts, st: Stream, lnx, qfc_ap, qpr_ap, lnw_ap, lnb_ap, tag,
               tg0=0, tg1=NTG, col0=0):
    es = ExitStack()
    wfc = [p.sbuf(f"{tag}_wfc{i}", [128, DC, 128], BF16, stack=es) for i in range(2)]
    wpr = [p.sbuf(f"{tag}_wpr{i}", [128, DC, 128], BF16, stack=es) for i in range(2)]
    hsb = [p.sbuf(f"{tag}_h{i}", [128, 512], F32, stack=es) for i in range(HC_FFN)]
    hbf = [p.sbuf(f"{tag}_hb{i}", [128, 512], BF16, stack=es) for i in range(2)]
    lnw = p.sbuf(f"{tag}_lnw", [128, HC_FFN], F32, stack=es)
    lnb = p.sbuf(f"{tag}_lnb", [128, HC_FFN], F32, stack=es)
    sq = p.sbuf(f"{tag}_sq", [128, 512], BF16, stack=es)
    hcast = p.sbuf(f"{tag}_hcast", [128, 512], BF16, stack=es)
    tmp = p.sbuf(f"{tag}_tmp", [128, 512], F32, stack=es)
    rowm = p.sbuf(f"{tag}_rm", [1, 512], F32, stack=es)
    rowv = p.sbuf(f"{tag}_rv", [1, 512], F32, stack=es)
    rowr = p.sbuf(f"{tag}_rr", [1, 512], F32, stack=es)
    rowt = p.sbuf(f"{tag}_rt", [1, 512], F32, stack=es)
    brc = p.sbuf(f"{tag}_brc", [128, 512], F32, stack=es)
    bmc = p.sbuf(f"{tag}_bmc", [128, 512], F32, stack=es)
    pj = [p.psum(f"{tag}_pj{i}", [128, 512], F32, stack=es) for i in range(8)]
    p.dma("sync", lnw[:, :], lnw_ap[:, :], inc=f"{tag}_lw")
    p.dma("sync", lnb[:, :], lnb_ap[:, :], inc=f"{tag}_lw")
    p.wait("vector", f"{tag}_lw")

    fcmm_marks = {}
    ge_marks = {}
    for g in range(tg0, tg1):
        sl = slice(g * 512 - col0, (g + 1) * 512 - col0)
        p.barrier()
        fcw_marks = [0, 0]
        ge_marks = [0, 0]
        for hc in range(HC_FFN):
            wb = wfc[hc % 2]
            p.wait("sync", f"{tag}_fcmm")
            wlm = p.dma("sync", wb[:, :, :].rearrange("p kc m -> p kc m"),
                  qfc_ap[hc, :, :, :].rearrange("kc p m -> p kc m"), inc=f"{tag}_wl")
            p.wait("tensor", f"{tag}_wl", wlm)
            p.wait("tensor", f"{tag}_ge", ge_marks[hc % 2])
            pb = pj[hc % 2]
            for kc in range(DC):
                fcw_marks[hc % 2] = p.do("tensor",
                     lambda e, wb=wb, kc=kc, sl=sl, pb=pb:
                     e.matmul(pb[:, :], wb[:, kc, :], lnx[kc][:, sl],
                              start=(kc == 0), stop=(kc == DC - 1)),
                     inc=f"{tag}_fcmm")
            p.wait("scalar", f"{tag}_fcmm", fcw_marks[hc % 2])
            ge_marks[hc % 2] = p.do("scalar", lambda e, hc=hc, pb=pb:
                 e.activation(hsb[hc][:, :], pb[:, :], AF.Gelu), inc=f"{tag}_ge")
            p.wait("vector", f"{tag}_ge")
            p.wait("vector", f"{tag}_st", p.cnt.get(f"{tag}_st", 0))
            p.do("vector", lambda e, hc=hc:
                 e.tensor_tensor(sq[:, :], hsb[hc][:, :], hsb[hc][:, :], ALU.mult),
                 inc=f"{tag}_sqd")
            p.do("vector", lambda e, hc=hc:
                 e.tensor_copy(hcast[:, :], hsb[hc][:, :]), inc=f"{tag}_sqd")
            p.wait("tensor", f"{tag}_sqd")
            p.do("tensor", lambda e, hc=hc:
                 e.matmul(pj[2][0:1, :], c.ones_bf[:, 0:1], hcast[:, :],
                          start=(hc == 0), stop=(hc == HC_FFN - 1)), inc=f"{tag}_st")
            p.do("tensor", lambda e, hc=hc:
                 e.matmul(pj[3][0:1, :], c.ones_bf[:, 0:1], sq[:, :],
                          start=(hc == 0), stop=(hc == HC_FFN - 1)), inc=f"{tag}_st")
        p.barrier()
        p.wait("vector", f"{tag}_st")
        p.do("vector", lambda e: e.tensor_scalar(rowm[:, :], pj[2][0:1, :],
                                                 1.0 / H_FFN, None, op0=ALU.mult),
             inc=f"{tag}_r")
        p.do("vector", lambda e: e.tensor_scalar(rowv[:, :], pj[3][0:1, :],
                                                 1.0 / H_FFN, None, op0=ALU.mult),
             inc=f"{tag}_r")
        p.wait("vector", f"{tag}_r")
        p.do("vector", lambda e: e.tensor_tensor(rowt[:, :], rowm[:, :], rowm[:, :],
                                                 ALU.mult), inc=f"{tag}_r")
        p.wait("vector", f"{tag}_r")
        p.do("vector", lambda e: e.tensor_tensor(rowv[:, :], rowv[:, :], rowt[:, :],
                                                 ALU.subtract), inc=f"{tag}_r")
        p.wait("scalar", f"{tag}_r")
        p.do("scalar", lambda e: e.activation(rowt[:, :], rowv[:, :], AF.Ln,
                                              bias=c.col(LN_EPS, 1)), inc=f"{tag}_a")
        p.wait("scalar", f"{tag}_a")
        p.do("scalar", lambda e: e.activation(rowr[:, :], rowt[:, :], AF.Exp,
                                              scale=-0.5), inc=f"{tag}_a")
        p.wait("vector", f"{tag}_a")
        p.do("vector", lambda e: e.tensor_tensor(rowm[:, :], rowm[:, :], rowr[:, :],
                                                 ALU.mult), inc=f"{tag}_r")
        p.wait("tensor", f"{tag}_r")
        p.do("tensor", lambda e: e.matmul(pj[2][:, :], c.ones_f[0:1, :], rowr[:, :],
                                          start=True, stop=True), inc=f"{tag}_bcm")
        p.wait("vector", f"{tag}_bcm")
        p.do("vector", lambda e: e.tensor_copy(brc[:, :], pj[2][:, :]), inc=f"{tag}_r")
        p.wait("tensor", f"{tag}_r")
        p.do("tensor", lambda e: e.matmul(pj[2][:, :], c.ones_f[0:1, :], rowm[:, :],
                                          start=True, stop=True), inc=f"{tag}_bcm")
        p.wait("vector", f"{tag}_bcm")
        p.do("vector", lambda e: e.tensor_copy(bmc[:, :], pj[2][:, :]), inc=f"{tag}_r")
        p.wait("vector", f"{tag}_r")
        for hc in range(HC_FFN):
            p.do("vector", lambda e, hc=hc:
                 e.tensor_tensor(tmp[:, :], hsb[hc][:, :], brc[:, :], ALU.mult),
                 inc=f"{tag}_ap")
            p.wait("vector", f"{tag}_ap")
            p.do("vector", lambda e, hc=hc:
                 e.tensor_tensor(tmp[:, :], tmp[:, :], bmc[:, :], ALU.subtract),
                 inc=f"{tag}_ap")
            p.wait("vector", f"{tag}_ap")
            p.do("vector", lambda e, hc=hc:
                 e.tensor_scalar(hsb[hc][:, :], tmp[:, :], lnw[:, hc:hc + 1],
                                 lnb[:, hc:hc + 1], op0=ALU.mult, op1=ALU.add),
                 inc=f"{tag}_ap")
            p.wait("vector", f"{tag}_ap")
            p.wait("scalar", f"{tag}_ap")
        p.barrier()
        p.wait("tensor", f"{tag}_ap")
        hb_marks = [0, 0]
        for hc in range(HC_FFN):
            wb = wpr[hc % 2]
            p.wait("sync", f"{tag}_pjmm")
            wpm = p.dma("sync", wb[:, :, :],
                  qpr_ap[:, hc, :, :].rearrange("dc p m -> p dc m"), inc=f"{tag}_wp")
            hb = hbf[hc % 2]
            p.wait("vector", f"{tag}_pjmm", hb_marks[hc % 2])
            hbm = p.do("vector", lambda e, hb=hb, hc=hc:
                 e.tensor_copy(hb[:, :], hsb[hc][:, :]), inc=f"{tag}_hbc")
            p.wait("tensor", f"{tag}_wp", wpm)
            p.wait("tensor", f"{tag}_hbc", hbm)
            for dc in range(DC):
                hb_marks[hc % 2] = p.do("tensor", lambda e, wb=wb, dc=dc, hc=hc, hb=hb:
                     e.matmul(pj[dc][:, :], wb[:, dc, :], hb[:, :],
                              start=(hc == 0), stop=(hc == HC_FFN - 1)),
                     inc=f"{tag}_pjmm")
        p.wait("vector", f"{tag}_pjmm")
        for dc in range(DC):
            p.do("vector", lambda e, dc=dc, sl=sl:
                 e.scalar_tensor_tensor(st.resid[dc][:, sl], pj[dc][:, :], 0.5,
                                        st.resid[dc][:, sl], op0=ALU.mult, op1=ALU.add),
                 inc=f"{tag}_ev")
        p.wait("tensor", f"{tag}_ev")
    p.barrier()
    es.close()




def emit_l0ffn_route(p: Prog, c: Consts, lnx, lgall, aps, tag,
                     srcst, wst):
    CAPF = float(CAP_GLOBAL)
    CLF = float(CL)
    es = ExitStack()
    ohcat = p.sbuf(f"{tag}_oh", [128, NTT * 32], F32, stack=es)
    Ast = p.sbuf(f"{tag}_A", [128, NTT * 16], F32, stack=es)
    i12f = p.sbuf(f"{tag}_i12", [128, NTT * 2], F32, stack=es)
    keepst = p.sbuf(f"{tag}_kp", [128, NTT * 2], F32, stack=es)
    destst = p.sbuf(f"{tag}_dst", [128, NTT * 2], U32, stack=es)
    ix8 = p.sbuf(f"{tag}_ix8", [128, 8], U32, stack=es)
    junk = p.sbuf(f"{tag}_junk", [128, 16], F32, stack=es)
    scr = p.sbuf(f"{tag}_scr", [128, 16], F32, stack=es)
    col = p.sbuf(f"{tag}_col", [128, 12], F32, stack=es)
    sm32 = p.sbuf(f"{tag}_s32", [32, 32], F32, stack=es)
    sm32b = p.sbuf(f"{tag}_s32b", [32, 32], F32, stack=es)
    sm32c = p.sbuf(f"{tag}_s32c", [32, 32], F32, stack=es)
    trsc = p.sbuf(f"{tag}_tr", [32, 32], F32, stack=es)
    trsc2 = p.sbuf(f"{tag}_tr2", [32, 16], F32, stack=es)
    corow = p.sbuf(f"{tag}_cor", [1, 32], F32, stack=es)
    ccsb = p.sbuf(f"{tag}_ccs", [8, 32], F32, stack=es)
    cmsk = p.sbuf(f"{tag}_cm", [8, 1], F32, stack=es)
    eselb = [p.sbuf(f"{tag}_es{i}", [128, 32], F32, stack=es) for i in range(2)]
    eib = [p.sbuf(f"{tag}_ei{i}", [32, 128], F32, stack=es) for i in range(2)]
    p.dma("sync", cmsk[:, :], aps["cmask"][:, :], inc=f"{tag}_cml")
    rbc = p.nc.gpsimd.alloc_register(name=f"{tag}_rbc")
    p.do("gpsimd", lambda e: e.reg_mov(rbc, NSLOT - 1))

    for i in range(NTT):
        lg = lgall[:, i * 16:(i + 1) * 16]
        mx8 = scr[:, 0:8]
        p.wait("vector", f"{tag}_p1s")
        p.do("vector", lambda e, lg=lg, mx8=mx8: e.max(out=mx8, in_=lg),
             inc=f"{tag}_p1v")
        p.wait("vector", f"{tag}_p1v")
        p.do("vector", lambda e, lg=lg, mx8=mx8:
             e.max_index(out=ix8[:, :], in_max=mx8, in_values=lg), inc=f"{tag}_p1v")
        p.wait("vector", f"{tag}_p1v")
        p.do("vector", lambda e, i=i:
             e.tensor_copy(i12f[:, i * 2:i * 2 + 2], ix8[:, 0:2]), inc=f"{tag}_p1v")
        p.do("vector", lambda e: e.tensor_scalar(col[:, 0:1], scr[:, 0:1], -1.0,
                                                 None, op0=ALU.mult), inc=f"{tag}_p1v")
        p.wait("scalar", f"{tag}_p1v")
        p.do("scalar", lambda e, lg=lg:
             e.activation(junk[:, :], lg, AF.Exp, bias=col[:, 0:1],
                          accum_out=col[:, 1:2]), inc=f"{tag}_p1s")
        p.wait("vector", f"{tag}_p1s")
        p.do("vector", lambda e: e.reciprocal(col[:, 2:3], col[:, 1:2]),
             inc=f"{tag}_p1v")
        p.do("vector", lambda e, mx8=mx8:
             e.tensor_tensor(col[:, 3:4], mx8[:, 1:2], mx8[:, 0:1], ALU.subtract),
             inc=f"{tag}_p1v")
        p.wait("scalar", f"{tag}_p1v")
        p.do("scalar", lambda e: e.activation(col[:, 4:5], col[:, 3:4], AF.Exp),
             inc=f"{tag}_p1s")
        p.wait("vector", f"{tag}_p1s")
        p.do("vector", lambda e: e.tensor_tensor(col[:, 5:6], col[:, 4:5],
                                                 col[:, 2:3], ALU.mult),
             inc=f"{tag}_p1v")
        p.wait("vector", f"{tag}_p1v")
        p.do("vector", lambda e, i=i: e.tensor_scalar(wst[:, i * 2:i * 2 + 1],
             col[:, 2:3], 0.5, None, op0=ALU.mult), inc=f"{tag}_p1v")
        p.do("vector", lambda e, i=i: e.tensor_scalar(wst[:, i * 2 + 1:i * 2 + 2],
             col[:, 5:6], 0.5, None, op0=ALU.mult), inc=f"{tag}_p1v")
        p.do("vector", lambda e, i=i: e.tensor_scalar(ohcat[:, i * 32:i * 32 + 16],
             c.iota_row16[:, :], i12f[:, i * 2:i * 2 + 1], None, op0=ALU.is_equal),
             inc=f"{tag}_p1v")
        p.do("vector", lambda e, i=i: e.tensor_scalar(ohcat[:, i * 32 + 16:i * 32 + 32],
             c.iota_row16[:, :], i12f[:, i * 2 + 1:i * 2 + 2], None, op0=ALU.is_equal),
             inc=f"{tag}_p1v")
        p.wait("vector", f"{tag}_p1v")
    p.barrier()

    ps_t = p.psum(f"{tag}_pt", [32, 32], F32, stack=es)
    esel_marks = [0, 0]
    for i in range(NTT):
        eb = eselb[i % 2]
        p.wait("vector", f"{tag}_p2m", esel_marks[i % 2])
        p.do("vector", lambda e, i=i, eb=eb:
             e.tensor_scalar(eb[:, :], c.iota_row32[:, :], float(i), None,
                             op0=ALU.is_equal), inc=f"{tag}_p2v")
        p.wait("tensor", f"{tag}_p2v")
        esel_marks[i % 2] = p.do("tensor", lambda e, i=i, eb=eb:
             e.matmul(ps_t[:, :], eb[:, :], ohcat[:, i * 32:(i + 1) * 32],
                      start=(i == 0), stop=(i == NTT - 1)), inc=f"{tag}_p2m")
    p.wait("vector", f"{tag}_p2m")
    p.do("vector", lambda e: e.tensor_copy(sm32[:, :], ps_t[:, :]), inc=f"{tag}_p2v")
    p.wait("vector", f"{tag}_p2v")
    p.do("vector", lambda e: e.transpose(sm32b[:, :], sm32[:, :]), inc=f"{tag}_p2v")
    p.wait("vector", f"{tag}_p2v")
    p.do("vector", lambda e: e.tensor_tensor_scan(sm32c[:, :], c.ones_f[0:32, 0:32],
         sm32b[:, :], 0.0, ALU.mult, ALU.add), inc=f"{tag}_p2v")
    p.wait("vector", f"{tag}_p2v")
    p.do("vector", lambda e: e.tensor_tensor(sm32b[:, :], sm32c[:, :], sm32b[:, :],
                                             ALU.subtract), inc=f"{tag}_p2v")
    p.wait("vector", f"{tag}_p2v")
    p.do("vector", lambda e: e.transpose(trsc[:, :], sm32b[:, :]), inc=f"{tag}_p2v")
    p.wait("vector", f"{tag}_p2v")
    p.do("vector", lambda e: e.transpose(sm32[:, :], sm32c[:, :]), inc=f"{tag}_p2v")
    p.wait("gpsimd", f"{tag}_p2v")
    p.dma("gpsimd", aps["cc_in"][:, :], sm32[31:32, :], inc=f"{tag}_cc")
    p.wait("gpsimd", f"{tag}_cc")
    p.do("gpsimd", lambda e: e.collective_compute(
        "AllGather", ALU.bypass, replica_groups=[list(range(NCORES))],
        ins=[aps["cc_in"][:, :].opt()], outs=[aps["cc_out"][:, :].opt()]),
        inc=f"{tag}_cc2")
    p.wait("gpsimd", f"{tag}_cc2", 1)
    p.dma("gpsimd", ccsb[:, :], aps["cc_out"][:, :], inc=f"{tag}_cc")
    p.wait("tensor", f"{tag}_cc")
    p.wait("tensor", f"{tag}_cml")
    p.do("tensor", lambda e: e.matmul(ps_t[0:1, :], cmsk[:, 0:1],
                                      ccsb[:, :], start=True, stop=True),
         inc=f"{tag}_p2m")
    p.wait("vector", f"{tag}_p2m")
    p.do("vector", lambda e: e.tensor_copy(corow[:, :], ps_t[0:1, :]), inc=f"{tag}_p2v")
    p.wait("vector", f"{tag}_p2v")
    p.barrier()

    ps_r = [p.psum(f"{tag}_pr{i}", [128, 32], F32, stack=es) for i in range(2)]
    rk_marks = [0, 0]
    ei_marks = [0, 0]
    for i in range(NTT):
        pb = ps_r[i % 2]
        eb = eib[i % 2]
        p.wait("vector", f"{tag}_p3m", ei_marks[i % 2])
        p.do("vector", lambda e, i=i, eb=eb:
             e.tensor_scalar(eb[:, :], c.iota_col[0:32, 0:1].to_broadcast([32, 128]),
                             float(i), None, op0=ALU.is_equal), inc=f"{tag}_p3v")
        p.wait("tensor", f"{tag}_p3v")
        p.wait("tensor", f"{tag}_p3r", rk_marks[i % 2])
        p.do("tensor", lambda e, i=i, pb=pb:
             e.matmul(pb[:, :], c.triu_excl_f[:, :], ohcat[:, i * 32:(i + 1) * 32],
                      start=True, stop=False), inc=f"{tag}_p3m")
        ei_marks[i % 2] = p.do("tensor", lambda e, pb=pb, eb=eb:
             e.matmul(pb[:, :], eb[:, :], trsc[:, :], start=False, stop=False),
             inc=f"{tag}_p3m")
        mm = p.do("tensor", lambda e, pb=pb:
             e.matmul(pb[:, :], c.ones_f[0:1, :], corow[:, :], start=False,
                      stop=True), inc=f"{tag}_p3m")
        p.wait("vector", f"{tag}_p3m", mm)
        for k in range(2):
            p.do("vector", lambda e, i=i, k=k, pb=pb:
                 e.tensor_tensor(scr[:, :], pb[:, k * 16:(k + 1) * 16],
                                 ohcat[:, i * 32 + k * 16:i * 32 + (k + 1) * 16],
                                 ALU.mult), inc=f"{tag}_p3v")
            p.wait("vector", f"{tag}_p3v")
            p.do("vector", lambda e:
                 e.tensor_reduce(col[:, 6:7], scr[:, :], AX.X, ALU.add),
                 inc=f"{tag}_p3v")
            p.wait("vector", f"{tag}_p3v")
            p.do("vector", lambda e, i=i, k=k:
                 e.tensor_scalar(keepst[:, i * 2 + k:i * 2 + k + 1], col[:, 6:7],
                                 CAPF, None, op0=ALU.is_lt), inc=f"{tag}_p3v")
        p.wait("vector", f"{tag}_p3v")
        p.do("vector", lambda e, i=i:
             e.tensor_scalar(Ast[:, i * 16:(i + 1) * 16],
                             ohcat[:, i * 32:i * 32 + 16],
                             keepst[:, i * 2:i * 2 + 1], None, op0=ALU.mult),
             inc=f"{tag}_p3r")
        p.wait("vector", f"{tag}_p3r")
        rk_marks[i % 2] = p.do("vector", lambda e, i=i:
             e.scalar_tensor_tensor(Ast[:, i * 16:(i + 1) * 16],
                                    ohcat[:, i * 32 + 16:i * 32 + 32],
                                    keepst[:, i * 2 + 1:i * 2 + 2],
                                    Ast[:, i * 16:(i + 1) * 16],
                                    op0=ALU.mult, op1=ALU.add), inc=f"{tag}_p3r")
    p.barrier()

    esel_marks = [0, 0]
    for i in range(NTT):
        eb = eselb[i % 2]
        p.wait("vector", f"{tag}_p4m", esel_marks[i % 2])
        p.do("vector", lambda e, i=i, eb=eb:
             e.tensor_scalar(eb[:, :], c.iota_row32[:, :], float(i), None,
                             op0=ALU.is_equal), inc=f"{tag}_p4v")
        p.wait("tensor", f"{tag}_p4v")
        esel_marks[i % 2] = p.do("tensor", lambda e, i=i, eb=eb:
             e.matmul(ps_t[:, 0:16], eb[:, :], Ast[:, i * 16:(i + 1) * 16],
                      start=(i == 0), stop=(i == NTT - 1)), inc=f"{tag}_p4m")
    p.wait("vector", f"{tag}_p4m")
    p.do("vector", lambda e: e.tensor_copy(sm32[:, 0:16], ps_t[:, 0:16]),
         inc=f"{tag}_p4v")
    p.wait("vector", f"{tag}_p4v")
    p.do("vector", lambda e: e.transpose(sm32b[:, :], sm32[:, :]), inc=f"{tag}_p4v")
    p.wait("vector", f"{tag}_p4v")
    p.do("vector", lambda e: e.tensor_tensor_scan(sm32c[0:16, :], c.ones_f[0:16, 0:32],
         sm32b[0:16, :], 0.0, ALU.mult, ALU.add), inc=f"{tag}_p4v")
    p.wait("vector", f"{tag}_p4v")
    p.do("vector", lambda e: e.tensor_tensor(sm32b[0:16, :], sm32c[0:16, :],
                                             sm32b[0:16, :], ALU.subtract),
         inc=f"{tag}_p4v")
    p.wait("vector", f"{tag}_p4v")
    p.do("vector", lambda e: e.transpose(sm32[:, :], sm32b[:, :]), inc=f"{tag}_p4v")
    p.wait("vector", f"{tag}_p4v")
    p.do("vector", lambda e: e.tensor_copy(trsc2[:, :], sm32[:, 0:16]),
         inc=f"{tag}_p4v")
    p.wait("vector", f"{tag}_p4v")
    p.barrier()

    ps_a = [p.psum(f"{tag}_pa{i}", [128, 16], F32, stack=es) for i in range(2)]
    ps_x = [p.psum(f"{tag}_px{i}", [128, 128], BF16, stack=es) for i in range(2)]
    xtm = [p.sbuf(f"{tag}_xtm{i}", [128, D], BF16, stack=es) for i in range(2)]
    sc_marks = [0, 0]
    a_marks = [0, 0]
    ei_marks = [0, 0]
    for i in range(NTT):
        pb = ps_a[i % 2]
        eb = eib[i % 2]
        p.wait("vector", f"{tag}_p5m", ei_marks[i % 2])
        p.do("vector", lambda e, i=i, eb=eb:
             e.tensor_scalar(eb[:, :], c.iota_col[0:32, 0:1].to_broadcast([32, 128]),
                             float(i), None, op0=ALU.is_equal), inc=f"{tag}_p5v")
        p.wait("tensor", f"{tag}_p5v")
        p.wait("tensor", f"{tag}_p5r", a_marks[i % 2])
        p.do("tensor", lambda e, i=i, pb=pb:
             e.matmul(pb[:, :], c.triu_excl_f[:, :], Ast[:, i * 16:(i + 1) * 16],
                      start=True, stop=False), inc=f"{tag}_p5m")
        ei_marks[i % 2] = p.do("tensor", lambda e, pb=pb, eb=eb:
             e.matmul(pb[:, :], eb[:, :], trsc2[:, :], start=False, stop=True),
             inc=f"{tag}_p5m")
        p.wait("vector", f"{tag}_p5m", ei_marks[i % 2])
        for k in range(2):
            p.do("vector", lambda e, i=i, k=k, pb=pb:
                 e.tensor_tensor(scr[:, :], pb[:, :],
                                 ohcat[:, i * 32 + k * 16:i * 32 + (k + 1) * 16],
                                 ALU.mult), inc=f"{tag}_p5v")
            p.wait("vector", f"{tag}_p5v")
            a_marks[i % 2] = p.do("vector", lambda e:
                 e.tensor_reduce(col[:, 7:8], scr[:, :], AX.X, ALU.add),
                 inc=f"{tag}_p5r")
            p.wait("vector", f"{tag}_p5r")
            p.do("vector", lambda e, i=i, k=k:
                 e.scalar_tensor_tensor(col[:, 8:9], i12f[:, i * 2 + k:i * 2 + k + 1],
                                        CLF, col[:, 7:8], op0=ALU.mult, op1=ALU.add),
                 inc=f"{tag}_p5v")
            p.do("vector", lambda e:
                 e.tensor_scalar(col[:, 9:10], col[:, 7:8], CLF, None,
                                 op0=ALU.is_lt), inc=f"{tag}_p5v")
            p.wait("vector", f"{tag}_p5v")
            p.do("vector", lambda e, i=i, k=k:
                 e.tensor_tensor(col[:, 10:11], col[:, 9:10],
                                 keepst[:, i * 2 + k:i * 2 + k + 1], ALU.mult),
                 inc=f"{tag}_p5v")
            p.wait("vector", f"{tag}_p5v")
            p.do("vector", lambda e:
                 e.tensor_tensor(col[:, 11:12], col[:, 8:9], col[:, 10:11],
                                 ALU.mult), inc=f"{tag}_p5v")
            BIGF = 2147483648.0
            p.do("vector", lambda e, BIGF=BIGF:
                 e.tensor_scalar(col[:, 6:7], col[:, 10:11], -BIGF, BIGF,
                                 op0=ALU.mult, op1=ALU.add), inc=f"{tag}_p5v")
            p.do("vector", lambda e:
                 e.tensor_scalar(col[:, 5:6], col[:, 10:11], -float(NSLOT),
                                 float(NSLOT), op0=ALU.mult, op1=ALU.add),
                 inc=f"{tag}_p5v")
            p.wait("vector", f"{tag}_p5v")
            p.do("vector", lambda e:
                 e.tensor_tensor(col[:, 4:5], col[:, 11:12], col[:, 6:7], ALU.add),
                 inc=f"{tag}_p5v")
            p.do("vector", lambda e:
                 e.tensor_tensor(col[:, 3:4], col[:, 11:12], col[:, 5:6], ALU.add),
                 inc=f"{tag}_p5v")
            p.wait("vector", f"{tag}_p5v")
            p.do("vector", lambda e, i=i, k=k:
                 e.tensor_copy(destst[:, i * 2 + k:i * 2 + k + 1], col[:, 4:5]),
                 inc=f"{tag}_p5v")
            p.do("vector", lambda e, i=i, k=k:
                 e.tensor_copy(srcst[:, i * 2 + k:i * 2 + k + 1], col[:, 3:4]),
                 inc=f"{tag}_p5v")
        buf = xtm[i % 2]
        p.wait("vector", f"{tag}_sc", sc_marks[i % 2])
        for dc in range(DC):
            pxb = ps_x[dc % 2]
            p.wait("tensor", f"{tag}_p5e")
            p.do("tensor", lambda e, i=i, dc=dc, pxb=pxb:
                 e.transpose(pxb[:, :], lnx[dc][:, i * 128:(i + 1) * 128],
                             c.ident_bf[:, :]), inc=f"{tag}_p5t")
            p.wait("vector", f"{tag}_p5t")
            p.do("vector", lambda e, dc=dc, pxb=pxb, buf=buf:
                 e.tensor_copy(buf[:, dc * 128:(dc + 1) * 128], pxb[:, :]),
                 inc=f"{tag}_p5e")
        p.wait("gpsimd", f"{tag}_p5e")
        p.wait("gpsimd", f"{tag}_p5v")
        p.wait("gpsimd", f"{tag}_sc", sc_marks[i % 2])
        for k in range(2):
            sc_marks[i % 2] = p.do("gpsimd", lambda e, i=i, k=k, buf=buf:
                 e.indirect_dma_start(
                     out=aps["xe"][:, :],
                     out_offset=IndirectOffsetOnAxis(
                         ap=destst[:, i * 2 + k:i * 2 + k + 1], axis=0),
                     in_=buf[:, :], in_offset=None,
                     bounds_check=rbc, oob_is_err=False),
                 inc=f"{tag}_sc", n=16)
    p.wait("gpsimd", f"{tag}_sc")
    p.barrier()
    es.close()


def emit_l0ffn_exec(p: Prog, c: Consts, aps, x_ap, y_ap, tag, srcst, wst):
    es2 = ExitStack()
    xeT = [[p.sbuf(f"{tag}_xeT{j}_{dc}", [128, CL], BF16, stack=es2)
            for dc in range(DC)] for j in range(2)]
    wg = [p.sbuf(f"{tag}_wg{j}", [128, DC, DC, 128], BF16, stack=es2) for j in range(2)]
    wu = [p.sbuf(f"{tag}_wu{j}", [128, DC, DC, 128], BF16, stack=es2) for j in range(2)]
    wd = [p.sbuf(f"{tag}_wd{j}", [128, DC, 1024], BF16, stack=es2) for j in range(2)]
    hT = [p.sbuf(f"{tag}_hT{i}", [128, 512], BF16, stack=es2) for i in range(DC)]
    sil = [p.sbuf(f"{tag}_sil{i}", [128, 512], F32, stack=es2) for i in range(2)]
    oub = [p.sbuf(f"{tag}_ou{i}", [128, D], BF16, stack=es2) for i in range(2)]
    zr = p.sbuf(f"{tag}_zr", [1, D], BF16, stack=es2)
    ps_g = [p.psum(f"{tag}_pg{i}", [128, 512], F32, stack=es2) for i in range(2)]
    ps_u = [p.psum(f"{tag}_pu{i}", [128, 512], F32, stack=es2) for i in range(2)]
    ps_d = [p.psum(f"{tag}_pd{i}", [128, 512], F32, stack=es2) for i in range(4)]
    p.do("vector", lambda e: e.memset(zr[:, :], 0.0), inc=f"{tag}_ez")
    p.wait("gpsimd", f"{tag}_ez")
    p.dma("gpsimd", aps["outsc"][NSLOT:NSLOT + 1, :], zr[:, :], inc=f"{tag}_ez2")
    ou_marks = [0, 0]
    mm_done = [0, 0]
    ld_marks = [0, 0]
    wl_marks = [0, 0]
    gu_marks = [[0, 0], [0, 0]]
    dn_marks = [0, 0, 0, 0]

    def load_expert(ex):
        j = ex % 2
        p.wait("sync", f"{tag}_emm", mm_done[j])
        lm = 0
        for dc in range(DC):
            lm = p.do("sync", lambda e, ex=ex, dc=dc, j=j:
                 e.dma_start_transpose(
                     out=xeT[j][dc][:, :],
                     in_=aps["xe"][ex * CL:(ex + 1) * CL, dc * 128:(dc + 1) * 128]),
                 inc=f"{tag}_el", n=16)
        p.wait("gpsimd", f"{tag}_emm", mm_done[j])
        p.dma("gpsimd", wg[j][:, :, :, :], aps["g5"][ex], inc=f"{tag}_ew")
        p.dma("gpsimd", wu[j][:, :, :, :], aps["u5"][ex], inc=f"{tag}_ew")
        wm = p.dma("gpsimd", wd[j][:, :, :], aps["d5"][ex], inc=f"{tag}_ew")
        ld_marks[j] = lm
        wl_marks[j] = wm

    load_expert(0)
    for ex in range(E_MOE):
        j = ex % 2
        if ex + 1 < E_MOE:
            load_expert(ex + 1)
        p.wait("tensor", f"{tag}_el", ld_marks[j])
        p.wait("tensor", f"{tag}_ew", wl_marks[j])
        for sg in range(CL // 512):
            for hc in range(DC):
                hb = hc % 2
                pg, pu = ps_g[hb], ps_u[hb]
                p.wait("tensor", f"{tag}_ev1", gu_marks[hb][0])
                for kc in range(DC):
                    p.do("tensor", lambda e, kc=kc, hc=hc, sg=sg, j=j, pg=pg:
                         e.matmul(pg[:, :], wg[j][:, kc, hc, :],
                                  xeT[j][kc][:, sg * 512:(sg + 1) * 512],
                                  start=(kc == 0), stop=(kc == DC - 1)),
                         inc=f"{tag}_emm")
                for kc in range(DC):
                    mmk = p.do("tensor", lambda e, kc=kc, hc=hc, sg=sg, j=j, pu=pu:
                         e.matmul(pu[:, :], wu[j][:, kc, hc, :],
                                  xeT[j][kc][:, sg * 512:(sg + 1) * 512],
                                  start=(kc == 0), stop=(kc == DC - 1)),
                         inc=f"{tag}_emm")
                p.wait("scalar", f"{tag}_emm", mmk)
                p.wait("scalar", f"{tag}_ev1", gu_marks[hb][1])
                p.do("scalar", lambda e, hb=hb, pg=pg:
                     e.activation(sil[hb][:, :], pg[:, :], AF.Silu), inc=f"{tag}_ea")
                p.wait("vector", f"{tag}_ea")
                m1 = p.do("vector", lambda e, hc=hc, hb=hb, pu=pu:
                     e.tensor_tensor(hT[hc][:, :], sil[hb][:, :], pu[:, :], ALU.mult),
                     inc=f"{tag}_ev1")
                gu_marks[hb] = [m1, m1]
            p.wait("tensor", f"{tag}_ev1")
            for st4 in range(4):
                ob = oub[st4 % 2]
                p.wait("vector", f"{tag}_eo", ou_marks[st4 % 2])
                for dg in range(2):
                    bi = (st4 * 2 + dg) % 4
                    pd = ps_d[bi]
                    p.wait("tensor", f"{tag}_ev2", dn_marks[bi])
                    for hc in range(DC):
                        mmk = p.do("tensor", lambda e, hc=hc, st4=st4, dg=dg, j=j, pd=pd:
                             e.matmul(pd[:, :], hT[hc][:, st4 * 128:(st4 + 1) * 128],
                                      wd[j][:, hc, dg * 512:(dg + 1) * 512],
                                      start=(hc == 0), stop=(hc == DC - 1)),
                             inc=f"{tag}_emm")
                    p.wait("vector", f"{tag}_emm", mmk)
                    dn_marks[bi] = p.do("vector", lambda e, dg=dg, pd=pd, ob=ob:
                         e.tensor_copy(ob[:, dg * 512:(dg + 1) * 512], pd[:, :]),
                         inc=f"{tag}_ev2")
                p.wait("gpsimd", f"{tag}_ev2", dn_marks[(st4 * 2 + 1) % 4])
                row0 = ex * CL + sg * 512 + st4 * 128
                ou_marks[st4 % 2] = p.dma("gpsimd",
                     aps["outsc"][row0:row0 + 128, :], ob[:, :], inc=f"{tag}_eo")
        mm_done[j] = p.mark(f"{tag}_emm")
    p.wait("gpsimd", f"{tag}_eo")
    p.wait("gpsimd", f"{tag}_ez2")
    p.barrier()
    es2.close()

    es3 = ExitStack()
    gth = [p.sbuf(f"{tag}_g{i}", [128, D], BF16, stack=es3) for i in range(2)]
    cmb = p.sbuf(f"{tag}_cmb", [128, D], F32, stack=es3)
    xin = [p.sbuf(f"{tag}_xi{i}", [128, D], F32, stack=es3) for i in range(2)]
    ybf = [p.sbuf(f"{tag}_yb{i}", [128, D], F32, stack=es3) for i in range(2)]
    rbc2 = p.nc.gpsimd.alloc_register(name=f"{tag}_rbc2")
    p.do("gpsimd", lambda e: e.reg_mov(rbc2, NSLOT))
    y_marks = [0, 0]
    x_marks = [0, 0]
    for i in range(NTT):
        p.wait("gpsimd", f"{tag}_cv")
        gm = 0
        for k in range(2):
            gm = p.do("gpsimd", lambda e, i=i, k=k:
                 e.indirect_dma_start(
                     out=gth[k][:, :], out_offset=None,
                     in_=aps["outsc"][:, :],
                     in_offset=IndirectOffsetOnAxis(
                         ap=srcst[:, i * 2 + k:i * 2 + k + 1], axis=0),
                     bounds_check=rbc2, oob_is_err=False),
                 inc=f"{tag}_cg", n=16)
        xb = xin[i % 2]
        p.wait("sync", f"{tag}_cv", x_marks[i % 2])
        xm = p.dma("sync", xb[:, :], x_ap[i * 128:(i + 1) * 128, :], inc=f"{tag}_xl")
        p.wait("vector", f"{tag}_cg", gm)
        p.do("vector", lambda e, i=i:
             e.tensor_scalar(cmb[:, :], gth[0][:, :], wst[:, i * 2:i * 2 + 1],
                             None, op0=ALU.mult), inc=f"{tag}_cv")
        p.wait("vector", f"{tag}_cv")
        p.do("vector", lambda e, i=i:
             e.scalar_tensor_tensor(cmb[:, :], gth[1][:, :],
                                    wst[:, i * 2 + 1:i * 2 + 2], cmb[:, :],
                                    op0=ALU.mult, op1=ALU.add), inc=f"{tag}_cv")
        yb = ybf[i % 2]
        p.wait("vector", f"{tag}_yo", y_marks[i % 2])
        p.wait("vector", f"{tag}_xl", xm)
        p.wait("vector", f"{tag}_cv")
        p.do("vector", lambda e, xb=xb, yb=yb:
             e.tensor_tensor(yb[:, :], xb[:, :], cmb[:, :], ALU.add),
             inc=f"{tag}_cv")
        x_marks[i % 2] = p.mark(f"{tag}_cv")
        p.wait("gpsimd", f"{tag}_cv")
        y_marks[i % 2] = p.dma("gpsimd", y_ap[i * 128:(i + 1) * 128, :], yb[:, :],
                               inc=f"{tag}_yo")
    p.wait("gpsimd", f"{tag}_yo")
    p.barrier()
    es3.close()




def emit_l0attn(p: Prog, c: Consts, waps, x_ap, y_ap, tag, dbg=None):
    es = ExitStack()
    lnx = [p.sbuf(f"{tag}_lnx{dc}", [128, S], BF16, stack=es) for dc in range(DC)]
    r2b = p.sbuf(f"{tag}_r2", [1, S], BF16, stack=es)
    r2bcs = p.sbuf(f"{tag}_r2b", [128, 128], F32, stack=es)
    r2col = p.sbuf(f"{tag}_r2c", [128, 1], F32, stack=es)
    sclc = p.sbuf(f"{tag}_scl", [128, 1], F32, stack=es)
    wq = p.sbuf(f"{tag}_wq", [128, DC, D], BF16, stack=es)
    wk = p.sbuf(f"{tag}_wk", [128, DC, D], BF16, stack=es)
    wv = p.sbuf(f"{tag}_wv", [128, DC, D], BF16, stack=es)
    wio = p.sbuf(f"{tag}_wio", [128, DC, 48], BF16, stack=es)
    wo = p.sbuf(f"{tag}_wo", [128, DC, D], BF16, stack=es)
    lnw = p.sbuf(f"{tag}_lnw", [128, DC], F32, stack=es)
    wtmp = p.sbuf(f"{tag}_wtmp", [128, 512], F32, stack=es)
    qfc = p.sbuf(f"{tag}_qfc", [128, DC, 128], BF16, stack=es)
    kfc = p.sbuf(f"{tag}_kfc", [128, DC, 128], BF16, stack=es)
    ktm = p.sbuf(f"{tag}_ktm", [128, D], BF16, stack=es)
    vtm = p.sbuf(f"{tag}_vtm", [128, D], BF16, stack=es)
    hout = p.sbuf(f"{tag}_hout", [128, DC, 128], BF16, stack=es)
    t48 = p.sbuf(f"{tag}_t48", [128, 48], F32, stack=es)
    g16 = p.sbuf(f"{tag}_g16", [128, 16 * 5], F32, stack=es)
    logi = p.sbuf(f"{tag}_logi", [128, 16], F32, stack=es)
    logf = p.sbuf(f"{tag}_logf", [128, 16], F32, stack=es)
    og = p.sbuf(f"{tag}_og", [128, 16], F32, stack=es)
    Fsb = p.sbuf(f"{tag}_F", [128, 16], F32, stack=es)
    blf = p.sbuf(f"{tag}_blf", [128, 16], F32, stack=es)
    FTsb = p.sbuf(f"{tag}_FT", [16, 128], F32, stack=es)
    EFt = p.sbuf(f"{tag}_EFt", [16, 128], F32, stack=es)
    OTsb = p.sbuf(f"{tag}_OT", [16, 128], F32, stack=es)
    EF2sb = p.sbuf(f"{tag}_EF2", [128, 128], F32, stack=es)
    O2sb = p.sbuf(f"{tag}_O2", [128, 128], F32, stack=es)
    Mexp = p.sbuf(f"{tag}_M", [128, 128], BF16, stack=es)
    SpT = p.sbuf(f"{tag}_SpT", [128, 128], BF16, stack=es)
    qp = p.sbuf(f"{tag}_qp", [64, 128], BF16, stack=es)
    ccol = p.sbuf(f"{tag}_ccol", [128, 1], F32, stack=es)
    kpp = p.sbuf(f"{tag}_kpp", [128, 64], BF16, stack=es)
    wev = p.sbuf(f"{tag}_wev", [128, 128], F32, stack=es)
    Hb = p.sbuf(f"{tag}_H", [64, NH * HD], F32, stack=es)
    Hbf = p.sbuf(f"{tag}_Hbf", [64, NH * HD], BF16, stack=es)
    sel2 = p.sbuf(f"{tag}_sel2", [16, 128], F32, stack=es)
    eh1 = p.sbuf(f"{tag}_eh1", [16, 128], F32, stack=es)
    i16r = p.sbuf(f"{tag}_i16r", [16, 128], F32, stack=es)
    i16c = p.sbuf(f"{tag}_i16c", [16, 1], F32, stack=es)
    hm = p.sbuf(f"{tag}_hm", [16, 128], F32, stack=es)
    PB = [p.psum(f"{tag}_PB{i}", [128, 512], F32, stack=es) for i in range(8)]
    ps_a = PB[0]
    ps_b = [PB[1][:, 0:128], PB[2][:, 0:128]]
    ps_T1 = PB[3][:, 0:128]
    ps_S = PB[4][:, 0:128]
    ps_o = PB[5][:, 0:128]
    ps_h = PB[6][0:64, 0:64]
    ps_F = PB[7][:, 0:128]

    p.dma("sync", lnw[:, :], waps["alnw"], inc=f"{tag}_lnl")
    p.wait("vector", f"{tag}_lnl")
    for name, sb, ncol, fold in (("aq", wq, D, True), ("ak", wk, D, True),
                                 ("av", wv, D, True), ("aio", wio, 48, True),
                                 ("aout", wo, D, False)):
        for kc in range(DC):
            for hf in range(1 if ncol <= 512 else 2):
                w0 = hf * 512
                w1 = min(ncol, w0 + 512)
                p.wait("gpsimd", f"{tag}_wf")
                p.dma("gpsimd", wtmp[:, 0:w1 - w0], waps[name][:, kc, w0:w1],
                      inc=f"{tag}_wl")
                p.wait("vector", f"{tag}_wl")
                if fold:
                    p.do("vector", lambda e, sb=sb, kc=kc, w0=w0, w1=w1:
                         e.tensor_scalar(sb[:, kc, w0:w1], wtmp[:, 0:w1 - w0],
                                         lnw[:, kc:kc + 1], None, op0=ALU.mult),
                         inc=f"{tag}_wf")
                else:
                    p.do("vector", lambda e, sb=sb, kc=kc, w0=w0, w1=w1:
                         e.tensor_copy(sb[:, kc, w0:w1], wtmp[:, 0:w1 - w0]),
                         inc=f"{tag}_wf")
    p.do("gpsimd", lambda e: e.iota(i16r[:, :], pattern=[[1, 128]], base=0,
                                    channel_multiplier=0,
                                    allow_small_or_imprecise_dtypes=True), inc=f"{tag}_ci")
    p.do("gpsimd", lambda e: e.iota(i16c[:, :], pattern=[[0, 1]], base=0,
                                    channel_multiplier=1,
                                    allow_small_or_imprecise_dtypes=True), inc=f"{tag}_ci")
    p.wait("vector", f"{tag}_ci")
    p.do("vector", lambda e: e.tensor_scalar(hm[:, :], i16r[:, :], 63.5, None,
                                             op0=ALU.is_gt), inc=f"{tag}_cs")
    p.wait("vector", f"{tag}_cs")
    p.wait("vector", f"{tag}_cs")
    p.barrier()

    for b in range(BLOC):
        rsub = ExitStack()
        st = Stream(p, c, cols=S, stack=rsub)
        emit_load_resid(p, c, st, x_ap, t0=b * S, t1=(b + 1) * S,
                        psums=[PB[i][:, 0:128] for i in range(4)])
        emit_ln(p, c, st, lnx, rms2=True, name=f"{tag}_ln{b}",
                tg0=b * (S // 512), tg1=(b + 1) * (S // 512), col0=b * S,
                psums=(PB[1], PB[2], PB[3], PB[4]), r2out=r2b)
        p.do("vector", lambda e: e.memset(Hb[:, :], 0.0), inc=f"{tag}_hz")
        p.do("vector", lambda e: e.memset(Hbf[:, :], 0.0), inc=f"{tag}_hz")
        p.wait("tensor", f"{tag}_hz")
        for n in range(NCH):
            sl = slice(n * CHUNK, (n + 1) * CHUNK)
            slg = sl
            p.barrier()
            p.wait("tensor", f"{tag}_r2v")
            p.do("tensor", lambda e, sl=sl:
                 e.matmul(ps_T1[:, :], c.ones_bf[0:1, :], r2b[0:1, sl],
                          start=True, stop=True), inc=f"{tag}_r2m")
            p.wait("vector", f"{tag}_r2m")
            p.do("vector", lambda e: e.tensor_copy(r2bcs[:, :], ps_T1[:, :]),
                 inc=f"{tag}_r2v")
            p.wait("tensor", f"{tag}_r2v")
            p.do("tensor", lambda e:
                 e.matmul(ps_T1[:, 0:1], r2bcs[:, :], c.col(1.0 / 128.0),
                          start=True, stop=True), inc=f"{tag}_r2m")
            p.wait("vector", f"{tag}_r2m")
            p.do("vector", lambda e: e.tensor_copy(r2col[:, 0:1], ps_T1[:, 0:1]),
                 inc=f"{tag}_r2v")
            p.wait("vector", f"{tag}_r2v")
            p.do("vector", lambda e: e.tensor_scalar(sclc[:, 0:1], r2col[:, 0:1],
                 1.0 / GATE_CAP, None, op0=ALU.mult), inc=f"{tag}_r2v")
            p.wait("tensor", f"{tag}_r2v")
            for oc in range(DC):
                pb = ps_b[oc % 2]
                p.wait("tensor", f"{tag}_pe")
                for kc in range(DC):
                    p.do("tensor", lambda e, oc=oc, kc=kc, sl=sl, pb=pb:
                         e.matmul(pb[:, :], wq[:, kc, oc * 128:(oc + 1) * 128],
                                  lnx[kc][:, sl], start=(kc == 0), stop=(kc == DC - 1)),
                         inc=f"{tag}_pm")
                p.wait("vector", f"{tag}_pm")
                p.do("vector", lambda e, oc=oc, pb=pb:
                     e.tensor_tensor(qfc[:, oc, :], pb[:, :], r2bcs[:, :],
                                     ALU.mult), inc=f"{tag}_pe")
                pb = ps_b[(oc + 1) % 2]
                p.wait("tensor", f"{tag}_pe")
                for kc in range(DC):
                    p.do("tensor", lambda e, oc=oc, kc=kc, sl=sl, pb=pb:
                         e.matmul(pb[:, :], wk[:, kc, oc * 128:(oc + 1) * 128],
                                  lnx[kc][:, sl], start=(kc == 0), stop=(kc == DC - 1)),
                         inc=f"{tag}_pm")
                p.wait("vector", f"{tag}_pm")
                p.do("vector", lambda e, oc=oc, pb=pb:
                     e.tensor_tensor(kfc[:, oc, :], pb[:, :], r2bcs[:, :],
                                     ALU.mult), inc=f"{tag}_pe")
            for g in range(2):
                p.wait("tensor", f"{tag}_pe")
                for kc in range(DC):
                    p.do("tensor", lambda e, g=g, kc=kc, sl=sl:
                         e.matmul(ps_a[:, :], lnx[kc][:, sl],
                                  wk[:, kc, g * 512:(g + 1) * 512],
                                  start=(kc == 0), stop=(kc == DC - 1)), inc=f"{tag}_pm")
                p.wait("vector", f"{tag}_pm")
                p.do("vector", lambda e, g=g:
                     e.tensor_scalar(ktm[:, g * 512:(g + 1) * 512], ps_a[:, :],
                                     r2col[:, 0:1], None, op0=ALU.mult),
                     inc=f"{tag}_pe")
                p.wait("tensor", f"{tag}_pe")
                for kc in range(DC):
                    p.do("tensor", lambda e, g=g, kc=kc, sl=sl:
                         e.matmul(ps_a[:, :], lnx[kc][:, sl],
                                  wv[:, kc, g * 512:(g + 1) * 512],
                                  start=(kc == 0), stop=(kc == DC - 1)), inc=f"{tag}_pm")
                p.wait("vector", f"{tag}_pm")
                p.do("vector", lambda e, g=g:
                     e.tensor_scalar(vtm[:, g * 512:(g + 1) * 512], ps_a[:, :],
                                     r2col[:, 0:1], None, op0=ALU.mult),
                     inc=f"{tag}_pe")
            p.wait("tensor", f"{tag}_pe")
            for kc in range(DC):
                p.do("tensor", lambda e, kc=kc, sl=sl:
                     e.matmul(ps_a[:, 0:48], lnx[kc][:, sl], wio[:, kc, :],
                              start=(kc == 0), stop=(kc == DC - 1)), inc=f"{tag}_pm")
            p.wait("scalar", f"{tag}_pm")
            p.wait("scalar", f"{tag}_r2v")
            p.do("scalar", lambda e: e.activation(t48[:, :],
                 ps_a[:, 0:48], AF.Tanh, scale=sclc[:, 0:1]), inc=f"{tag}_ga")
            p.wait("vector", f"{tag}_ga")
            for dst, c0 in ((logi, 0), (logf, 16)):
                p.do("vector", lambda e, dst=dst, c0=c0:
                     e.tensor_scalar(g16[:, 0:16], t48[:, c0:c0 + 16], -GATE_CAP,
                                     None, op0=ALU.mult), inc=f"{tag}_gv")
                p.wait("vector", f"{tag}_gv")
                p.do("vector", lambda e: e.tensor_scalar(g16[:, 16:32], g16[:, 0:16],
                     -1.0, None, op0=ALU.mult), inc=f"{tag}_gv")
                p.do("vector", lambda e: e.tensor_scalar(g16[:, 32:48], g16[:, 0:16],
                     0.0, None, op0=ALU.max), inc=f"{tag}_gv")
                p.wait("vector", f"{tag}_gv")
                p.do("vector", lambda e: e.tensor_tensor(g16[:, 16:32], g16[:, 0:16],
                     g16[:, 16:32], ALU.max), inc=f"{tag}_gv")
                p.wait("scalar", f"{tag}_gv")
                p.do("scalar", lambda e: e.activation(g16[:, 48:64], g16[:, 16:32],
                     AF.Exp, scale=-1.0), inc=f"{tag}_ga")
                p.wait("scalar", f"{tag}_ga")
                p.do("scalar", lambda e: e.activation(g16[:, 64:80], g16[:, 48:64],
                     AF.Ln, bias=c.col(1.0)), inc=f"{tag}_ga")
                p.wait("vector", f"{tag}_ga")
                p.do("vector", lambda e: e.tensor_tensor(g16[:, 48:64], g16[:, 32:48],
                     g16[:, 64:80], ALU.add), inc=f"{tag}_gv")
                p.wait("vector", f"{tag}_gv")
                p.do("vector", lambda e, dst=dst:
                     e.tensor_scalar(dst[:, :], g16[:, 48:64], -1.0, None,
                                     op0=ALU.mult), inc=f"{tag}_gv")
            p.wait("scalar", f"{tag}_gv")
            p.do("scalar", lambda e: e.activation(og[:, :], t48[:, 32:48], AF.Sigmoid,
                                                  scale=GATE_CAP), inc=f"{tag}_ga")
            p.wait("tensor", f"{tag}_ga")
            p.wait("tensor", f"{tag}_gv")
            p.do("tensor", lambda e: e.matmul(ps_F[:, 0:16], c.triu_incl_f[:, :],
                                              logf[:, :], start=True, stop=True),
                 inc=f"{tag}_fm")
            p.wait("vector", f"{tag}_fm")
            p.do("vector", lambda e: e.tensor_copy(Fsb[:, :], ps_F[:, 0:16]),
                 inc=f"{tag}_fv")
            p.wait("vector", f"{tag}_fv")
            p.do("vector", lambda e: e.tensor_tensor(blf[:, :], logi[:, :], Fsb[:, :],
                                                     ALU.subtract), inc=f"{tag}_fv")
            p.wait("tensor", f"{tag}_fv")
            p.do("tensor", lambda e: e.transpose(ps_F[0:16, 0:128],
                 Fsb[:, :], c.ident_f[:, :]), inc=f"{tag}_fm")
            p.wait("vector", f"{tag}_fm")
            p.do("vector", lambda e: e.tensor_copy(FTsb[:, :], ps_F[0:16, 0:128]),
                 inc=f"{tag}_fv")
            p.wait("tensor", f"{tag}_fv")
            p.do("tensor", lambda e: e.transpose(ps_F[0:16, 0:128], og[:, :],
                                                 c.ident_f[:, :]), inc=f"{tag}_fm")
            p.wait("vector", f"{tag}_fm")
            p.do("vector", lambda e: e.tensor_copy(OTsb[:, :], ps_F[0:16, 0:128]),
                 inc=f"{tag}_fv")
            p.wait("scalar", f"{tag}_fv")
            p.do("scalar", lambda e: e.activation(EFt[:, :], FTsb[:, :], AF.Exp),
                 inc=f"{tag}_ga")
            p.wait("tensor", f"{tag}_ga")
            for h in range(NH):
                hp, hr = h // 2, (h % 2) * 64
                if h % 2 == 0:
                    p.wait("vector", f"{tag}_e2m")
                    p.do("vector", lambda e, hp=hp: e.tensor_scalar(
                         sel2[:, :], hm[:, :], 1.0, float(2 * hp),
                         op0=ALU.mult, op1=ALU.add), inc=f"{tag}_slv")
                    p.wait("vector", f"{tag}_slv")
                    p.do("vector", lambda e: e.tensor_scalar(
                         sel2[:, :], sel2[:, :], i16c[:, 0:1], None,
                         op0=ALU.is_equal), inc=f"{tag}_slv")
                    p.wait("tensor", f"{tag}_slv")
                    p.wait("tensor", f"{tag}_e2v")
                    p.do("tensor", lambda e, hp=hp:
                         e.matmul(ps_o[:, :], sel2[:, :], EFt[:, :],
                                  start=True, stop=True), inc=f"{tag}_e2m")
                    p.wait("vector", f"{tag}_e2m")
                    p.do("vector", lambda e: e.tensor_copy(EF2sb[:, :], ps_o[:, :]),
                         inc=f"{tag}_e2v")
                    p.wait("tensor", f"{tag}_e2v")
                    p.do("tensor", lambda e, hp=hp:
                         e.matmul(ps_o[:, :], sel2[:, :], OTsb[:, :],
                                  start=True, stop=True), inc=f"{tag}_e2m")
                    p.wait("vector", f"{tag}_e2m")
                    p.do("vector", lambda e: e.tensor_copy(O2sb[:, :], ps_o[:, :]),
                         inc=f"{tag}_e2v")
                    p.wait("tensor", f"{tag}_e2v")
                p.wait("vector", f"{tag}_t1m")
                p.do("vector", lambda e, h=h: e.tensor_scalar(
                     eh1[:, :], i16c[:, 0:1].to_broadcast([16, 128]), float(h), None,
                     op0=ALU.is_equal), inc=f"{tag}_ehv")
                p.wait("tensor", f"{tag}_ga2")
                p.wait("tensor", f"{tag}_ehv")
                p.do("tensor", lambda e, h=h:
                     e.matmul(ps_T1[:, :], eh1[:, :], FTsb[:, :],
                              start=True, stop=False), inc=f"{tag}_t1m")
                p.do("tensor", lambda e:
                     e.matmul(ps_T1[:, :], c.ident_f[:, :], c.mask_sq[:, :],
                              start=False, stop=True), inc=f"{tag}_t1m")
                p.wait("scalar", f"{tag}_t1m")
                p.wait("scalar", f"{tag}_sv")
                p.do("scalar", lambda e, h=h:
                     e.activation(Mexp[:, :], ps_T1[:, :], AF.Exp,
                                  bias=blf[:, h:h + 1]), inc=f"{tag}_ga2")
                p.do("tensor", lambda e, hp=hp, hr=hr:
                     e.matmul(ps_S[:, :], kfc[hr:hr + 64, hp, :], qfc[hr:hr + 64, hp, :],
                              start=True, stop=True), inc=f"{tag}_sm")
                p.wait("vector", f"{tag}_sm")
                p.wait("vector", f"{tag}_ga2")
                p.do("vector", lambda e: e.tensor_tensor(SpT[:, :], ps_S[:, :],
                     Mexp[:, :], ALU.mult), inc=f"{tag}_sv")
                p.do("vector", lambda e, hp=hp, hr=hr:
                     e.tensor_tensor(qp[:, :], qfc[hr:hr + 64, hp, :],
                                     EF2sb[hr:hr + 64, :], ALU.mult), inc=f"{tag}_sv")
                p.do("vector", lambda e:
                     e.tensor_copy(ccol[:, :], Mexp[:, 127:128]), inc=f"{tag}_sv")
                p.wait("vector", f"{tag}_sv")
                p.do("vector", lambda e, h=h:
                     e.tensor_scalar(kpp[:, :], ktm[:, h * 64:(h + 1) * 64],
                                     ccol[:, 0:1], None, op0=ALU.mult),
                     inc=f"{tag}_sv")
                p.wait("tensor", f"{tag}_sv")
                p.do("tensor", lambda e, h=h:
                     e.matmul(ps_o[0:64, :], vtm[:, h * 64:(h + 1) * 64], SpT[:, :],
                              start=True, stop=False), inc=f"{tag}_om")
                mo = p.do("tensor", lambda e, h=h:
                     e.matmul(ps_o[0:64, :], Hbf[:, h * 64:(h + 1) * 64], qp[:, :],
                              start=False, stop=True), inc=f"{tag}_om")
                p.do("tensor", lambda e, h=h:
                     e.matmul(ps_h[:, :], kpp[:, :], vtm[:, h * 64:(h + 1) * 64],
                              start=True, stop=True), inc=f"{tag}_hm")
                p.wait("vector", f"{tag}_om")
                p.do("vector", lambda e, h=h, hp=hp, hr=hr:
                     e.tensor_tensor(hout[hr:hr + 64, hp, :], ps_o[0:64, :],
                                     O2sb[hr:hr + 64, :], ALU.mult), inc=f"{tag}_ov")
                p.wait("vector", f"{tag}_hm")
                p.do("vector", lambda e, h=h, hr=hr:
                     e.tensor_scalar(Hb[:, h * 64:(h + 1) * 64],
                                     Hb[:, h * 64:(h + 1) * 64],
                                     EF2sb[hr:hr + 64, 127:128], None, op0=ALU.mult),
                     inc=f"{tag}_hv")
                p.wait("vector", f"{tag}_hv")
                p.do("vector", lambda e, h=h:
                     e.tensor_tensor(Hb[:, h * 64:(h + 1) * 64],
                                     Hb[:, h * 64:(h + 1) * 64], ps_h[:, :], ALU.add),
                     inc=f"{tag}_hv")
                p.wait("vector", f"{tag}_hv")
                p.wait("vector", f"{tag}_om", mo)
                p.do("vector", lambda e, h=h:
                     e.tensor_copy(Hbf[:, h * 64:(h + 1) * 64],
                                   Hb[:, h * 64:(h + 1) * 64]), inc=f"{tag}_hv")
                p.wait("tensor", f"{tag}_hv")
                p.wait("tensor", f"{tag}_ov")
            for dc in range(DC):
                pb = ps_b[dc % 2]
                p.wait("tensor", f"{tag}_we")
                for fc in range(DC):
                    p.do("tensor", lambda e, dc=dc, fc=fc, pb=pb:
                         e.matmul(pb[:, :], wo[:, fc, dc * 128:(dc + 1) * 128],
                                  hout[:, fc, :], start=(fc == 0), stop=(fc == DC - 1)),
                         inc=f"{tag}_wm")
                p.wait("vector", f"{tag}_wm")
                p.do("vector", lambda e, dc=dc, sl=sl, pb=pb:
                     e.tensor_tensor(wev[:, :], pb[:, :],
                                     lnx[dc][:, sl], ALU.add), inc=f"{tag}_we")
                p.wait("vector", f"{tag}_we")
                p.do("vector", lambda e, dc=dc, slg=slg:
                     e.tensor_tensor(st.resid[dc][:, slg], st.resid[dc][:, slg],
                                     wev[:, :], ALU.add), inc=f"{tag}_we")
        p.barrier()
        emit_store_out(p, c, st, y_ap, t0=b * S, t1=(b + 1) * S,
                       psums=[PB[i][:, 0:128] for i in range(2)])
        rsub.close()
    p.barrier()
    es.close()


def emit_l1attn(p: Prog, c: Consts, waps, x_ap, y_ap, tag, upto=9, dbga=None, dbgb=None):
    SCALE = 0.125
    NKT = S // 128
    NQG = S // 512
    es = ExitStack()
    lnx = [p.sbuf(f"{tag}_lnx{dc}", [128, S], BF16, stack=es) for dc in range(DC)]
    qf = p.sbuf(f"{tag}_qf", [128, S], BF16, stack=es)
    k1f = p.sbuf(f"{tag}_k1f", [128, S], BF16, stack=es)
    k2f = p.sbuf(f"{tag}_k2f", [128, S], BF16, stack=es)
    vtm = p.sbuf(f"{tag}_vtm", [128, NKT, 128], BF16, stack=es)
    es1 = p.sbuf(f"{tag}_es1", [128, NKT, 512], BF16, stack=es)
    es2_ = p.sbuf(f"{tag}_es2", [128, NKT, 512], BF16, stack=es)
    hout = p.sbuf(f"{tag}_ho", [128, S], BF16, stack=es)
    wpr = [p.sbuf(f"{tag}_wp{i}", [128, DC, 128], BF16, stack=es) for i in range(4)]
    wosl = p.sbuf(f"{tag}_wo", [128, D], BF16, stack=es)
    Ba = p.sbuf(f"{tag}_Ba", [128, 512], F32, stack=es)
    Bb = p.sbuf(f"{tag}_Bb", [128, 512], F32, stack=es)
    Bg = p.sbuf(f"{tag}_Bg", [64, 512], F32, stack=es)
    Nt = p.sbuf(f"{tag}_Nt", [128, 512], F32, stack=es)
    Nsb = p.sbuf(f"{tag}_Nsb", [128, 512], BF16, stack=es)
    Asb = p.sbuf(f"{tag}_Asb", [128, 512], BF16, stack=es)
    lam16 = p.sbuf(f"{tag}_lam", [128, 16], F32, stack=es)
    llrow = p.sbuf(f"{tag}_ll", [1, 16], F32, stack=es)
    PB = [p.psum(f"{tag}_PB{i}", [128, 512], F32, stack=es) for i in range(8)]
    psA, psZ1, psZ2, psD, psO = PB[5], PB[6], PB[7], PB[0], PB[4]
    psA2 = [PB[5], PB[1]]
    p.dma("sync", llrow[:, :], waps["dll"], inc=f"{tag}_ll")
    p.wait("tensor", f"{tag}_ll")
    p.do("tensor", lambda e: e.matmul(PB[4][:, 0:16], c.ones_f[0:1, :], llrow[:, :],
                                      start=True, stop=True), inc=f"{tag}_lm")
    p.wait("scalar", f"{tag}_lm")
    p.do("scalar", lambda e: e.activation(lam16[:, :], PB[4][:, 0:16], AF.Sigmoid),
         inc=f"{tag}_la")
    p.wait("tensor", f"{tag}_la")
    p.wait("vector", f"{tag}_la")
    for b in range(BLOC):
        rsub = ExitStack()
        st = Stream(p, c, cols=S, stack=rsub)
        emit_load_resid(p, c, st, x_ap, t0=b * S, t1=(b + 1) * S,
                        psums=[PB[i][:, 0:128] for i in range(4)])
        emit_ln(p, c, st, lnx, rms2=False, name=f"{tag}_ln{b}",
                tg0=b * (S // 512), tg1=(b + 1) * (S // 512), col0=b * S,
                psums=(PB[0], PB[1], PB[2], PB[3]))
        if dbga is not None and b == 1:
            p.wait("gpsimd", f"{tag}_woe")
            p.dma("gpsimd", dbgb[8, 0:128, 0:512], st.resid[0][:, 0:512],
                  inc=f"{tag}_dbg")
            p.wait("gpsimd", f"{tag}_dbg")
        for hp in range(DC):
            if dbga is not None and b == 0:
                p.wait("gpsimd", f"{tag}_woe")
                p.dma("gpsimd", dbgb[hp, 0:128, 0:512], st.resid[0][:, 0:512],
                      inc=f"{tag}_dbg")
                p.wait("gpsimd", f"{tag}_dbg")
            p.wait("gpsimd", f"{tag}_pm")
            p.wait("gpsimd", f"{tag}_wom")
            for i, nm in enumerate(("dq", "dk1", "dk2", "dv")):
                p.dma("gpsimd", wpr[i][:, :, :],
                      waps[nm][:, :, hp * 128:(hp + 1) * 128], inc=f"{tag}_wl")
            p.dma("gpsimd", wosl[:, :], waps["do"][:, hp, :], inc=f"{tag}_wl")
            p.wait("tensor", f"{tag}_wl")
            p.wait("tensor", f"{tag}_woe")
            for dst, wi in ((qf, 0), (k1f, 1), (k2f, 2)):
                for tg in range(NQG):
                    pb = PB[1 + tg % 2]
                    p.wait("tensor", f"{tag}_pe")
                    for kc in range(DC):
                        p.do("tensor", lambda e, wi=wi, kc=kc, tg=tg, b=b, pb=pb:
                             e.matmul(pb[:, :], wpr[wi][:, kc, :],
                                      lnx[kc][:, tg * 512:(tg + 1) * 512],
                                      start=(kc == 0), stop=(kc == DC - 1)),
                             inc=f"{tag}_pm")
                    p.wait("vector", f"{tag}_pm")
                    p.do("vector", lambda e, dst=dst, tg=tg, pb=pb:
                         e.tensor_copy(dst[:, tg * 512:(tg + 1) * 512], pb[:, :]),
                         inc=f"{tag}_pe")
            for kt in range(NKT):
                pb = PB[1 + kt % 2]
                p.wait("tensor", f"{tag}_pe")
                for kc in range(DC):
                    p.do("tensor", lambda e, kc=kc, kt=kt, pb=pb:
                         e.matmul(pb[:, 0:128], lnx[kc][:, kt * 128:(kt + 1) * 128],
                                  wpr[3][:, kc, :], start=(kc == 0),
                                  stop=(kc == DC - 1)), inc=f"{tag}_pm")
                p.wait("vector", f"{tag}_pm")
                p.do("vector", lambda e, kt=kt, pb=pb:
                     e.tensor_copy(vtm[:, kt, :], pb[:, 0:128]), inc=f"{tag}_pe")
            p.wait("tensor", f"{tag}_pe")
            for h2 in range(2 if upto >= 2 else 0):
                h = 2 * hp + h2
                hr = h2 * 64
                for qg in range(NQG):
                    nkt = 4 * (qg + 1)
                    qsl = slice(qg * 512, (qg + 1) * 512)
                    p.wait("scalar", f"{tag}_cv")
                    ea_marks = [p.mark(f"{tag}_ea")] * 2
                    for mat, (kf, est, psZ) in enumerate(
                            ((k1f, es1, psZ1), (k2f, es2_, psZ2))):
                        for kt in range(nkt):
                            jj = kt - 4 * qg
                            pa = psA2[(mat * nkt + kt) % 2]
                            p.wait("tensor", f"{tag}_ea",
                                   ea_marks[(mat * nkt + kt) % 2])
                            smk = p.do("tensor",
                                 lambda e, kf=kf, kt=kt, hr=hr, qsl=qsl, pa=pa:
                                 e.matmul(pa[:, :],
                                          kf[hr:hr + 64, kt * 128:(kt + 1) * 128],
                                          qf[hr:hr + 64, qsl],
                                          start=True, stop=(kt < 4 * qg)),
                                 inc=f"{tag}_sm")
                            if jj >= 0:
                                smk = p.do("tensor", lambda e, jj=jj, pa=pa:
                                     e.matmul(pa[:, jj * 128:(jj + 1) * 128],
                                              c.ident_f[:, :], c.mask_sq[:, :],
                                              start=False, stop=True),
                                     inc=f"{tag}_sm")
                            p.wait("scalar", f"{tag}_sm", smk)
                            ea_marks[(mat * nkt + kt) % 2] = p.do("scalar",
                                 lambda e, est=est, kt=kt, pa=pa:
                                 e.activation(est[:, kt, :], pa[:, :], AF.Exp,
                                              scale=SCALE, bias=c.col(-12.0)),
                                 inc=f"{tag}_ea")
                            if jj >= 1:
                                p.wait("vector", f"{tag}_ea")
                                p.do("vector", lambda e, est=est, kt=kt, jj=jj:
                                     e.memset(est[:, kt, 0:jj * 128], 0.0),
                                     inc=f"{tag}_ev")
                            p.wait("tensor", f"{tag}_ea")
                            p.wait("tensor", f"{tag}_ev")
                            p.do("tensor", lambda e, est=est, kt=kt, psZ=psZ:
                                 e.matmul(psZ[:, :], c.ones_bf[:, :], est[:, kt, :],
                                          start=(kt == 0), stop=(kt == nkt - 1)),
                                 inc=f"{tag}_zm")
                    if upto < 3:
                        continue
                    p.wait("vector", f"{tag}_zm")
                    p.do("vector", lambda e: e.reciprocal(Ba[:, :], psZ1[:, :]),
                         inc=f"{tag}_bv")
                    p.do("vector", lambda e: e.reciprocal(Bb[:, :], psZ2[:, :]),
                         inc=f"{tag}_bv")
                    p.wait("vector", f"{tag}_bv")
                    p.do("vector", lambda e, h=h:
                         e.tensor_scalar(Bb[:, :], Bb[:, :], lam16[:, h:h + 1],
                                         None, op0=ALU.mult), inc=f"{tag}_bv")
                    p.wait("vector", f"{tag}_bv")
                    if upto < 4:
                        continue
                    p.wait("tensor", f"{tag}_hv")
                    p.wait("tensor", f"{tag}_gv")
                    for kt in range(nkt):
                        p.wait("vector", f"{tag}_om")
                        p.wait("vector", f"{tag}_dm")
                        p.do("vector", lambda e, kt=kt:
                             e.tensor_tensor(Nt[:, :], es1[:, kt, :], Ba[:, :],
                                             ALU.mult), inc=f"{tag}_cv")
                        p.wait("vector", f"{tag}_cv")
                        p.do("vector", lambda e, kt=kt:
                             e.tensor_tensor(Asb[:, :], es2_[:, kt, :], Bb[:, :],
                                             ALU.mult), inc=f"{tag}_cv")
                        p.wait("vector", f"{tag}_cv")
                        p.do("vector", lambda e:
                             e.tensor_tensor(Nsb[:, :], Nt[:, :], Asb[:, :],
                                             ALU.subtract), inc=f"{tag}_cv")
                        p.wait("vector", f"{tag}_cv")
                        p.do("vector", lambda e:
                             e.tensor_scalar(Nt[:, :], Nsb[:, :], -1.0, None,
                                             op0=ALU.mult), inc=f"{tag}_cv")
                        p.wait("vector", f"{tag}_cv")
                        p.do("vector", lambda e:
                             e.tensor_tensor(Asb[:, :], Nsb[:, :], Nt[:, :],
                                             ALU.max), inc=f"{tag}_cv")
                        p.wait("tensor", f"{tag}_cv")
                        p.do("tensor", lambda e, kt=kt:
                             e.matmul(psD[:, :], c.ones_bf[:, :], Asb[:, :],
                                      start=(kt == 0), stop=(kt == nkt - 1)),
                             inc=f"{tag}_dm")
                        p.do("tensor", lambda e, kt=kt, hr=hr:
                             e.matmul(psO[0:64, :], vtm[:, kt, hr:hr + 64],
                                      Nsb[:, :], start=(kt == 0),
                                      stop=(kt == nkt - 1)), inc=f"{tag}_om")
                    if upto < 5:
                        continue
                    if dbga is not None and b == 0 and hp == 3 and h2 == 0 and qg == 0:
                        p.wait("gpsimd", f"{tag}_cv")
                        p.wait("gpsimd", f"{tag}_bv")
                        p.dma("gpsimd", dbga[0, 0:128, 0:512], es1[:, 0, :], inc=f"{tag}_dbg")
                        p.dma("gpsimd", dbga[1, 0:128, 0:512], es2_[:, 0, :], inc=f"{tag}_dbg")
                        p.dma("gpsimd", dbga[2, 0:128, 0:512], Ba[:, :], inc=f"{tag}_dbg")
                        p.dma("gpsimd", dbga[3, 0:128, 0:512], Bb[:, :], inc=f"{tag}_dbg")
                        p.dma("gpsimd", dbga[4, 0:128, 0:512], Nsb[:, :], inc=f"{tag}_dbg")
                        p.dma("gpsimd", dbga[5, 0:128, 0:512], Asb[:, :], inc=f"{tag}_dbg")
                        p.wait("gpsimd", f"{tag}_dbg")
                    p.wait("vector", f"{tag}_dm")
                    p.do("vector", lambda e: e.tensor_scalar(Bg[:, :], psD[0:64, :],
                         1e-6, None, op0=ALU.max), inc=f"{tag}_gv")
                    p.wait("vector", f"{tag}_gv")
                    p.do("vector", lambda e: e.reciprocal(Bg[:, :], Bg[:, :]),
                         inc=f"{tag}_gv")
                    p.wait("vector", f"{tag}_om")
                    p.wait("vector", f"{tag}_gv")
                    p.do("vector", lambda e, hr=hr, qsl=qsl:
                         e.tensor_tensor(hout[hr:hr + 64, qsl], psO[0:64, :],
                                         Bg[:, :], ALU.mult), inc=f"{tag}_hv")
                    p.wait("tensor", f"{tag}_hv")
                    if dbga is not None and b == 0 and hp == 3 and h2 == 1 and qg == 0:
                        p.wait("gpsimd", f"{tag}_hv")
                        p.wait("gpsimd", f"{tag}_gv")
                        p.dma("gpsimd", dbga[6, 0:64, 0:512], Bg[:, :], inc=f"{tag}_dbg")
                        p.dma("gpsimd", dbga[7, 0:128, 0:512], hout[:, 0:512], inc=f"{tag}_dbg")
                        p.wait("gpsimd", f"{tag}_dbg")
            for tg in range(NQG if upto >= 6 else 0):
                for dc in range(DC):
                    pb = PB[1 + dc % 2]
                    p.wait("tensor", f"{tag}_woe")
                    p.do("tensor", lambda e, dc=dc, tg=tg, pb=pb:
                         e.matmul(pb[:, :], wosl[:, dc * 128:(dc + 1) * 128],
                                  hout[:, tg * 512:(tg + 1) * 512],
                                  start=True, stop=True), inc=f"{tag}_wom")
                    p.wait("vector", f"{tag}_wom")
                    p.do("vector", lambda e, dc=dc, tg=tg, pb=pb:
                         e.tensor_tensor(st.resid[dc][:, tg * 512:(tg + 1) * 512],
                                         st.resid[dc][:, tg * 512:(tg + 1) * 512],
                                         pb[:, :], ALU.add), inc=f"{tag}_woe")
        p.barrier()
        emit_store_out(p, c, st, y_ap, t0=b * S, t1=(b + 1) * S,
                       psums=[PB[i][:, 0:128] for i in range(2)])
        rsub.close()
    p.barrier()
    es.close()


def _prep_common(inputs):
    d = {}
    d["l1_wfcT"] = np.ascontiguousarray(inputs["L1_ffn_W_fc"].T)
    d["l1_wprT"] = np.ascontiguousarray(inputs["L1_ffn_W_proj"].T)
    d["l1_lnw"] = np.ascontiguousarray(
        np.asarray(inputs["L1_ffn_ln_w"]).reshape(HC_FFN, 128).T)
    d["l1_lnb"] = np.ascontiguousarray(
        np.asarray(inputs["L1_ffn_ln_b"]).reshape(HC_FFN, 128).T)
    d["router"] = np.ascontiguousarray(
        np.asarray(inputs["L0_ffn_router_w"], dtype=np.float32)
        .reshape(DC, 128, E_MOE).transpose(1, 0, 2))
    d["g5"] = np.ascontiguousarray(
        np.asarray(inputs["L0_ffn_gate_w"], dtype=np.float32)
        .reshape(E_MOE, DC, 128, DC, 128).transpose(0, 2, 1, 3, 4))
    d["u5"] = np.ascontiguousarray(
        np.asarray(inputs["L0_ffn_up_w"], dtype=np.float32)
        .reshape(E_MOE, DC, 128, DC, 128).transpose(0, 2, 1, 3, 4))
    d["d5"] = np.ascontiguousarray(
        np.asarray(inputs["L0_ffn_down_w"], dtype=np.float32)
        .reshape(E_MOE, DC, 128, D).transpose(0, 2, 1, 3))
    def _wt(w):
        return np.ascontiguousarray(np.asarray(w, dtype=np.float32).T
                                    .reshape(DC, 128, -1).transpose(1, 0, 2))
    d["aq"] = _wt(inputs["L0_attn_Wq"])
    d["ak"] = _wt(inputs["L0_attn_Wk"])
    d["av"] = _wt(inputs["L0_attn_Wv"])
    d["aio"] = np.ascontiguousarray(np.concatenate(
        [np.asarray(inputs["L0_attn_Wi"], dtype=np.float32).T,
         np.asarray(inputs["L0_attn_Wf"], dtype=np.float32).T,
         np.asarray(inputs["L0_attn_Wo"], dtype=np.float32).T], axis=1)
        .reshape(DC, 128, 48).transpose(1, 0, 2))
    d["aout"] = _wt(inputs["L0_attn_Wout"])
    d["alnw"] = np.ascontiguousarray(
        np.asarray(inputs["L0_attn_ln_w"], dtype=np.float32).reshape(DC, 128).T)
    d["dq"] = _wt(inputs["L1_attn_Wq"])
    d["dk1"] = _wt(inputs["L1_attn_Wk1"])
    d["dk2"] = _wt(inputs["L1_attn_Wk2"])
    d["dv"] = _wt(inputs["L1_attn_Wv"])
    d["do"] = _wt(inputs["L1_attn_Wo"])
    d["dll"] = np.ascontiguousarray(
        np.asarray(inputs["L1_attn_lambda_logit"], dtype=np.float32).reshape(1, 16))
    return d


def _cmask(core):
    m = np.zeros((8, 1), np.float32)
    m[:core] = 1.0
    return m


def _x_shard(x, core):
    return np.ascontiguousarray(
        np.asarray(x).reshape(16, S, D)[core * BLOC:(core + 1) * BLOC]
        .reshape(T, D))


def build_l1ffn_launch(first, parts=("quant", "ln", "ffn")):
    nc = bass.Bass("TRN2", target_bir_lowering=False)
    x_in = nc.dram_tensor("x", [T, D], F32, kind="ExternalInput").ap()
    x_out = nc.dram_tensor("y", [T, D], F32, kind="ExternalOutput").ap()
    wfcT = nc.dram_tensor("l1_wfcT", [D, H_FFN], F32, kind="ExternalInput").ap()
    wprT = nc.dram_tensor("l1_wprT", [H_FFN, D], F32, kind="ExternalInput").ap()
    lnw = nc.dram_tensor("l1_lnw", [128, HC_FFN], F32, kind="ExternalInput").ap()
    lnb = nc.dram_tensor("l1_lnb", [128, HC_FFN], F32, kind="ExternalInput").ap()
    if first:
        qfc = nc.dram_tensor("qfc", [HC_FFN, DC, 128, 128], BF16,
                             kind="ExternalOutput").ap()
        qpr = nc.dram_tensor("qpr", [DC, HC_FFN, 128, 128], BF16,
                             kind="ExternalOutput").ap()
    else:
        qfc = nc.dram_tensor("qfc", [HC_FFN, DC, 128, 128], BF16,
                             kind="ExternalInput").ap()
        qpr = nc.dram_tensor("qpr", [DC, HC_FFN, 128, 128], BF16,
                             kind="ExternalInput").ap()
    p = Prog(nc)
    c = Consts(p)
    c.emit()
    if first and "quant" in parts:
        emit_quant(p, c, wfcT, qfc, D, H_FFN, "qf")
        emit_quant(p, c, wprT, qpr, H_FFN, D, "qp")
    emit_l1ffn_sub(p, c, x_in, x_out, qfc, qpr, lnw, lnb, "ff")
    return p.build()


def emit_l1ffn_sub(p, c, x_in, x_out, qfc, qpr, lnw, lnb, tag):
    for b in range(BLOC):
        sub = ExitStack()
        st = Stream(p, c, cols=S, stack=sub)
        lnxt = [p.sbuf(f"{tag}lnx{b}_{dc}", [128, S], BF16, stack=sub)
                for dc in range(DC)]
        emit_load_resid(p, c, st, x_in, t0=b * S, t1=(b + 1) * S)
        emit_ln(p, c, st, lnxt, rms2=False, name=f"{tag}ln{b}",
                tg0=b * (S // 512), tg1=(b + 1) * (S // 512), col0=b * S)
        emit_l1ffn(p, c, st, lnxt, qfc, qpr, lnw, lnb, f"{tag}{b}",
                   tg0=b * (S // 512), tg1=(b + 1) * (S // 512), col0=b * S)
        emit_store_out(p, c, st, x_out, t0=b * S, t1=(b + 1) * S)
        sub.close()
        p.epoch()


def emit_moe_sub(p, c, x_in, x_out, router, aps, tag):
    esk = ExitStack()
    rsb = p.sbuf(f"{tag}_rsb", [128, DC, E_MOE], F32, stack=esk)
    lgall = p.sbuf(f"{tag}_lg", [128, NTT * E_MOE], F32, stack=esk)
    p.dma("sync", rsb[:, :, :], router[:, :, :], inc=f"{tag}_rld")
    srcst = p.sbuf(f"{tag}_src", [128, NTT * 2], U32, stack=esk)
    wstash = p.sbuf(f"{tag}_wsh", [128, NTT * 2], F32, stack=esk)
    lnsub = ExitStack()
    lnxt = [p.sbuf(f"{tag}_lnx{dc}", [128, T], BF16, stack=lnsub)
            for dc in range(DC)]
    p.wait("tensor", f"{tag}_rld")
    for b in range(BLOC):
        rsub = ExitStack()
        rst = Stream(p, c, cols=S, stack=rsub)
        emit_load_resid(p, c, rst, x_in, t0=b * S, t1=(b + 1) * S)
        emit_ln(p, c, rst, lnxt, rms2=False, name=f"{tag}_ln{b}", router=rsb,
                lgall=lgall, tg0=b * (S // 512), tg1=(b + 1) * (S // 512),
                col0=0, rcol0=b * S)
        rsub.close()
    emit_l0ffn_route(p, c, lnxt, lgall, aps, tag, srcst, wstash)
    lnsub.close()
    emit_l0ffn_exec(p, c, aps, x_in, x_out, tag, srcst, wstash)
    esk.close()


def build_moe_launch():
    nc = bass.Bass("TRN2", target_bir_lowering=False)
    x_in = nc.dram_tensor("x", [T, D], F32, kind="ExternalInput").ap()
    x_out = nc.dram_tensor("y", [T, D], F32, kind="ExternalOutput").ap()
    router = nc.dram_tensor("router", [128, DC, E_MOE], F32, kind="ExternalInput").ap()
    g5 = nc.dram_tensor("g5", [E_MOE, 128, DC, DC, 128], F32, kind="ExternalInput").ap()
    u5 = nc.dram_tensor("u5", [E_MOE, 128, DC, DC, 128], F32, kind="ExternalInput").ap()
    d5 = nc.dram_tensor("d5", [E_MOE, 128, DC, D], F32, kind="ExternalInput").ap()
    cmask = nc.dram_tensor("cmask", [8, 1], F32, kind="ExternalInput").ap()
    aps = {
        "xe": nc.dram_tensor("moe_xe", [NSLOT, D], BF16, kind="Internal").ap(),
        "outsc": nc.dram_tensor("moe_out", [NSLOT + 1, D], BF16, kind="Internal").ap(),
        "cc_in": nc.dram_tensor("moe_ccin", [1, 32], F32, kind="Internal").ap(),
        "cc_out": nc.dram_tensor("moe_ccout", [8, 32], F32, kind="Internal").ap(),
        "g5": g5, "u5": u5, "d5": d5, "cmask": cmask,
    }
    p = Prog(nc)
    c = Consts(p)
    c.emit()
    emit_moe_sub(p, c, x_in, x_out, router, aps, "mo")
    return p.build()


def build_l0attn_launch():
    nc = bass.Bass("TRN2", target_bir_lowering=False)
    x_in = nc.dram_tensor("x", [T, D], F32, kind="ExternalInput").ap()
    x_out = nc.dram_tensor("y", [T, D], F32, kind="ExternalOutput").ap()
    waps = {}
    for nm, sh in (("aq", [128, DC, D]), ("ak", [128, DC, D]), ("av", [128, DC, D]),
                   ("aio", [128, DC, 48]), ("aout", [128, DC, D]),
                   ("alnw", [128, DC])):
        waps[nm] = nc.dram_tensor(nm, sh, F32, kind="ExternalInput").ap()
    p = Prog(nc)
    c = Consts(p)
    c.emit()
    emit_l0attn(p, c, waps, x_in, x_out, "at")
    return p.build()


def build_l1attn_launch(upto=9, dbg=False):
    nc = bass.Bass("TRN2", target_bir_lowering=False)
    x_in = nc.dram_tensor("x", [T, D], F32, kind="ExternalInput").ap()
    x_out = nc.dram_tensor("y", [T, D], F32, kind="ExternalOutput").ap()
    dbga = (nc.dram_tensor("dbga", [8, 128, 512], F32, kind="ExternalOutput").ap()
            if dbg else None)
    dbgb = (nc.dram_tensor("dbgb", [9, 128, 512], BF16, kind="ExternalOutput").ap()
            if dbg else None)
    waps = {}
    for nm in ("dq", "dk1", "dk2", "dv", "do"):
        waps[nm] = nc.dram_tensor(nm, [128, DC, D], F32, kind="ExternalInput").ap()
    waps["dll"] = nc.dram_tensor("dll", [1, 16], F32, kind="ExternalInput").ap()
    p = Prog(nc)
    c = Consts(p)
    c.emit()
    emit_l1attn(p, c, waps, x_in, x_out, "da", upto=upto, dbga=dbga, dbgb=dbgb)
    return p.build()


def build_fused(dbg=False):
    nc = bass.Bass("TRN2", target_bir_lowering=False)
    x_in = nc.dram_tensor("x", [T, D], F32, kind="ExternalInput").ap()
    y_out = nc.dram_tensor("y", [T, D], F32, kind="ExternalOutput").ap()
    xi = [nc.dram_tensor(f"xi{i}", [T, D], F32,
                         kind=("ExternalOutput" if dbg else "Internal")).ap()
          for i in range(5)]
    router = nc.dram_tensor("router", [128, DC, E_MOE], F32, kind="ExternalInput").ap()
    g5 = nc.dram_tensor("g5", [E_MOE, 128, DC, DC, 128], F32, kind="ExternalInput").ap()
    u5 = nc.dram_tensor("u5", [E_MOE, 128, DC, DC, 128], F32, kind="ExternalInput").ap()
    d5 = nc.dram_tensor("d5", [E_MOE, 128, DC, D], F32, kind="ExternalInput").ap()
    cmask = nc.dram_tensor("cmask", [8, 1], F32, kind="ExternalInput").ap()
    aps = {
        "xe": nc.dram_tensor("moe_xe", [NSLOT, D], BF16, kind="Internal").ap(),
        "outsc": nc.dram_tensor("moe_out", [NSLOT + 1, D], BF16, kind="Internal").ap(),
        "cc_in": nc.dram_tensor("moe_ccin", [1, 32], F32, kind="Internal").ap(),
        "cc_out": nc.dram_tensor("moe_ccout", [8, 32], F32, kind="Internal").ap(),
        "g5": g5, "u5": u5, "d5": d5, "cmask": cmask,
    }
    aps2 = dict(aps)
    aps2["cc_in"] = nc.dram_tensor("moe_ccin2", [1, 32], F32, kind="Internal").ap()
    aps2["cc_out"] = nc.dram_tensor("moe_ccout2", [8, 32], F32, kind="Internal").ap()
    awaps = {}
    for nm, sh in (("aq", [128, DC, D]), ("ak", [128, DC, D]), ("av", [128, DC, D]),
                   ("aio", [128, DC, 48]), ("aout", [128, DC, D]),
                   ("alnw", [128, DC])):
        awaps[nm] = nc.dram_tensor(nm, sh, F32, kind="ExternalInput").ap()
    dwaps = {}
    for nm in ("dq", "dk1", "dk2", "dv", "do"):
        dwaps[nm] = nc.dram_tensor(nm, [128, DC, D], F32, kind="ExternalInput").ap()
    dwaps["dll"] = nc.dram_tensor("dll", [1, 16], F32, kind="ExternalInput").ap()
    wfcT = nc.dram_tensor("l1_wfcT", [D, H_FFN], F32, kind="ExternalInput").ap()
    wprT = nc.dram_tensor("l1_wprT", [H_FFN, D], F32, kind="ExternalInput").ap()
    lnw = nc.dram_tensor("l1_lnw", [128, HC_FFN], F32, kind="ExternalInput").ap()
    lnb = nc.dram_tensor("l1_lnb", [128, HC_FFN], F32, kind="ExternalInput").ap()
    qfc = nc.dram_tensor("qfc", [HC_FFN, DC, 128, 128], BF16, kind="Internal").ap()
    qpr = nc.dram_tensor("qpr", [DC, HC_FFN, 128, 128], BF16, kind="Internal").ap()

    p = Prog(nc)
    c = Consts(p)
    c.emit()
    emit_quant(p, c, wfcT, qfc, D, H_FFN, "qf")
    p.epoch()
    emit_quant(p, c, wprT, qpr, H_FFN, D, "qp")
    p.epoch()
    emit_moe_sub(p, c, x_in, xi[0], router, aps, "m1")
    p.epoch()
    emit_l0attn(p, c, awaps, xi[0], xi[1], "at")
    p.epoch()
    emit_moe_sub(p, c, xi[1], xi[2], router, aps2, "m2")
    p.epoch()
    emit_l1ffn_sub(p, c, xi[2], xi[3], qfc, qpr, lnw, lnb, "fa")
    p.epoch()
    emit_l1attn(p, c, dwaps, xi[3], xi[4], "da")
    p.epoch()
    emit_l1ffn_sub(p, c, xi[4], y_out, qfc, qpr, lnw, lnb, "fb")
    return p.build()


def kernel(**inputs):
    com = _prep_common(inputs)
    x = np.asarray(inputs["L0_attn_x"], dtype=np.float32)
    nc = build_fused()
    in_maps = []
    for core in range(NCORES):
        m = {"x": _x_shard(x, core), "cmask": _cmask(core)}
        for k in ("router", "g5", "u5", "d5", "aq", "ak", "av", "aio", "aout",
                  "alnw", "dq", "dk1", "dk2", "dv", "do", "dll",
                  "l1_wfcT", "l1_wprT", "l1_lnw", "l1_lnb"):
            m[k] = com[k]
        in_maps.append(m)
    res = run_bass_kernel_spmd(nc, in_maps, core_ids=list(range(NCORES)))
    out = np.concatenate([r["y"].reshape(BLOC, S, D) for r in res.results], axis=0)
    return out


def kernel_unfused(**inputs):
    com = _prep_common(inputs)
    x = np.asarray(inputs["L0_attn_x"], dtype=np.float32)
    shards = [_x_shard(x, core) for core in range(NCORES)]

    nc_moe = build_moe_launch()
    moe_keys = ("router", "g5", "u5", "d5")
    def run_moe(shards):
        in_maps = [{"x": shards[core], **{k: com[k] for k in moe_keys},
                    "cmask": _cmask(core)} for core in range(NCORES)]
        res = run_bass_kernel_spmd(nc_moe, in_maps, core_ids=list(range(NCORES)))
        return [r["y"] for r in res.results]

    shards = run_moe(shards)

    nc_at = build_l0attn_launch()
    at_keys = ("aq", "ak", "av", "aio", "aout", "alnw")
    in_maps = [{"x": shards[core], **{k: com[k] for k in at_keys}}
               for core in range(NCORES)]
    res = run_bass_kernel_spmd(nc_at, in_maps, core_ids=list(range(NCORES)))
    shards = [r["y"] for r in res.results]

    shards = run_moe(shards)

    nc_f1 = build_l1ffn_launch(first=True)
    ff_keys = ("l1_wfcT", "l1_wprT", "l1_lnw", "l1_lnb")
    in_maps = [{"x": shards[core], **{k: com[k] for k in ff_keys}}
               for core in range(NCORES)]
    res = run_bass_kernel_spmd(nc_f1, in_maps, core_ids=list(range(NCORES)))
    shards = [r["y"] for r in res.results]
    qfc_s = [r["qfc"] for r in res.results]
    qpr_s = [r["qpr"] for r in res.results]

    nc_da = build_l1attn_launch()
    da_keys = ("dq", "dk1", "dk2", "dv", "do", "dll")
    in_maps = [{"x": shards[core], **{k: com[k] for k in da_keys}}
               for core in range(NCORES)]
    res = run_bass_kernel_spmd(nc_da, in_maps, core_ids=list(range(NCORES)))
    shards = [r["y"] for r in res.results]

    nc_f2 = build_l1ffn_launch(first=False)
    in_maps = [{"x": shards[core], **{k: com[k] for k in ff_keys},
                "qfc": qfc_s[core], "qpr": qpr_s[core]}
               for core in range(NCORES)]
    res = run_bass_kernel_spmd(nc_f2, in_maps, core_ids=list(range(NCORES)))
    out = np.concatenate([r["y"].reshape(BLOC, S, D) for r in res.results], axis=0)
    return out


# version 12 (final)
# speedup vs baseline: 1.0613x; 1.0600x over previous
import numpy as np
from contextlib import ExitStack
import concourse.bass as bass
import concourse.mybir as mybir
from concourse.bass_utils import run_bass_kernel_spmd
from concourse.bass import IndirectOffsetOnAxis

F32 = mybir.dt.float32
BF16 = mybir.dt.bfloat16
U32 = mybir.dt.uint32
AF = mybir.ActivationFunctionType
ALU = mybir.AluOpType
AX = mybir.AxisListType

NCORES = 8
S = 2048
D = 1024
BLOC = 2
T = BLOC * S
DC = D // 128
NTT = T // 128
NTG = T // 512
H_FFN = 4096
HC_FFN = H_FFN // 128
EPS = 1e-5
LN_EPS = 1e-5

E_MOE = 16
K_MOE = 2
CAP_GLOBAL = 5120
CL = 1024
NSLOT = E_MOE * CL

NH = 16
HD = 64
GATE_CAP = 15.0
CHUNK = 128
NCH = S // CHUNK


class _CntView:

    def __init__(self, prog):
        self.p = prog

    def __getitem__(self, name):
        self.p.sem(name)
        return self.p._physcnt[self.p._name2phys[name]]

    def get(self, name, default=0):
        return self[name]

    def __setitem__(self, name, v):
        self.p._physcnt[self.p._name2phys[name]] = v


class Prog:
    ENGINES = ("sync", "scalar", "vector", "tensor", "gpsimd")
    NPHYS = 96

    def __init__(self, nc):
        self.nc = nc
        self.es = ExitStack()
        self.q = {e: [] for e in self.ENGINES}
        self._phys = [self.es.enter_context(nc.semaphore(f"ps{i}"))
                      for i in range(self.NPHYS)]
        self._physcnt = [0] * self.NPHYS
        self._name2phys = {"bar": 0}
        self._cursor = 1
        self.cnt = _CntView(self)

    def epoch(self):
        self._name2phys = {"bar": 0}
        self._cursor = 1

    def sem(self, name):
        if name not in self._name2phys:
            if self._cursor >= self.NPHYS:
                raise RuntimeError("out of physical semaphores in this epoch")
            self._name2phys[name] = self._cursor
            self._cursor += 1
        return self._phys[self._name2phys[name]]

    def sbuf(self, name, shape, dtype=F32, stack=None):
        return (stack or self.es).enter_context(self.nc.sbuf_tensor(name, shape, dtype))

    def psum(self, name, shape, dtype=F32, stack=None):
        return (stack or self.es).enter_context(self.nc.psum_tensor(name, shape, dtype))

    def do(self, eng, fn, inc=None, n=1):
        if inc is not None:
            s = self.sem(inc)
            idx = self._name2phys[inc]
            self._physcnt[idx] += n
            def wrapped(e, fn=fn, s=s, n=n):
                fn(e).then_inc(s, n)
            self.q[eng].append(wrapped)
            return self._physcnt[idx]
        self.q[eng].append(fn)
        return None

    def dma(self, eng, out, in_, inc):
        return self.do(eng, lambda e, o=out, i=in_: e.dma_start(out=o, in_=i),
                       inc=inc, n=16)

    def wait(self, eng, sem, val=None):
        s = self.sem(sem)
        if val is None:
            val = self.cnt[sem]
        if val <= 0:
            return
        self.q[eng].append(lambda e, s=s, v=val: e.wait_ge(s, v))

    def mark(self, sem):
        self.sem(sem)
        return self.cnt[sem]

    def barrier_epoch(self):
        self.barrier()
        self.epoch()

    def barrier(self, engines=None):
        engines = list(engines or self.ENGINES)
        for eng in engines:
            if eng in ("vector", "scalar", "tensor", "gpsimd"):
                self.q[eng].append(lambda e: e.drain())
            self.do(eng, lambda e: e.nop(), inc="bar")
        tgt = self.cnt["bar"]
        for eng in engines:
            self.wait(eng, "bar", tgt)

    def build(self):
        nc = self.nc
        with nc.Block() as block:
            for engname in self.ENGINES:
                if not self.q[engname]:
                    continue
                def body(eng, engname=engname):
                    for fn in self.q[engname]:
                        fn(eng)
                getattr(block, engname)(body)
        self.es.close()
        return nc


CONST_COL_VALS = (LN_EPS, 1e-6, 1.0, -12.0, 0.5, 2147483648.0, float(NSLOT), 1.0 / 128.0)


class Consts:
    def __init__(self, p: Prog):
        self.p = p
        self.cpool = p.sbuf("c_pool", [128, len(CONST_COL_VALS)], F32)
        self.ones_bf = p.sbuf("c_ones_bf", [128, 128], BF16)
        self.ones_f = p.sbuf("c_ones_f", [128, 128], F32)
        self.ident_bf = p.sbuf("c_id_bf", [128, 128], BF16)
        self.ident_f = p.sbuf("c_id_f", [128, 128], F32)
        self.iota_col = p.sbuf("c_iotac", [128, 1], F32)
        self.iota_row16 = p.sbuf("c_iotar16", [128, 16], F32)
        self.triu_excl_bf = p.sbuf("c_triux", [128, 128], BF16)
        self.triu_excl_f = p.sbuf("c_triuxf", [128, 128], F32)
        self.triu_incl_f = p.sbuf("c_triuif", [128, 128], F32)
        self.iota_row32 = p.sbuf("c_iotar32", [128, 32], F32)
        self.mask_sq = p.sbuf("c_masksq", [128, 128], F32)

    def col(self, v, parts=128):
        i = CONST_COL_VALS.index(v)
        return self.cpool[0:parts, i:i + 1]

    def emit(self):
        p = self.p
        es = ExitStack()
        for i, v in enumerate(CONST_COL_VALS):
            p.do("vector", lambda e, i=i, v=v:
                 e.memset(self.cpool[:, i:i + 1], float(v)), inc="cst")
        s = p.sbuf("c_scr", [128, 128], F32, stack=es)
        s2 = p.sbuf("c_scr2", [128, 128], F32, stack=es)
        p.do("gpsimd", lambda e: e.iota(self.iota_col[:, :], pattern=[[0, 1]],
                                        base=0, channel_multiplier=1,
                                        allow_small_or_imprecise_dtypes=True), inc="cst")
        p.do("gpsimd", lambda e: e.iota(self.iota_row16[:, :], pattern=[[1, 16]],
                                        base=0, channel_multiplier=0,
                                        allow_small_or_imprecise_dtypes=True), inc="cst")
        p.do("gpsimd", lambda e: e.iota(self.iota_row32[:, :], pattern=[[1, 32]],
                                        base=0, channel_multiplier=0,
                                        allow_small_or_imprecise_dtypes=True), inc="cst")
        p.do("gpsimd", lambda e: e.iota(s[:, :], pattern=[[1, 128]],
                                        base=0, channel_multiplier=0,
                                        allow_small_or_imprecise_dtypes=True), inc="cst")
        p.wait("vector", "cst")
        p.do("vector", lambda e: e.memset(self.ones_bf[:, :], 1.0), inc="cst")
        p.do("vector", lambda e: e.memset(self.ones_f[:, :], 1.0), inc="cst")
        p.do("vector", lambda e: e.tensor_scalar(self.ident_f[:, :], s[:, :],
                                                 self.iota_col[:, 0:1], None,
                                                 op0=ALU.is_equal), inc="cst")
        p.wait("vector", "cst")
        p.do("vector", lambda e: e.tensor_copy(self.ident_bf[:, :], self.ident_f[:, :]),
             inc="cst")
        p.do("vector", lambda e: e.tensor_scalar(s2[:, :], s[:, :],
                                                 self.iota_col[:, 0:1], None,
                                                 op0=ALU.is_gt), inc="cst")
        p.wait("vector", "cst")
        p.do("vector", lambda e: e.tensor_copy(self.triu_excl_bf[:, :], s2[:, :]),
             inc="cst")
        p.do("vector", lambda e: e.tensor_copy(self.triu_excl_f[:, :], s2[:, :]),
             inc="cst")
        p.wait("vector", "cst")
        p.do("vector", lambda e: e.tensor_scalar(s2[:, :], s[:, :],
                                                 self.iota_col[:, 0:1], None,
                                                 op0=ALU.is_ge), inc="cst")
        p.wait("vector", "cst")
        p.do("vector", lambda e: e.tensor_copy(self.triu_incl_f[:, :], s2[:, :]),
             inc="cst")
        p.wait("vector", "cst")
        p.do("vector", lambda e: e.tensor_scalar(s2[:, :], s[:, :],
                                                 self.iota_col[:, 0:1], None,
                                                 op0=ALU.is_lt), inc="cst")
        p.wait("vector", "cst")
        p.do("vector", lambda e: e.tensor_scalar(self.mask_sq[:, :], s2[:, :],
                                                 -1e30, None, op0=ALU.mult), inc="cst")
        p.wait("vector", "cst")
        p.barrier()
        es.close()


class Stream:
    def __init__(self, p: Prog, c: Consts, cols=T, stack=None):
        self.p = p
        self.c = c
        self.cols = cols
        Stream._n = getattr(Stream, "_n", 0) + 1
        n = Stream._n
        self.resid = [p.sbuf(f"resid{n}_{dc}", [128, cols], F32, stack=stack)
                      for dc in range(DC)]
        self.nln = 0

    def alloc_lnx(self, stack):
        self.nln += 1
        return [self.p.sbuf(f"lnx{self.nln}_{dc}", [128, T], BF16, stack=stack)
                for dc in range(DC)]


def emit_load_resid(p: Prog, c: Consts, st: Stream, x_ap, t0=0, t1=None,
                    psums=None):
    t1 = T if t1 is None else t1
    es = ExitStack()
    Stream._ld = getattr(Stream, "_ld", 0) + 1
    pre = f"ld{Stream._ld}"
    xtm = [p.sbuf(f"{pre}_xtm{i}", [128, D], F32, stack=es) for i in range(2)]
    pst = (psums if psums is not None else
           [p.psum(f"{pre}_pst{i}", [128, 128], F32, stack=es) for i in range(4)])
    tp_marks = []
    ev_marks = []
    dma_marks = []
    ntt = (t1 - t0) // 128
    for tt in range(ntt):
        buf = xtm[tt % 2]
        if tt >= 2:
            p.wait("gpsimd", f"{pre}_tp", tp_marks[(tt - 2) * DC + DC - 1])
        dma_marks.append(p.dma("gpsimd", buf[:, :],
                               x_ap[t0 + tt * 128:t0 + (tt + 1) * 128, :],
                               inc=f"{pre}_x"))
        for dc in range(DC):
            n = tt * DC + dc
            bank = n % 4
            p.wait("tensor", f"{pre}_x", dma_marks[tt])
            if n >= 4:
                p.wait("tensor", f"{pre}_ev", ev_marks[n - 4])
            tp_marks.append(p.do("tensor", lambda e, b=buf, dc=dc, bank=bank:
                 e.transpose(pst[bank][:, :],
                             b[:, dc * 128:(dc + 1) * 128], c.ident_f[:, :]),
                 inc=f"{pre}_tp"))
            p.wait("vector", f"{pre}_tp", tp_marks[-1])
            ev_marks.append(p.do("vector", lambda e, dc=dc, tt=tt, bank=bank:
                 e.tensor_copy(st.resid[dc][:, tt * 128:(tt + 1) * 128],
                               pst[bank][:, :]),
                 inc=f"{pre}_ev"))
    p.barrier()
    es.close()


def emit_store_out(p: Prog, c: Consts, st: Stream, out_ap, t0=0, t1=None,
                   psums=None):
    t1 = T if t1 is None else t1
    es = ExitStack()
    Stream._st = getattr(Stream, "_st", 0) + 1
    pre = f"st{Stream._st}"
    otm = [p.sbuf(f"{pre}_otm{i}", [128, D], F32, stack=es) for i in range(2)]
    psf = (psums if psums is not None else
           [p.psum(f"{pre}_ps{i}", [128, 128], F32, stack=es) for i in range(2)])
    out_marks = []
    ntt = (t1 - t0) // 128
    for tt in range(ntt):
        buf = otm[tt % 2]
        if tt >= 2:
            p.wait("vector", f"{pre}_out", out_marks[tt - 2])
        for dc in range(DC):
            bank = dc % 2
            p.wait("tensor", f"{pre}_st")
            p.do("tensor", lambda e, dc=dc, tt=tt, bank=bank:
                 e.transpose(psf[bank][:, :],
                             st.resid[dc][:, tt * 128:(tt + 1) * 128],
                             c.ident_f[:, :]), inc=f"{pre}_tp")
            p.wait("vector", f"{pre}_tp")
            p.do("vector", lambda e, b=buf, dc=dc, bank=bank:
                 e.tensor_copy(b[:, dc * 128:(dc + 1) * 128],
                               psf[bank][:, :]), inc=f"{pre}_st")
        p.wait("sync", f"{pre}_st")
        out_marks.append(p.dma("sync", out_ap[t0 + tt * 128:t0 + (tt + 1) * 128, :],
                               buf[:, :], inc=f"{pre}_out"))
    p.wait("sync", f"{pre}_out")
    p.barrier()
    es.close()


def emit_ln(p: Prog, c: Consts, st: Stream, lnx, rms2=False, name="ln",
            router=None, lgall=None, tg0=0, tg1=NTG, col0=0, psums=None,
            r2out=None, rcol0=None):
    es = ExitStack()
    sq = p.sbuf(f"{name}_sq", [128, 512], BF16, stack=es)
    xbf = p.sbuf(f"{name}_xbf", [128, 512], BF16, stack=es)
    rowm = p.sbuf(f"{name}_rm", [1, 512], F32, stack=es)
    rowv = p.sbuf(f"{name}_rv", [1, 512], F32, stack=es)
    rowr = p.sbuf(f"{name}_rr", [1, 512], F32, stack=es)
    rowt = p.sbuf(f"{name}_rt", [1, 512], F32, stack=es)
    rowu = p.sbuf(f"{name}_ru", [1, 512], F32, stack=es)
    tmp = p.sbuf(f"{name}_tmp", [128, 512], F32, stack=es)
    tmpf = (p.sbuf(f"{name}_tmpf", [128, 512], F32, stack=es)
            if router is not None else None)
    if psums is None:
        ps_s = p.psum(f"{name}_ps", [128, 512], F32, stack=es)
        ps_q = p.psum(f"{name}_pq", [128, 512], F32, stack=es)
        ps_br = p.psum(f"{name}_br", [128, 512], F32, stack=es)
        ps_bm = p.psum(f"{name}_bm", [128, 512], F32, stack=es)
    else:
        ps_s, ps_q, ps_br, ps_bm = psums
    ps_lg = [p.psum(f"{name}_lg{i}", [128, 16], F32, stack=es)
             for i in range(4)] if router is not None else None
    if rcol0 is None:
        rcol0 = col0
    sqmm = 0
    for g in range(tg0, tg1):
        sl = slice(g * 512, (g + 1) * 512)
        sls = slice(g * 512 - col0, (g + 1) * 512 - col0)
        rsl = slice(g * 512 - rcol0, (g + 1) * 512 - rcol0)
        for dc in range(DC):
            p.wait("vector", f"{name}_bc")
            p.wait("vector", f"{name}_mm", sqmm)
            p.do("vector", lambda e, dc=dc, rsl=rsl:
                 e.tensor_tensor(sq[:, :], st.resid[dc][:, rsl], st.resid[dc][:, rsl],
                                 ALU.mult), inc=f"{name}_sqd")
            p.do("vector", lambda e, dc=dc, rsl=rsl:
                 e.tensor_copy(xbf[:, :], st.resid[dc][:, rsl]), inc=f"{name}_sqd")
            p.wait("tensor", f"{name}_sqd")
            p.do("tensor", lambda e, dc=dc:
                 e.matmul(ps_s[0:1, :], c.ones_bf[:, 0:1], xbf[:, :],
                          start=(dc == 0), stop=(dc == DC - 1)), inc=f"{name}_mm")
            sqmm = p.do("tensor", lambda e, dc=dc:
                 e.matmul(ps_q[0:1, :], c.ones_bf[:, 0:1], sq[:, :],
                          start=(dc == 0), stop=(dc == DC - 1)), inc=f"{name}_mm")
        p.wait("vector", f"{name}_mm")
        p.do("vector", lambda e: e.tensor_scalar(rowm[:, :], ps_s[0:1, :],
                                                 1.0 / D, None, op0=ALU.mult),
             inc=f"{name}_r")
        p.do("vector", lambda e: e.tensor_scalar(rowv[:, :], ps_q[0:1, :],
                                                 1.0 / D, None, op0=ALU.mult),
             inc=f"{name}_r")
        p.wait("vector", f"{name}_r")
        p.do("vector", lambda e: e.tensor_tensor(rowt[:, :], rowm[:, :], rowm[:, :],
                                                 ALU.mult), inc=f"{name}_r")
        p.wait("vector", f"{name}_r")
        p.do("vector", lambda e: e.tensor_tensor(rowv[:, :], rowv[:, :], rowt[:, :],
                                                 ALU.subtract), inc=f"{name}_r")
        p.wait("scalar", f"{name}_r")
        p.do("scalar", lambda e: e.activation(rowt[:, :], rowv[:, :], AF.Ln,
                                              bias=c.col(LN_EPS, 1)), inc=f"{name}_a")
        p.wait("scalar", f"{name}_a")
        p.do("scalar", lambda e: e.activation(rowr[:, :], rowt[:, :], AF.Exp,
                                              scale=-0.5), inc=f"{name}_a")
        if rms2:
            p.wait("vector", f"{name}_a")
            p.do("vector", lambda e: e.tensor_scalar(rowt[:, :], rowv[:, :],
                                                     LN_EPS, None, op0=ALU.add),
                 inc=f"{name}_r")
            p.wait("vector", f"{name}_r")
            p.do("vector", lambda e: e.reciprocal(rowu[:, :], rowt[:, :]),
                 inc=f"{name}_r")
            p.wait("vector", f"{name}_r")
            p.do("vector", lambda e: e.tensor_tensor(rowu[:, :], rowu[:, :],
                                                     rowv[:, :], ALU.mult),
                 inc=f"{name}_r")
            p.wait("scalar", f"{name}_r")
            p.do("scalar", lambda e: e.activation(rowt[:, :], rowu[:, :], AF.Ln,
                                                  bias=c.col(LN_EPS, 1)), inc=f"{name}_a")
            p.wait("scalar", f"{name}_a")
            p.do("scalar", lambda e: e.activation(rowu[:, :], rowt[:, :], AF.Exp,
                                                  scale=-0.5), inc=f"{name}_a")
            p.wait("vector", f"{name}_a")
            p.do("vector", lambda e, sls=sls: e.tensor_copy(r2out[0:1, sls],
                                                            rowu[:, :]),
                 inc=f"{name}_r")
        p.wait("vector", f"{name}_a")
        p.wait("vector", f"{name}_r")
        p.do("vector", lambda e: e.tensor_tensor(rowm[:, :], rowm[:, :], rowr[:, :],
                                                 ALU.mult), inc=f"{name}_r")
        p.wait("tensor", f"{name}_r")
        p.do("tensor", lambda e: e.matmul(ps_br[:, :], c.ones_f[0:1, :], rowr[:, :],
                                          start=True, stop=True), inc=f"{name}_bcm")
        p.do("tensor", lambda e: e.matmul(ps_bm[:, :], c.ones_f[0:1, :], rowm[:, :],
                                          start=True, stop=True), inc=f"{name}_bcm")
        p.wait("vector", f"{name}_bcm")
        for dc in range(DC):
            p.do("vector", lambda e, dc=dc, rsl=rsl:
                 e.tensor_tensor(tmp[:, :], st.resid[dc][:, rsl], ps_br[:, :],
                                 ALU.mult), inc=f"{name}_bc")
            p.wait("vector", f"{name}_bc")
            if router is not None:
                p.wait("vector", f"{name}_lgm")
                p.do("vector", lambda e, dc=dc, sl=sl:
                     e.tensor_tensor(tmpf[:, :], tmp[:, :], ps_bm[:, :],
                                     ALU.subtract), inc=f"{name}_bc")
                p.wait("vector", f"{name}_bc")
                p.do("vector", lambda e, dc=dc, sls=sls:
                     e.tensor_copy(lnx[dc][:, sls], tmpf[:, :]), inc=f"{name}_bc")
                p.wait("tensor", f"{name}_bc")
                for ts4 in range(4):
                    tile = g * 4 + ts4
                    p.do("tensor", lambda e, dc=dc, ts4=ts4, tile=tile:
                         e.matmul(ps_lg[ts4][:, :],
                                  tmpf[:, ts4 * 128:(ts4 + 1) * 128],
                                  router[:, dc, :],
                                  start=(dc == 0),
                                  stop=(dc == DC - 1)),
                         inc=f"{name}_lgm")
            else:
                p.do("vector", lambda e, dc=dc, sls=sls:
                     e.tensor_tensor(lnx[dc][:, sls], tmp[:, :], ps_bm[:, :],
                                     ALU.subtract), inc=f"{name}_bc")
        if router is not None:
            p.wait("vector", f"{name}_lgm")
            for ts4 in range(4):
                tile = g * 4 + ts4
                p.do("vector", lambda e, ts4=ts4, tile=tile:
                     e.tensor_copy(lgall[:, tile * 16:(tile + 1) * 16],
                                   ps_lg[ts4][:, :]),
                     inc=f"{name}_bc")
            p.wait("tensor", f"{name}_bc")
    p.barrier()
    es.close()


def emit_quant(p: Prog, c: Consts, w_ap, q_ap, nrow, ncol, name):
    es = ExitStack()
    nrt = nrow // 128
    wt = [p.sbuf(f"{name}_w{i}", [128, ncol], F32, stack=es) for i in range(2)]
    ab = p.sbuf(f"{name}_ab", [128, ncol], F32, stack=es)
    sg = p.sbuf(f"{name}_sg", [128, ncol], F32, stack=es)
    qo = p.sbuf(f"{name}_qo", [128, ncol], BF16, stack=es)
    acc = p.sbuf(f"{name}_acc", [128, nrt], F32, stack=es)
    red = p.sbuf(f"{name}_red", [128, 1], F32, stack=es)
    thr = p.sbuf(f"{name}_thr", [128, 1], F32, stack=es)
    pst = p.psum(f"{name}_ps", [128, 512], F32, stack=es)
    a_marks = []
    for i in range(nrt):
        if i >= 2:
            p.wait("sync", f"{name}_a", a_marks[i - 2])
        m = p.dma("sync", wt[i % 2][:, :], w_ap[i * 128:(i + 1) * 128, :],
                  inc=f"{name}_l")
        p.wait("scalar", f"{name}_l", m)
        a_marks.append(p.do("scalar", lambda e, i=i:
             e.activation(ab[:, :], wt[i % 2][:, :], AF.Abs,
                          accum_out=acc[:, i:i + 1]), inc=f"{name}_a"))
    p.wait("vector", f"{name}_a")
    p.do("vector", lambda e: e.tensor_reduce(red[:, :], acc[:, 0:nrt], AX.X, ALU.add),
         inc=f"{name}_r")
    p.wait("tensor", f"{name}_r")
    p.do("tensor", lambda e: e.matmul(pst[0:1, 0:1], c.ones_f[:, 0:1], red[:, :],
                                      start=True, stop=True), inc=f"{name}_mm")
    p.wait("vector", f"{name}_mm")
    p.do("vector", lambda e: e.tensor_scalar(red[0:1, 0:1], pst[0:1, 0:1],
                                             0.5 / (nrow * ncol), 0.5e-6,
                                             op0=ALU.mult, op1=ALU.add), inc=f"{name}_r")
    p.wait("tensor", f"{name}_r")
    p.do("tensor", lambda e: e.matmul(pst[:, 2:3], c.ones_f[0:1, :], red[0:1, 0:1],
                                      start=True, stop=True), inc=f"{name}_mm")
    p.wait("vector", f"{name}_mm")
    p.do("vector", lambda e: e.tensor_copy(thr[:, :], pst[:, 2:3]), inc=f"{name}_r")
    p.wait("scalar", f"{name}_r")
    s_marks = []
    for i in range(nrt):
        if i >= 2:
            p.wait("sync", f"{name}_r2", s_marks[i - 2][0])
        m = p.dma("sync", wt[i % 2][:, :], w_ap[i * 128:(i + 1) * 128, :],
                  inc=f"{name}_l")
        p.wait("scalar", f"{name}_l", m)
        if i >= 1:
            p.wait("scalar", f"{name}_r2", s_marks[i - 1][0])
        p.do("scalar", lambda e, i=i: e.activation(ab[:, :], wt[i % 2][:, :], AF.Abs),
             inc=f"{name}_a2")
        p.do("scalar", lambda e, i=i: e.activation(sg[:, :], wt[i % 2][:, :], AF.Sign),
             inc=f"{name}_a2")
        p.wait("vector", f"{name}_a2")
        if i >= 1:
            p.wait("vector", f"{name}_sc", s_marks[i - 1][1])
        p.do("vector", lambda e: e.tensor_scalar(ab[:, :], ab[:, :], thr[:, 0:1],
                                                 None, op0=ALU.is_gt), inc=f"{name}_r2")
        p.wait("vector", f"{name}_r2")
        mr = p.do("vector", lambda e: e.tensor_tensor(qo[:, :], ab[:, :], sg[:, :],
                                                      ALU.mult), inc=f"{name}_r2")
        p.wait("gpsimd", f"{name}_r2", mr)
        src = qo[:, :].rearrange("p (mc m) -> p mc m", m=128)
        dst = q_ap[:, i, :, :].rearrange("mc p m -> p mc m")
        ms = p.dma("gpsimd", dst, src, inc=f"{name}_sc")
        s_marks.append((mr, ms))
    p.wait("gpsimd", f"{name}_sc")
    p.barrier()
    es.close()


def emit_l1ffn(p: Prog, c: Consts, st: Stream, lnx, qfc_ap, qpr_ap, lnw_ap, lnb_ap, tag,
               tg0=0, tg1=NTG, col0=0):
    es = ExitStack()
    wfc = [p.sbuf(f"{tag}_wfc{i}", [128, DC, 128], BF16, stack=es) for i in range(2)]
    wpr = [p.sbuf(f"{tag}_wpr{i}", [128, DC, 128], BF16, stack=es) for i in range(2)]
    hsb = [p.sbuf(f"{tag}_h{i}", [128, 512], F32, stack=es) for i in range(HC_FFN)]
    hbf = [p.sbuf(f"{tag}_hb{i}", [128, 512], BF16, stack=es) for i in range(2)]
    lnw = p.sbuf(f"{tag}_lnw", [128, HC_FFN], F32, stack=es)
    lnb = p.sbuf(f"{tag}_lnb", [128, HC_FFN], F32, stack=es)
    sq = p.sbuf(f"{tag}_sq", [128, 512], BF16, stack=es)
    hcast = p.sbuf(f"{tag}_hcast", [128, 512], BF16, stack=es)
    tmp = p.sbuf(f"{tag}_tmp", [128, 512], F32, stack=es)
    rowm = p.sbuf(f"{tag}_rm", [1, 512], F32, stack=es)
    rowv = p.sbuf(f"{tag}_rv", [1, 512], F32, stack=es)
    rowr = p.sbuf(f"{tag}_rr", [1, 512], F32, stack=es)
    rowt = p.sbuf(f"{tag}_rt", [1, 512], F32, stack=es)
    brc = p.sbuf(f"{tag}_brc", [128, 512], F32, stack=es)
    bmc = p.sbuf(f"{tag}_bmc", [128, 512], F32, stack=es)
    pj = [p.psum(f"{tag}_pj{i}", [128, 512], F32, stack=es) for i in range(8)]
    p.dma("sync", lnw[:, :], lnw_ap[:, :], inc=f"{tag}_lw")
    p.dma("sync", lnb[:, :], lnb_ap[:, :], inc=f"{tag}_lw")
    p.wait("vector", f"{tag}_lw")

    fcmm_marks = {}
    ge_marks = {}
    for g in range(tg0, tg1):
        sl = slice(g * 512 - col0, (g + 1) * 512 - col0)
        p.barrier()
        fcw_marks = [0, 0]
        ge_marks = [0, 0]
        for hc in range(HC_FFN):
            wb = wfc[hc % 2]
            p.wait("sync", f"{tag}_fcmm", fcw_marks[hc % 2])
            wlm = p.dma("sync", wb[:, :, :].rearrange("p kc m -> p kc m"),
                  qfc_ap[hc, :, :, :].rearrange("kc p m -> p kc m"),
                  inc=f"{tag}_wl{hc % 2}")
            p.wait("tensor", f"{tag}_wl{hc % 2}", wlm)
            p.wait("tensor", f"{tag}_ge", ge_marks[hc % 2])
            pb = pj[hc % 2]
            for kc in range(DC):
                fcw_marks[hc % 2] = p.do("tensor",
                     lambda e, wb=wb, kc=kc, sl=sl, pb=pb:
                     e.matmul(pb[:, :], wb[:, kc, :], lnx[kc][:, sl],
                              start=(kc == 0), stop=(kc == DC - 1)),
                     inc=f"{tag}_fcmm")
            p.wait("scalar", f"{tag}_fcmm", fcw_marks[hc % 2])
            ge_marks[hc % 2] = p.do("scalar", lambda e, hc=hc, pb=pb:
                 e.activation(hsb[hc][:, :], pb[:, :], AF.Gelu), inc=f"{tag}_ge")
            p.wait("vector", f"{tag}_ge")
            p.wait("vector", f"{tag}_st", p.cnt.get(f"{tag}_st", 0))
            p.do("vector", lambda e, hc=hc:
                 e.tensor_tensor(sq[:, :], hsb[hc][:, :], hsb[hc][:, :], ALU.mult),
                 inc=f"{tag}_sqd")
            p.do("vector", lambda e, hc=hc:
                 e.tensor_copy(hcast[:, :], hsb[hc][:, :]), inc=f"{tag}_sqd")
            p.wait("tensor", f"{tag}_sqd")
            p.do("tensor", lambda e, hc=hc:
                 e.matmul(pj[2][0:1, :], c.ones_bf[:, 0:1], hcast[:, :],
                          start=(hc == 0), stop=(hc == HC_FFN - 1)), inc=f"{tag}_st")
            p.do("tensor", lambda e, hc=hc:
                 e.matmul(pj[3][0:1, :], c.ones_bf[:, 0:1], sq[:, :],
                          start=(hc == 0), stop=(hc == HC_FFN - 1)), inc=f"{tag}_st")
        p.barrier()
        p.wait("vector", f"{tag}_st")
        p.do("vector", lambda e: e.tensor_scalar(rowm[:, :], pj[2][0:1, :],
                                                 1.0 / H_FFN, None, op0=ALU.mult),
             inc=f"{tag}_r")
        p.do("vector", lambda e: e.tensor_scalar(rowv[:, :], pj[3][0:1, :],
                                                 1.0 / H_FFN, None, op0=ALU.mult),
             inc=f"{tag}_r")
        p.wait("vector", f"{tag}_r")
        p.do("vector", lambda e: e.tensor_tensor(rowt[:, :], rowm[:, :], rowm[:, :],
                                                 ALU.mult), inc=f"{tag}_r")
        p.wait("vector", f"{tag}_r")
        p.do("vector", lambda e: e.tensor_tensor(rowv[:, :], rowv[:, :], rowt[:, :],
                                                 ALU.subtract), inc=f"{tag}_r")
        p.wait("scalar", f"{tag}_r")
        p.do("scalar", lambda e: e.activation(rowt[:, :], rowv[:, :], AF.Ln,
                                              bias=c.col(LN_EPS, 1)), inc=f"{tag}_a")
        p.wait("scalar", f"{tag}_a")
        p.do("scalar", lambda e: e.activation(rowr[:, :], rowt[:, :], AF.Exp,
                                              scale=-0.5), inc=f"{tag}_a")
        p.wait("vector", f"{tag}_a")
        p.do("vector", lambda e: e.tensor_tensor(rowm[:, :], rowm[:, :], rowr[:, :],
                                                 ALU.mult), inc=f"{tag}_r")
        p.wait("tensor", f"{tag}_r")
        p.do("tensor", lambda e: e.matmul(pj[2][:, :], c.ones_f[0:1, :], rowr[:, :],
                                          start=True, stop=True), inc=f"{tag}_bcm")
        p.wait("vector", f"{tag}_bcm")
        p.do("vector", lambda e: e.tensor_copy(brc[:, :], pj[2][:, :]), inc=f"{tag}_r")
        p.wait("tensor", f"{tag}_r")
        p.do("tensor", lambda e: e.matmul(pj[2][:, :], c.ones_f[0:1, :], rowm[:, :],
                                          start=True, stop=True), inc=f"{tag}_bcm")
        p.wait("vector", f"{tag}_bcm")
        p.do("vector", lambda e: e.tensor_copy(bmc[:, :], pj[2][:, :]), inc=f"{tag}_r")
        p.wait("vector", f"{tag}_r")
        for hc in range(HC_FFN):
            p.do("vector", lambda e, hc=hc:
                 e.tensor_tensor(tmp[:, :], hsb[hc][:, :], brc[:, :], ALU.mult),
                 inc=f"{tag}_ap")
            p.wait("vector", f"{tag}_ap")
            p.do("vector", lambda e, hc=hc:
                 e.tensor_tensor(tmp[:, :], tmp[:, :], bmc[:, :], ALU.subtract),
                 inc=f"{tag}_ap")
            p.wait("vector", f"{tag}_ap")
            p.do("vector", lambda e, hc=hc:
                 e.tensor_scalar(hsb[hc][:, :], tmp[:, :], lnw[:, hc:hc + 1],
                                 lnb[:, hc:hc + 1], op0=ALU.mult, op1=ALU.add),
                 inc=f"{tag}_ap")
            p.wait("vector", f"{tag}_ap")
            p.wait("scalar", f"{tag}_ap")
        p.barrier()
        p.wait("tensor", f"{tag}_ap")
        hb_marks = [0, 0]
        for hc in range(HC_FFN):
            wb = wpr[hc % 2]
            p.wait("sync", f"{tag}_pjmm", hb_marks[hc % 2])
            wpm = p.dma("sync", wb[:, :, :],
                  qpr_ap[:, hc, :, :].rearrange("dc p m -> p dc m"),
                  inc=f"{tag}_wp{hc % 2}")
            hb = hbf[hc % 2]
            p.wait("vector", f"{tag}_pjmm", hb_marks[hc % 2])
            hbm = p.do("vector", lambda e, hb=hb, hc=hc:
                 e.tensor_copy(hb[:, :], hsb[hc][:, :]), inc=f"{tag}_hbc")
            p.wait("tensor", f"{tag}_wp{hc % 2}", wpm)
            p.wait("tensor", f"{tag}_hbc", hbm)
            for dc in range(DC):
                hb_marks[hc % 2] = p.do("tensor", lambda e, wb=wb, dc=dc, hc=hc, hb=hb:
                     e.matmul(pj[dc][:, :], wb[:, dc, :], hb[:, :],
                              start=(hc == 0), stop=(hc == HC_FFN - 1)),
                     inc=f"{tag}_pjmm")
        p.wait("vector", f"{tag}_pjmm")
        for dc in range(DC):
            p.do("vector", lambda e, dc=dc, sl=sl:
                 e.scalar_tensor_tensor(st.resid[dc][:, sl], pj[dc][:, :], 0.5,
                                        st.resid[dc][:, sl], op0=ALU.mult, op1=ALU.add),
                 inc=f"{tag}_ev")
        p.wait("tensor", f"{tag}_ev")
    p.barrier()
    es.close()




def emit_l0ffn_route(p: Prog, c: Consts, lnx, lgall, aps, tag,
                     srcst, wst):
    CAPF = float(CAP_GLOBAL)
    CLF = float(CL)
    es = ExitStack()
    ohcat = p.sbuf(f"{tag}_oh", [128, NTT * 32], F32, stack=es)
    Ast = p.sbuf(f"{tag}_A", [128, NTT * 16], F32, stack=es)
    i12f = p.sbuf(f"{tag}_i12", [128, NTT * 2], F32, stack=es)
    keepst = p.sbuf(f"{tag}_kp", [128, NTT * 2], F32, stack=es)
    destst = p.sbuf(f"{tag}_dst", [128, NTT * 2], U32, stack=es)
    ix8 = p.sbuf(f"{tag}_ix8", [128, 8], U32, stack=es)
    junk = p.sbuf(f"{tag}_junk", [128, 16], F32, stack=es)
    scr = p.sbuf(f"{tag}_scr", [128, 16], F32, stack=es)
    col = p.sbuf(f"{tag}_col", [128, 12], F32, stack=es)
    sm32 = p.sbuf(f"{tag}_s32", [32, 32], F32, stack=es)
    sm32b = p.sbuf(f"{tag}_s32b", [32, 32], F32, stack=es)
    sm32c = p.sbuf(f"{tag}_s32c", [32, 32], F32, stack=es)
    trsc = p.sbuf(f"{tag}_tr", [32, 32], F32, stack=es)
    trsc2 = p.sbuf(f"{tag}_tr2", [32, 16], F32, stack=es)
    corow = p.sbuf(f"{tag}_cor", [1, 32], F32, stack=es)
    ccsb = p.sbuf(f"{tag}_ccs", [8, 32], F32, stack=es)
    cmsk = p.sbuf(f"{tag}_cm", [8, 1], F32, stack=es)
    eselb = [p.sbuf(f"{tag}_es{i}", [128, 32], F32, stack=es) for i in range(2)]
    eib = [p.sbuf(f"{tag}_ei{i}", [32, 128], F32, stack=es) for i in range(2)]
    p.dma("sync", cmsk[:, :], aps["cmask"][:, :], inc=f"{tag}_cml")
    rbc = p.nc.gpsimd.alloc_register(name=f"{tag}_rbc")
    p.do("gpsimd", lambda e: e.reg_mov(rbc, NSLOT - 1))

    for i in range(NTT):
        lg = lgall[:, i * 16:(i + 1) * 16]
        mx8 = scr[:, 0:8]
        p.wait("vector", f"{tag}_p1s")
        p.do("vector", lambda e, lg=lg, mx8=mx8: e.max(out=mx8, in_=lg),
             inc=f"{tag}_p1v")
        p.wait("vector", f"{tag}_p1v")
        p.do("vector", lambda e, lg=lg, mx8=mx8:
             e.max_index(out=ix8[:, :], in_max=mx8, in_values=lg), inc=f"{tag}_p1v")
        p.wait("vector", f"{tag}_p1v")
        p.do("vector", lambda e, i=i:
             e.tensor_copy(i12f[:, i * 2:i * 2 + 2], ix8[:, 0:2]), inc=f"{tag}_p1v")
        p.do("vector", lambda e: e.tensor_scalar(col[:, 0:1], scr[:, 0:1], -1.0,
                                                 None, op0=ALU.mult), inc=f"{tag}_p1v")
        p.wait("scalar", f"{tag}_p1v")
        p.do("scalar", lambda e, lg=lg:
             e.activation(junk[:, :], lg, AF.Exp, bias=col[:, 0:1],
                          accum_out=col[:, 1:2]), inc=f"{tag}_p1s")
        p.wait("vector", f"{tag}_p1s")
        p.do("vector", lambda e: e.reciprocal(col[:, 2:3], col[:, 1:2]),
             inc=f"{tag}_p1v")
        p.do("vector", lambda e, mx8=mx8:
             e.tensor_tensor(col[:, 3:4], mx8[:, 1:2], mx8[:, 0:1], ALU.subtract),
             inc=f"{tag}_p1v")
        p.wait("scalar", f"{tag}_p1v")
        p.do("scalar", lambda e: e.activation(col[:, 4:5], col[:, 3:4], AF.Exp),
             inc=f"{tag}_p1s")
        p.wait("vector", f"{tag}_p1s")
        p.do("vector", lambda e: e.tensor_tensor(col[:, 5:6], col[:, 4:5],
                                                 col[:, 2:3], ALU.mult),
             inc=f"{tag}_p1v")
        p.wait("vector", f"{tag}_p1v")
        p.do("vector", lambda e, i=i: e.tensor_scalar(wst[:, i * 2:i * 2 + 1],
             col[:, 2:3], 0.5, None, op0=ALU.mult), inc=f"{tag}_p1v")
        p.do("vector", lambda e, i=i: e.tensor_scalar(wst[:, i * 2 + 1:i * 2 + 2],
             col[:, 5:6], 0.5, None, op0=ALU.mult), inc=f"{tag}_p1v")
        p.do("vector", lambda e, i=i: e.tensor_scalar(ohcat[:, i * 32:i * 32 + 16],
             c.iota_row16[:, :], i12f[:, i * 2:i * 2 + 1], None, op0=ALU.is_equal),
             inc=f"{tag}_p1v")
        p.do("vector", lambda e, i=i: e.tensor_scalar(ohcat[:, i * 32 + 16:i * 32 + 32],
             c.iota_row16[:, :], i12f[:, i * 2 + 1:i * 2 + 2], None, op0=ALU.is_equal),
             inc=f"{tag}_p1v")
        p.wait("vector", f"{tag}_p1v")
    p.barrier()

    ps_t = p.psum(f"{tag}_pt", [32, 32], F32, stack=es)
    esel_marks = [0, 0]
    for i in range(NTT):
        eb = eselb[i % 2]
        p.wait("vector", f"{tag}_p2m", esel_marks[i % 2])
        p.do("vector", lambda e, i=i, eb=eb:
             e.tensor_scalar(eb[:, :], c.iota_row32[:, :], float(i), None,
                             op0=ALU.is_equal), inc=f"{tag}_p2v")
        p.wait("tensor", f"{tag}_p2v")
        esel_marks[i % 2] = p.do("tensor", lambda e, i=i, eb=eb:
             e.matmul(ps_t[:, :], eb[:, :], ohcat[:, i * 32:(i + 1) * 32],
                      start=(i == 0), stop=(i == NTT - 1)), inc=f"{tag}_p2m")
    p.wait("vector", f"{tag}_p2m")
    p.do("vector", lambda e: e.tensor_copy(sm32[:, :], ps_t[:, :]), inc=f"{tag}_p2v")
    p.wait("vector", f"{tag}_p2v")
    p.do("vector", lambda e: e.transpose(sm32b[:, :], sm32[:, :]), inc=f"{tag}_p2v")
    p.wait("vector", f"{tag}_p2v")
    p.do("vector", lambda e: e.tensor_tensor_scan(sm32c[:, :], c.ones_f[0:32, 0:32],
         sm32b[:, :], 0.0, ALU.mult, ALU.add), inc=f"{tag}_p2v")
    p.wait("vector", f"{tag}_p2v")
    p.do("vector", lambda e: e.tensor_tensor(sm32b[:, :], sm32c[:, :], sm32b[:, :],
                                             ALU.subtract), inc=f"{tag}_p2v")
    p.wait("vector", f"{tag}_p2v")
    p.do("vector", lambda e: e.transpose(trsc[:, :], sm32b[:, :]), inc=f"{tag}_p2v")
    p.wait("vector", f"{tag}_p2v")
    p.do("vector", lambda e: e.transpose(sm32[:, :], sm32c[:, :]), inc=f"{tag}_p2v")
    p.wait("gpsimd", f"{tag}_p2v")
    p.dma("gpsimd", aps["cc_in"][:, :], sm32[31:32, :], inc=f"{tag}_cc")
    p.wait("gpsimd", f"{tag}_cc")
    p.do("gpsimd", lambda e: e.collective_compute(
        "AllGather", ALU.bypass, replica_groups=[list(range(NCORES))],
        ins=[aps["cc_in"][:, :].opt()], outs=[aps["cc_out"][:, :].opt()]),
        inc=f"{tag}_cc2")
    p.wait("gpsimd", f"{tag}_cc2", 1)
    p.dma("gpsimd", ccsb[:, :], aps["cc_out"][:, :], inc=f"{tag}_cc")
    p.wait("tensor", f"{tag}_cc")
    p.wait("tensor", f"{tag}_cml")
    p.do("tensor", lambda e: e.matmul(ps_t[0:1, :], cmsk[:, 0:1],
                                      ccsb[:, :], start=True, stop=True),
         inc=f"{tag}_p2m")
    p.wait("vector", f"{tag}_p2m")
    p.do("vector", lambda e: e.tensor_copy(corow[:, :], ps_t[0:1, :]), inc=f"{tag}_p2v")
    p.wait("vector", f"{tag}_p2v")
    p.barrier()

    ps_r = [p.psum(f"{tag}_pr{i}", [128, 32], F32, stack=es) for i in range(2)]
    rk_marks = [0, 0]
    ei_marks = [0, 0]
    for i in range(NTT):
        pb = ps_r[i % 2]
        eb = eib[i % 2]
        p.wait("vector", f"{tag}_p3m", ei_marks[i % 2])
        p.do("vector", lambda e, i=i, eb=eb:
             e.tensor_scalar(eb[:, :], c.iota_col[0:32, 0:1].to_broadcast([32, 128]),
                             float(i), None, op0=ALU.is_equal), inc=f"{tag}_p3v")
        p.wait("tensor", f"{tag}_p3v")
        p.wait("tensor", f"{tag}_p3r", rk_marks[i % 2])
        p.do("tensor", lambda e, i=i, pb=pb:
             e.matmul(pb[:, :], c.triu_excl_f[:, :], ohcat[:, i * 32:(i + 1) * 32],
                      start=True, stop=False), inc=f"{tag}_p3m")
        ei_marks[i % 2] = p.do("tensor", lambda e, pb=pb, eb=eb:
             e.matmul(pb[:, :], eb[:, :], trsc[:, :], start=False, stop=False),
             inc=f"{tag}_p3m")
        mm = p.do("tensor", lambda e, pb=pb:
             e.matmul(pb[:, :], c.ones_f[0:1, :], corow[:, :], start=False,
                      stop=True), inc=f"{tag}_p3m")
        p.wait("vector", f"{tag}_p3m", mm)
        for k in range(2):
            p.do("vector", lambda e, i=i, k=k, pb=pb:
                 e.tensor_tensor(scr[:, :], pb[:, k * 16:(k + 1) * 16],
                                 ohcat[:, i * 32 + k * 16:i * 32 + (k + 1) * 16],
                                 ALU.mult), inc=f"{tag}_p3v")
            p.wait("vector", f"{tag}_p3v")
            p.do("vector", lambda e:
                 e.tensor_reduce(col[:, 6:7], scr[:, :], AX.X, ALU.add),
                 inc=f"{tag}_p3v")
            p.wait("vector", f"{tag}_p3v")
            p.do("vector", lambda e, i=i, k=k:
                 e.tensor_scalar(keepst[:, i * 2 + k:i * 2 + k + 1], col[:, 6:7],
                                 CAPF, None, op0=ALU.is_lt), inc=f"{tag}_p3v")
        p.wait("vector", f"{tag}_p3v")
        p.do("vector", lambda e, i=i:
             e.tensor_scalar(Ast[:, i * 16:(i + 1) * 16],
                             ohcat[:, i * 32:i * 32 + 16],
                             keepst[:, i * 2:i * 2 + 1], None, op0=ALU.mult),
             inc=f"{tag}_p3r")
        p.wait("vector", f"{tag}_p3r")
        rk_marks[i % 2] = p.do("vector", lambda e, i=i:
             e.scalar_tensor_tensor(Ast[:, i * 16:(i + 1) * 16],
                                    ohcat[:, i * 32 + 16:i * 32 + 32],
                                    keepst[:, i * 2 + 1:i * 2 + 2],
                                    Ast[:, i * 16:(i + 1) * 16],
                                    op0=ALU.mult, op1=ALU.add), inc=f"{tag}_p3r")
    p.barrier()

    esel_marks = [0, 0]
    for i in range(NTT):
        eb = eselb[i % 2]
        p.wait("vector", f"{tag}_p4m", esel_marks[i % 2])
        p.do("vector", lambda e, i=i, eb=eb:
             e.tensor_scalar(eb[:, :], c.iota_row32[:, :], float(i), None,
                             op0=ALU.is_equal), inc=f"{tag}_p4v")
        p.wait("tensor", f"{tag}_p4v")
        esel_marks[i % 2] = p.do("tensor", lambda e, i=i, eb=eb:
             e.matmul(ps_t[:, 0:16], eb[:, :], Ast[:, i * 16:(i + 1) * 16],
                      start=(i == 0), stop=(i == NTT - 1)), inc=f"{tag}_p4m")
    p.wait("vector", f"{tag}_p4m")
    p.do("vector", lambda e: e.tensor_copy(sm32[:, 0:16], ps_t[:, 0:16]),
         inc=f"{tag}_p4v")
    p.wait("vector", f"{tag}_p4v")
    p.do("vector", lambda e: e.transpose(sm32b[:, :], sm32[:, :]), inc=f"{tag}_p4v")
    p.wait("vector", f"{tag}_p4v")
    p.do("vector", lambda e: e.tensor_tensor_scan(sm32c[0:16, :], c.ones_f[0:16, 0:32],
         sm32b[0:16, :], 0.0, ALU.mult, ALU.add), inc=f"{tag}_p4v")
    p.wait("vector", f"{tag}_p4v")
    p.do("vector", lambda e: e.tensor_tensor(sm32b[0:16, :], sm32c[0:16, :],
                                             sm32b[0:16, :], ALU.subtract),
         inc=f"{tag}_p4v")
    p.wait("vector", f"{tag}_p4v")
    p.do("vector", lambda e: e.transpose(sm32[:, :], sm32b[:, :]), inc=f"{tag}_p4v")
    p.wait("vector", f"{tag}_p4v")
    p.do("vector", lambda e: e.tensor_copy(trsc2[:, :], sm32[:, 0:16]),
         inc=f"{tag}_p4v")
    p.wait("vector", f"{tag}_p4v")
    p.barrier()

    ps_a = [p.psum(f"{tag}_pa{i}", [128, 16], F32, stack=es) for i in range(2)]
    ps_x = [p.psum(f"{tag}_px{i}", [128, 128], BF16, stack=es) for i in range(2)]
    xtm = [p.sbuf(f"{tag}_xtm{i}", [128, D], BF16, stack=es) for i in range(2)]
    sc_marks = [0, 0]
    a_marks = [0, 0]
    ei_marks = [0, 0]
    for i in range(NTT):
        pb = ps_a[i % 2]
        eb = eib[i % 2]
        p.wait("vector", f"{tag}_p5m", ei_marks[i % 2])
        p.do("vector", lambda e, i=i, eb=eb:
             e.tensor_scalar(eb[:, :], c.iota_col[0:32, 0:1].to_broadcast([32, 128]),
                             float(i), None, op0=ALU.is_equal), inc=f"{tag}_p5v")
        p.wait("tensor", f"{tag}_p5v")
        p.wait("tensor", f"{tag}_p5r", a_marks[i % 2])
        p.do("tensor", lambda e, i=i, pb=pb:
             e.matmul(pb[:, :], c.triu_excl_f[:, :], Ast[:, i * 16:(i + 1) * 16],
                      start=True, stop=False), inc=f"{tag}_p5m")
        ei_marks[i % 2] = p.do("tensor", lambda e, pb=pb, eb=eb:
             e.matmul(pb[:, :], eb[:, :], trsc2[:, :], start=False, stop=True),
             inc=f"{tag}_p5m")
        p.wait("vector", f"{tag}_p5m", ei_marks[i % 2])
        for k in range(2):
            p.do("vector", lambda e, i=i, k=k, pb=pb:
                 e.tensor_tensor(scr[:, :], pb[:, :],
                                 ohcat[:, i * 32 + k * 16:i * 32 + (k + 1) * 16],
                                 ALU.mult), inc=f"{tag}_p5v")
            p.wait("vector", f"{tag}_p5v")
            a_marks[i % 2] = p.do("vector", lambda e:
                 e.tensor_reduce(col[:, 7:8], scr[:, :], AX.X, ALU.add),
                 inc=f"{tag}_p5r")
            p.wait("vector", f"{tag}_p5r")
            p.do("vector", lambda e, i=i, k=k:
                 e.scalar_tensor_tensor(col[:, 8:9], i12f[:, i * 2 + k:i * 2 + k + 1],
                                        CLF, col[:, 7:8], op0=ALU.mult, op1=ALU.add),
                 inc=f"{tag}_p5v")
            p.do("vector", lambda e:
                 e.tensor_scalar(col[:, 9:10], col[:, 7:8], CLF, None,
                                 op0=ALU.is_lt), inc=f"{tag}_p5v")
            p.wait("vector", f"{tag}_p5v")
            p.do("vector", lambda e, i=i, k=k:
                 e.tensor_tensor(col[:, 10:11], col[:, 9:10],
                                 keepst[:, i * 2 + k:i * 2 + k + 1], ALU.mult),
                 inc=f"{tag}_p5v")
            p.wait("vector", f"{tag}_p5v")
            p.do("vector", lambda e:
                 e.tensor_tensor(col[:, 11:12], col[:, 8:9], col[:, 10:11],
                                 ALU.mult), inc=f"{tag}_p5v")
            BIGF = 2147483648.0
            p.do("vector", lambda e, BIGF=BIGF:
                 e.tensor_scalar(col[:, 6:7], col[:, 10:11], -BIGF, BIGF,
                                 op0=ALU.mult, op1=ALU.add), inc=f"{tag}_p5v")
            p.do("vector", lambda e:
                 e.tensor_scalar(col[:, 5:6], col[:, 10:11], -float(NSLOT),
                                 float(NSLOT), op0=ALU.mult, op1=ALU.add),
                 inc=f"{tag}_p5v")
            p.wait("vector", f"{tag}_p5v")
            p.do("vector", lambda e:
                 e.tensor_tensor(col[:, 4:5], col[:, 11:12], col[:, 6:7], ALU.add),
                 inc=f"{tag}_p5v")
            p.do("vector", lambda e:
                 e.tensor_tensor(col[:, 3:4], col[:, 11:12], col[:, 5:6], ALU.add),
                 inc=f"{tag}_p5v")
            p.wait("vector", f"{tag}_p5v")
            p.do("vector", lambda e, i=i, k=k:
                 e.tensor_copy(destst[:, i * 2 + k:i * 2 + k + 1], col[:, 4:5]),
                 inc=f"{tag}_p5v")
            p.do("vector", lambda e, i=i, k=k:
                 e.tensor_copy(srcst[:, i * 2 + k:i * 2 + k + 1], col[:, 3:4]),
                 inc=f"{tag}_p5v")
        buf = xtm[i % 2]
        p.wait("vector", f"{tag}_sc", sc_marks[i % 2])
        for dc in range(DC):
            pxb = ps_x[dc % 2]
            p.wait("tensor", f"{tag}_p5e")
            p.do("tensor", lambda e, i=i, dc=dc, pxb=pxb:
                 e.transpose(pxb[:, :], lnx[dc][:, i * 128:(i + 1) * 128],
                             c.ident_bf[:, :]), inc=f"{tag}_p5t")
            p.wait("vector", f"{tag}_p5t")
            p.do("vector", lambda e, dc=dc, pxb=pxb, buf=buf:
                 e.tensor_copy(buf[:, dc * 128:(dc + 1) * 128], pxb[:, :]),
                 inc=f"{tag}_p5e")
        p.wait("gpsimd", f"{tag}_p5e")
        p.wait("gpsimd", f"{tag}_p5v")
        p.wait("gpsimd", f"{tag}_sc", sc_marks[i % 2])
        for k in range(2):
            sc_marks[i % 2] = p.do("gpsimd", lambda e, i=i, k=k, buf=buf:
                 e.indirect_dma_start(
                     out=aps["xe"][:, :],
                     out_offset=IndirectOffsetOnAxis(
                         ap=destst[:, i * 2 + k:i * 2 + k + 1], axis=0),
                     in_=buf[:, :], in_offset=None,
                     bounds_check=rbc, oob_is_err=False),
                 inc=f"{tag}_sc", n=16)
    p.wait("gpsimd", f"{tag}_sc")
    p.barrier()
    es.close()


def emit_l0ffn_exec(p: Prog, c: Consts, aps, x_ap, y_ap, tag, srcst, wst):
    es2 = ExitStack()
    xeT = [[p.sbuf(f"{tag}_xeT{j}_{dc}", [128, CL], BF16, stack=es2)
            for dc in range(DC)] for j in range(2)]
    wg = [p.sbuf(f"{tag}_wg{j}", [128, DC, DC, 128], BF16, stack=es2) for j in range(2)]
    wu = [p.sbuf(f"{tag}_wu{j}", [128, DC, DC, 128], BF16, stack=es2) for j in range(2)]
    wd = [p.sbuf(f"{tag}_wd{j}", [128, DC, 1024], BF16, stack=es2) for j in range(2)]
    hT = [p.sbuf(f"{tag}_hT{i}", [128, 512], BF16, stack=es2) for i in range(DC)]
    sil = [p.sbuf(f"{tag}_sil{i}", [128, 512], F32, stack=es2) for i in range(2)]
    oub = [p.sbuf(f"{tag}_ou{i}", [128, D], BF16, stack=es2) for i in range(2)]
    zr = p.sbuf(f"{tag}_zr", [1, D], BF16, stack=es2)
    ps_g = [p.psum(f"{tag}_pg{i}", [128, 512], F32, stack=es2) for i in range(2)]
    ps_u = [p.psum(f"{tag}_pu{i}", [128, 512], F32, stack=es2) for i in range(2)]
    ps_d = [p.psum(f"{tag}_pd{i}", [128, 512], F32, stack=es2) for i in range(4)]
    p.do("vector", lambda e: e.memset(zr[:, :], 0.0), inc=f"{tag}_ez")
    p.wait("gpsimd", f"{tag}_ez")
    p.dma("gpsimd", aps["outsc"][NSLOT:NSLOT + 1, :], zr[:, :], inc=f"{tag}_ez2")
    ou_marks = [0, 0]
    mm_done = [0, 0]
    ld_marks = [0, 0]
    wl_marks = [0, 0]
    gu_marks = [[0, 0], [0, 0]]
    dn_marks = [0, 0, 0, 0]

    def load_expert(ex):
        j = ex % 2
        p.wait("sync", f"{tag}_emm", mm_done[j])
        lm = 0
        for dc in range(DC):
            lm = p.do("sync", lambda e, ex=ex, dc=dc, j=j:
                 e.dma_start_transpose(
                     out=xeT[j][dc][:, :],
                     in_=aps["xe"][ex * CL:(ex + 1) * CL, dc * 128:(dc + 1) * 128]),
                 inc=f"{tag}_el", n=16)
        p.wait("gpsimd", f"{tag}_emm", mm_done[j])
        p.dma("gpsimd", wg[j][:, :, :, :], aps["g5"][ex], inc=f"{tag}_ew")
        p.dma("gpsimd", wu[j][:, :, :, :], aps["u5"][ex], inc=f"{tag}_ew")
        wm = p.dma("gpsimd", wd[j][:, :, :], aps["d5"][ex], inc=f"{tag}_ew")
        ld_marks[j] = lm
        wl_marks[j] = wm

    load_expert(0)
    for ex in range(E_MOE):
        j = ex % 2
        if ex + 1 < E_MOE:
            load_expert(ex + 1)
        p.wait("tensor", f"{tag}_el", ld_marks[j])
        p.wait("tensor", f"{tag}_ew", wl_marks[j])
        for sg in range(CL // 512):
            for hc in range(DC):
                hb = hc % 2
                pg, pu = ps_g[hb], ps_u[hb]
                p.wait("tensor", f"{tag}_ev1", gu_marks[hb][0])
                for kc in range(DC):
                    p.do("tensor", lambda e, kc=kc, hc=hc, sg=sg, j=j, pg=pg:
                         e.matmul(pg[:, :], wg[j][:, kc, hc, :],
                                  xeT[j][kc][:, sg * 512:(sg + 1) * 512],
                                  start=(kc == 0), stop=(kc == DC - 1)),
                         inc=f"{tag}_emm")
                for kc in range(DC):
                    mmk = p.do("tensor", lambda e, kc=kc, hc=hc, sg=sg, j=j, pu=pu:
                         e.matmul(pu[:, :], wu[j][:, kc, hc, :],
                                  xeT[j][kc][:, sg * 512:(sg + 1) * 512],
                                  start=(kc == 0), stop=(kc == DC - 1)),
                         inc=f"{tag}_emm")
                p.wait("scalar", f"{tag}_emm", mmk)
                p.wait("scalar", f"{tag}_ev1", gu_marks[hb][1])
                p.do("scalar", lambda e, hb=hb, pg=pg:
                     e.activation(sil[hb][:, :], pg[:, :], AF.Silu), inc=f"{tag}_ea")
                p.wait("vector", f"{tag}_ea")
                m1 = p.do("vector", lambda e, hc=hc, hb=hb, pu=pu:
                     e.tensor_tensor(hT[hc][:, :], sil[hb][:, :], pu[:, :], ALU.mult),
                     inc=f"{tag}_ev1")
                gu_marks[hb] = [m1, m1]
            p.wait("tensor", f"{tag}_ev1")
            for st4 in range(4):
                ob = oub[st4 % 2]
                p.wait("vector", f"{tag}_eo", ou_marks[st4 % 2])
                for dg in range(2):
                    bi = (st4 * 2 + dg) % 4
                    pd = ps_d[bi]
                    p.wait("tensor", f"{tag}_ev2", dn_marks[bi])
                    for hc in range(DC):
                        mmk = p.do("tensor", lambda e, hc=hc, st4=st4, dg=dg, j=j, pd=pd:
                             e.matmul(pd[:, :], hT[hc][:, st4 * 128:(st4 + 1) * 128],
                                      wd[j][:, hc, dg * 512:(dg + 1) * 512],
                                      start=(hc == 0), stop=(hc == DC - 1)),
                             inc=f"{tag}_emm")
                    p.wait("vector", f"{tag}_emm", mmk)
                    dn_marks[bi] = p.do("vector", lambda e, dg=dg, pd=pd, ob=ob:
                         e.tensor_copy(ob[:, dg * 512:(dg + 1) * 512], pd[:, :]),
                         inc=f"{tag}_ev2")
                p.wait("gpsimd", f"{tag}_ev2", dn_marks[(st4 * 2 + 1) % 4])
                row0 = ex * CL + sg * 512 + st4 * 128
                ou_marks[st4 % 2] = p.dma("gpsimd",
                     aps["outsc"][row0:row0 + 128, :], ob[:, :], inc=f"{tag}_eo")
        mm_done[j] = p.mark(f"{tag}_emm")
    p.wait("gpsimd", f"{tag}_eo")
    p.wait("gpsimd", f"{tag}_ez2")
    p.barrier()
    es2.close()

    es3 = ExitStack()
    gth = [p.sbuf(f"{tag}_g{i}", [128, D], BF16, stack=es3) for i in range(2)]
    cmb = p.sbuf(f"{tag}_cmb", [128, D], F32, stack=es3)
    xin = [p.sbuf(f"{tag}_xi{i}", [128, D], F32, stack=es3) for i in range(2)]
    ybf = [p.sbuf(f"{tag}_yb{i}", [128, D], F32, stack=es3) for i in range(2)]
    rbc2 = p.nc.gpsimd.alloc_register(name=f"{tag}_rbc2")
    p.do("gpsimd", lambda e: e.reg_mov(rbc2, NSLOT))
    y_marks = [0, 0]
    x_marks = [0, 0]
    for i in range(NTT):
        p.wait("gpsimd", f"{tag}_cv")
        gm = 0
        for k in range(2):
            gm = p.do("gpsimd", lambda e, i=i, k=k:
                 e.indirect_dma_start(
                     out=gth[k][:, :], out_offset=None,
                     in_=aps["outsc"][:, :],
                     in_offset=IndirectOffsetOnAxis(
                         ap=srcst[:, i * 2 + k:i * 2 + k + 1], axis=0),
                     bounds_check=rbc2, oob_is_err=False),
                 inc=f"{tag}_cg", n=16)
        xb = xin[i % 2]
        p.wait("sync", f"{tag}_cv", x_marks[i % 2])
        xm = p.dma("sync", xb[:, :], x_ap[i * 128:(i + 1) * 128, :], inc=f"{tag}_xl")
        p.wait("vector", f"{tag}_cg", gm)
        p.do("vector", lambda e, i=i:
             e.tensor_scalar(cmb[:, :], gth[0][:, :], wst[:, i * 2:i * 2 + 1],
                             None, op0=ALU.mult), inc=f"{tag}_cv")
        p.wait("vector", f"{tag}_cv")
        p.do("vector", lambda e, i=i:
             e.scalar_tensor_tensor(cmb[:, :], gth[1][:, :],
                                    wst[:, i * 2 + 1:i * 2 + 2], cmb[:, :],
                                    op0=ALU.mult, op1=ALU.add), inc=f"{tag}_cv")
        yb = ybf[i % 2]
        p.wait("vector", f"{tag}_yo", y_marks[i % 2])
        p.wait("vector", f"{tag}_xl", xm)
        p.wait("vector", f"{tag}_cv")
        p.do("vector", lambda e, xb=xb, yb=yb:
             e.tensor_tensor(yb[:, :], xb[:, :], cmb[:, :], ALU.add),
             inc=f"{tag}_cv")
        x_marks[i % 2] = p.mark(f"{tag}_cv")
        p.wait("gpsimd", f"{tag}_cv")
        y_marks[i % 2] = p.dma("gpsimd", y_ap[i * 128:(i + 1) * 128, :], yb[:, :],
                               inc=f"{tag}_yo")
    p.wait("gpsimd", f"{tag}_yo")
    p.barrier()
    es3.close()




def emit_l0attn(p: Prog, c: Consts, waps, x_ap, y_ap, tag, dbg=None):
    es = ExitStack()
    lnx = [p.sbuf(f"{tag}_lnx{dc}", [128, S], BF16, stack=es) for dc in range(DC)]
    r2b = p.sbuf(f"{tag}_r2", [1, S], BF16, stack=es)
    r2bcs = p.sbuf(f"{tag}_r2b", [128, 128], F32, stack=es)
    r2col = p.sbuf(f"{tag}_r2c", [128, 1], F32, stack=es)
    sclc = p.sbuf(f"{tag}_scl", [128, 1], F32, stack=es)
    wq = p.sbuf(f"{tag}_wq", [128, DC, D], BF16, stack=es)
    wk = p.sbuf(f"{tag}_wk", [128, DC, D], BF16, stack=es)
    wv = p.sbuf(f"{tag}_wv", [128, DC, D], BF16, stack=es)
    wio = p.sbuf(f"{tag}_wio", [128, DC, 48], BF16, stack=es)
    wo = p.sbuf(f"{tag}_wo", [128, DC, D], BF16, stack=es)
    lnw = p.sbuf(f"{tag}_lnw", [128, DC], F32, stack=es)
    wtmp = p.sbuf(f"{tag}_wtmp", [128, 512], F32, stack=es)
    qfc = p.sbuf(f"{tag}_qfc", [128, DC, 128], BF16, stack=es)
    kfc = p.sbuf(f"{tag}_kfc", [128, DC, 128], BF16, stack=es)
    ktm = p.sbuf(f"{tag}_ktm", [128, D], BF16, stack=es)
    vtm = p.sbuf(f"{tag}_vtm", [128, D], BF16, stack=es)
    hout = p.sbuf(f"{tag}_hout", [128, DC, 128], BF16, stack=es)
    t48 = p.sbuf(f"{tag}_t48", [128, 48], F32, stack=es)
    g16 = p.sbuf(f"{tag}_g16", [128, 16 * 5], F32, stack=es)
    logi = p.sbuf(f"{tag}_logi", [128, 16], F32, stack=es)
    logf = p.sbuf(f"{tag}_logf", [128, 16], F32, stack=es)
    og = p.sbuf(f"{tag}_og", [128, 16], F32, stack=es)
    Fsb = p.sbuf(f"{tag}_F", [128, 16], F32, stack=es)
    blf = p.sbuf(f"{tag}_blf", [128, 16], F32, stack=es)
    FTsb = p.sbuf(f"{tag}_FT", [16, 128], F32, stack=es)
    EFt = p.sbuf(f"{tag}_EFt", [16, 128], F32, stack=es)
    OTsb = p.sbuf(f"{tag}_OT", [16, 128], F32, stack=es)
    EF2sb = p.sbuf(f"{tag}_EF2", [128, 128], F32, stack=es)
    O2sb = p.sbuf(f"{tag}_O2", [128, 128], F32, stack=es)
    Mexp = p.sbuf(f"{tag}_M", [128, 128], BF16, stack=es)
    SpT = p.sbuf(f"{tag}_SpT", [128, 128], BF16, stack=es)
    qp = p.sbuf(f"{tag}_qp", [64, 128], BF16, stack=es)
    ccol = p.sbuf(f"{tag}_ccol", [128, 1], F32, stack=es)
    kpp = p.sbuf(f"{tag}_kpp", [128, 64], BF16, stack=es)
    wev = p.sbuf(f"{tag}_wev", [128, 128], F32, stack=es)
    Hb = p.sbuf(f"{tag}_H", [64, NH * HD], F32, stack=es)
    Hbf = p.sbuf(f"{tag}_Hbf", [64, NH * HD], BF16, stack=es)
    sel2 = p.sbuf(f"{tag}_sel2", [16, 128], F32, stack=es)
    eh1 = p.sbuf(f"{tag}_eh1", [16, 128], F32, stack=es)
    i16r = p.sbuf(f"{tag}_i16r", [16, 128], F32, stack=es)
    i16c = p.sbuf(f"{tag}_i16c", [16, 1], F32, stack=es)
    hm = p.sbuf(f"{tag}_hm", [16, 128], F32, stack=es)
    PB = [p.psum(f"{tag}_PB{i}", [128, 512], F32, stack=es) for i in range(8)]
    ps_a = PB[0]
    ps_b = [PB[1][:, 0:128], PB[2][:, 0:128]]
    ps_T1 = PB[3][:, 0:128]
    ps_S = PB[4][:, 0:128]
    ps_o = PB[5][:, 0:128]
    ps_h = PB[6][0:64, 0:64]
    ps_F = PB[7][:, 0:128]

    p.dma("sync", lnw[:, :], waps["alnw"], inc=f"{tag}_lnl")
    p.wait("vector", f"{tag}_lnl")
    for name, sb, ncol, fold in (("aq", wq, D, True), ("ak", wk, D, True),
                                 ("av", wv, D, True), ("aio", wio, 48, True),
                                 ("aout", wo, D, False)):
        for kc in range(DC):
            for hf in range(1 if ncol <= 512 else 2):
                w0 = hf * 512
                w1 = min(ncol, w0 + 512)
                p.wait("gpsimd", f"{tag}_wf")
                p.dma("gpsimd", wtmp[:, 0:w1 - w0], waps[name][:, kc, w0:w1],
                      inc=f"{tag}_wl")
                p.wait("vector", f"{tag}_wl")
                if fold:
                    p.do("vector", lambda e, sb=sb, kc=kc, w0=w0, w1=w1:
                         e.tensor_scalar(sb[:, kc, w0:w1], wtmp[:, 0:w1 - w0],
                                         lnw[:, kc:kc + 1], None, op0=ALU.mult),
                         inc=f"{tag}_wf")
                else:
                    p.do("vector", lambda e, sb=sb, kc=kc, w0=w0, w1=w1:
                         e.tensor_copy(sb[:, kc, w0:w1], wtmp[:, 0:w1 - w0]),
                         inc=f"{tag}_wf")
    p.do("gpsimd", lambda e: e.iota(i16r[:, :], pattern=[[1, 128]], base=0,
                                    channel_multiplier=0,
                                    allow_small_or_imprecise_dtypes=True), inc=f"{tag}_ci")
    p.do("gpsimd", lambda e: e.iota(i16c[:, :], pattern=[[0, 1]], base=0,
                                    channel_multiplier=1,
                                    allow_small_or_imprecise_dtypes=True), inc=f"{tag}_ci")
    p.wait("vector", f"{tag}_ci")
    p.do("vector", lambda e: e.tensor_scalar(hm[:, :], i16r[:, :], 63.5, None,
                                             op0=ALU.is_gt), inc=f"{tag}_cs")
    p.wait("vector", f"{tag}_cs")
    p.wait("vector", f"{tag}_cs")
    p.barrier()

    for b in range(BLOC):
        rsub = ExitStack()
        st = Stream(p, c, cols=S, stack=rsub)
        emit_load_resid(p, c, st, x_ap, t0=b * S, t1=(b + 1) * S,
                        psums=[PB[i][:, 0:128] for i in range(4)])
        emit_ln(p, c, st, lnx, rms2=True, name=f"{tag}_ln{b}",
                tg0=b * (S // 512), tg1=(b + 1) * (S // 512), col0=b * S,
                psums=(PB[1], PB[2], PB[3], PB[4]), r2out=r2b)
        p.do("vector", lambda e: e.memset(Hb[:, :], 0.0), inc=f"{tag}_hz")
        p.do("vector", lambda e: e.memset(Hbf[:, :], 0.0), inc=f"{tag}_hz")
        p.wait("tensor", f"{tag}_hz")
        for n in range(NCH):
            sl = slice(n * CHUNK, (n + 1) * CHUNK)
            slg = sl
            p.barrier()
            p.wait("tensor", f"{tag}_r2v")
            p.do("tensor", lambda e, sl=sl:
                 e.matmul(ps_T1[:, :], c.ones_bf[0:1, :], r2b[0:1, sl],
                          start=True, stop=True), inc=f"{tag}_r2m")
            p.wait("vector", f"{tag}_r2m")
            p.do("vector", lambda e: e.tensor_copy(r2bcs[:, :], ps_T1[:, :]),
                 inc=f"{tag}_r2v")
            p.wait("tensor", f"{tag}_r2v")
            p.do("tensor", lambda e:
                 e.matmul(ps_T1[:, 0:1], r2bcs[:, :], c.col(1.0 / 128.0),
                          start=True, stop=True), inc=f"{tag}_r2m")
            p.wait("vector", f"{tag}_r2m")
            p.do("vector", lambda e: e.tensor_copy(r2col[:, 0:1], ps_T1[:, 0:1]),
                 inc=f"{tag}_r2v")
            p.wait("vector", f"{tag}_r2v")
            p.do("vector", lambda e: e.tensor_scalar(sclc[:, 0:1], r2col[:, 0:1],
                 1.0 / GATE_CAP, None, op0=ALU.mult), inc=f"{tag}_r2v")
            p.wait("tensor", f"{tag}_r2v")
            for oc in range(DC):
                pb = ps_b[oc % 2]
                p.wait("tensor", f"{tag}_pe")
                for kc in range(DC):
                    p.do("tensor", lambda e, oc=oc, kc=kc, sl=sl, pb=pb:
                         e.matmul(pb[:, :], wq[:, kc, oc * 128:(oc + 1) * 128],
                                  lnx[kc][:, sl], start=(kc == 0), stop=(kc == DC - 1)),
                         inc=f"{tag}_pm")
                p.wait("vector", f"{tag}_pm")
                p.do("vector", lambda e, oc=oc, pb=pb:
                     e.tensor_tensor(qfc[:, oc, :], pb[:, :], r2bcs[:, :],
                                     ALU.mult), inc=f"{tag}_pe")
                pb = ps_b[(oc + 1) % 2]
                p.wait("tensor", f"{tag}_pe")
                for kc in range(DC):
                    p.do("tensor", lambda e, oc=oc, kc=kc, sl=sl, pb=pb:
                         e.matmul(pb[:, :], wk[:, kc, oc * 128:(oc + 1) * 128],
                                  lnx[kc][:, sl], start=(kc == 0), stop=(kc == DC - 1)),
                         inc=f"{tag}_pm")
                p.wait("vector", f"{tag}_pm")
                p.do("vector", lambda e, oc=oc, pb=pb:
                     e.tensor_tensor(kfc[:, oc, :], pb[:, :], r2bcs[:, :],
                                     ALU.mult), inc=f"{tag}_pe")
            for g in range(2):
                p.wait("tensor", f"{tag}_pe")
                for kc in range(DC):
                    p.do("tensor", lambda e, g=g, kc=kc, sl=sl:
                         e.matmul(ps_a[:, :], lnx[kc][:, sl],
                                  wk[:, kc, g * 512:(g + 1) * 512],
                                  start=(kc == 0), stop=(kc == DC - 1)), inc=f"{tag}_pm")
                p.wait("vector", f"{tag}_pm")
                p.do("vector", lambda e, g=g:
                     e.tensor_scalar(ktm[:, g * 512:(g + 1) * 512], ps_a[:, :],
                                     r2col[:, 0:1], None, op0=ALU.mult),
                     inc=f"{tag}_pe")
                p.wait("tensor", f"{tag}_pe")
                for kc in range(DC):
                    p.do("tensor", lambda e, g=g, kc=kc, sl=sl:
                         e.matmul(ps_a[:, :], lnx[kc][:, sl],
                                  wv[:, kc, g * 512:(g + 1) * 512],
                                  start=(kc == 0), stop=(kc == DC - 1)), inc=f"{tag}_pm")
                p.wait("vector", f"{tag}_pm")
                p.do("vector", lambda e, g=g:
                     e.tensor_scalar(vtm[:, g * 512:(g + 1) * 512], ps_a[:, :],
                                     r2col[:, 0:1], None, op0=ALU.mult),
                     inc=f"{tag}_pe")
            p.wait("tensor", f"{tag}_pe")
            for kc in range(DC):
                p.do("tensor", lambda e, kc=kc, sl=sl:
                     e.matmul(ps_a[:, 0:48], lnx[kc][:, sl], wio[:, kc, :],
                              start=(kc == 0), stop=(kc == DC - 1)), inc=f"{tag}_pm")
            p.wait("scalar", f"{tag}_pm")
            p.wait("scalar", f"{tag}_r2v")
            p.do("scalar", lambda e: e.activation(t48[:, :],
                 ps_a[:, 0:48], AF.Tanh, scale=sclc[:, 0:1]), inc=f"{tag}_ga")
            p.wait("vector", f"{tag}_ga")
            for dst, c0 in ((logi, 0), (logf, 16)):
                p.do("vector", lambda e, dst=dst, c0=c0:
                     e.tensor_scalar(g16[:, 0:16], t48[:, c0:c0 + 16], -GATE_CAP,
                                     None, op0=ALU.mult), inc=f"{tag}_gv")
                p.wait("vector", f"{tag}_gv")
                p.do("vector", lambda e: e.tensor_scalar(g16[:, 16:32], g16[:, 0:16],
                     -1.0, None, op0=ALU.mult), inc=f"{tag}_gv")
                p.do("vector", lambda e: e.tensor_scalar(g16[:, 32:48], g16[:, 0:16],
                     0.0, None, op0=ALU.max), inc=f"{tag}_gv")
                p.wait("vector", f"{tag}_gv")
                p.do("vector", lambda e: e.tensor_tensor(g16[:, 16:32], g16[:, 0:16],
                     g16[:, 16:32], ALU.max), inc=f"{tag}_gv")
                p.wait("scalar", f"{tag}_gv")
                p.do("scalar", lambda e: e.activation(g16[:, 48:64], g16[:, 16:32],
                     AF.Exp, scale=-1.0), inc=f"{tag}_ga")
                p.wait("scalar", f"{tag}_ga")
                p.do("scalar", lambda e: e.activation(g16[:, 64:80], g16[:, 48:64],
                     AF.Ln, bias=c.col(1.0)), inc=f"{tag}_ga")
                p.wait("vector", f"{tag}_ga")
                p.do("vector", lambda e: e.tensor_tensor(g16[:, 48:64], g16[:, 32:48],
                     g16[:, 64:80], ALU.add), inc=f"{tag}_gv")
                p.wait("vector", f"{tag}_gv")
                p.do("vector", lambda e, dst=dst:
                     e.tensor_scalar(dst[:, :], g16[:, 48:64], -1.0, None,
                                     op0=ALU.mult), inc=f"{tag}_gv")
            p.wait("scalar", f"{tag}_gv")
            p.do("scalar", lambda e: e.activation(og[:, :], t48[:, 32:48], AF.Sigmoid,
                                                  scale=GATE_CAP), inc=f"{tag}_ga")
            p.wait("tensor", f"{tag}_ga")
            p.wait("tensor", f"{tag}_gv")
            p.do("tensor", lambda e: e.matmul(ps_F[:, 0:16], c.triu_incl_f[:, :],
                                              logf[:, :], start=True, stop=True),
                 inc=f"{tag}_fm")
            p.wait("vector", f"{tag}_fm")
            p.do("vector", lambda e: e.tensor_copy(Fsb[:, :], ps_F[:, 0:16]),
                 inc=f"{tag}_fv")
            p.wait("vector", f"{tag}_fv")
            p.do("vector", lambda e: e.tensor_tensor(blf[:, :], logi[:, :], Fsb[:, :],
                                                     ALU.subtract), inc=f"{tag}_fv")
            p.wait("tensor", f"{tag}_fv")
            p.do("tensor", lambda e: e.transpose(ps_F[0:16, 0:128],
                 Fsb[:, :], c.ident_f[:, :]), inc=f"{tag}_fm")
            p.wait("vector", f"{tag}_fm")
            p.do("vector", lambda e: e.tensor_copy(FTsb[:, :], ps_F[0:16, 0:128]),
                 inc=f"{tag}_fv")
            p.wait("tensor", f"{tag}_fv")
            p.do("tensor", lambda e: e.transpose(ps_F[0:16, 0:128], og[:, :],
                                                 c.ident_f[:, :]), inc=f"{tag}_fm")
            p.wait("vector", f"{tag}_fm")
            p.do("vector", lambda e: e.tensor_copy(OTsb[:, :], ps_F[0:16, 0:128]),
                 inc=f"{tag}_fv")
            p.wait("scalar", f"{tag}_fv")
            p.do("scalar", lambda e: e.activation(EFt[:, :], FTsb[:, :], AF.Exp),
                 inc=f"{tag}_ga")
            p.wait("tensor", f"{tag}_ga")
            for h in range(NH):
                hp, hr = h // 2, (h % 2) * 64
                if h % 2 == 0:
                    p.wait("vector", f"{tag}_e2m")
                    p.do("vector", lambda e, hp=hp: e.tensor_scalar(
                         sel2[:, :], hm[:, :], 1.0, float(2 * hp),
                         op0=ALU.mult, op1=ALU.add), inc=f"{tag}_slv")
                    p.wait("vector", f"{tag}_slv")
                    p.do("vector", lambda e: e.tensor_scalar(
                         sel2[:, :], sel2[:, :], i16c[:, 0:1], None,
                         op0=ALU.is_equal), inc=f"{tag}_slv")
                    p.wait("tensor", f"{tag}_slv")
                    p.wait("tensor", f"{tag}_e2v")
                    p.do("tensor", lambda e, hp=hp:
                         e.matmul(ps_o[:, :], sel2[:, :], EFt[:, :],
                                  start=True, stop=True), inc=f"{tag}_e2m")
                    p.wait("vector", f"{tag}_e2m")
                    p.do("vector", lambda e: e.tensor_copy(EF2sb[:, :], ps_o[:, :]),
                         inc=f"{tag}_e2v")
                    p.wait("tensor", f"{tag}_e2v")
                    p.do("tensor", lambda e, hp=hp:
                         e.matmul(ps_o[:, :], sel2[:, :], OTsb[:, :],
                                  start=True, stop=True), inc=f"{tag}_e2m")
                    p.wait("vector", f"{tag}_e2m")
                    p.do("vector", lambda e: e.tensor_copy(O2sb[:, :], ps_o[:, :]),
                         inc=f"{tag}_e2v")
                    p.wait("tensor", f"{tag}_e2v")
                p.wait("vector", f"{tag}_t1m")
                p.do("vector", lambda e, h=h: e.tensor_scalar(
                     eh1[:, :], i16c[:, 0:1].to_broadcast([16, 128]), float(h), None,
                     op0=ALU.is_equal), inc=f"{tag}_ehv")
                p.wait("tensor", f"{tag}_ga2")
                p.wait("tensor", f"{tag}_ehv")
                p.do("tensor", lambda e, h=h:
                     e.matmul(ps_T1[:, :], eh1[:, :], FTsb[:, :],
                              start=True, stop=False), inc=f"{tag}_t1m")
                p.do("tensor", lambda e:
                     e.matmul(ps_T1[:, :], c.ident_f[:, :], c.mask_sq[:, :],
                              start=False, stop=True), inc=f"{tag}_t1m")
                p.wait("scalar", f"{tag}_t1m")
                p.wait("scalar", f"{tag}_sv")
                p.do("scalar", lambda e, h=h:
                     e.activation(Mexp[:, :], ps_T1[:, :], AF.Exp,
                                  bias=blf[:, h:h + 1]), inc=f"{tag}_ga2")
                p.do("tensor", lambda e, hp=hp, hr=hr:
                     e.matmul(ps_S[:, :], kfc[hr:hr + 64, hp, :], qfc[hr:hr + 64, hp, :],
                              start=True, stop=True), inc=f"{tag}_sm")
                p.wait("vector", f"{tag}_sm")
                p.wait("vector", f"{tag}_ga2")
                p.do("vector", lambda e: e.tensor_tensor(SpT[:, :], ps_S[:, :],
                     Mexp[:, :], ALU.mult), inc=f"{tag}_sv")
                p.do("vector", lambda e, hp=hp, hr=hr:
                     e.tensor_tensor(qp[:, :], qfc[hr:hr + 64, hp, :],
                                     EF2sb[hr:hr + 64, :], ALU.mult), inc=f"{tag}_sv")
                p.do("vector", lambda e:
                     e.tensor_copy(ccol[:, :], Mexp[:, 127:128]), inc=f"{tag}_sv")
                p.wait("vector", f"{tag}_sv")
                p.do("vector", lambda e, h=h:
                     e.tensor_scalar(kpp[:, :], ktm[:, h * 64:(h + 1) * 64],
                                     ccol[:, 0:1], None, op0=ALU.mult),
                     inc=f"{tag}_sv")
                p.wait("tensor", f"{tag}_sv")
                p.do("tensor", lambda e, h=h:
                     e.matmul(ps_o[0:64, :], vtm[:, h * 64:(h + 1) * 64], SpT[:, :],
                              start=True, stop=False), inc=f"{tag}_om")
                mo = p.do("tensor", lambda e, h=h:
                     e.matmul(ps_o[0:64, :], Hbf[:, h * 64:(h + 1) * 64], qp[:, :],
                              start=False, stop=True), inc=f"{tag}_om")
                p.do("tensor", lambda e, h=h:
                     e.matmul(ps_h[:, :], kpp[:, :], vtm[:, h * 64:(h + 1) * 64],
                              start=True, stop=True), inc=f"{tag}_hm")
                p.wait("vector", f"{tag}_om")
                p.do("vector", lambda e, h=h, hp=hp, hr=hr:
                     e.tensor_tensor(hout[hr:hr + 64, hp, :], ps_o[0:64, :],
                                     O2sb[hr:hr + 64, :], ALU.mult), inc=f"{tag}_ov")
                p.wait("vector", f"{tag}_hm")
                p.do("vector", lambda e, h=h, hr=hr:
                     e.tensor_scalar(Hb[:, h * 64:(h + 1) * 64],
                                     Hb[:, h * 64:(h + 1) * 64],
                                     EF2sb[hr:hr + 64, 127:128], None, op0=ALU.mult),
                     inc=f"{tag}_hv")
                p.wait("vector", f"{tag}_hv")
                p.do("vector", lambda e, h=h:
                     e.tensor_tensor(Hb[:, h * 64:(h + 1) * 64],
                                     Hb[:, h * 64:(h + 1) * 64], ps_h[:, :], ALU.add),
                     inc=f"{tag}_hv")
                p.wait("vector", f"{tag}_hv")
                p.wait("vector", f"{tag}_om", mo)
                p.do("vector", lambda e, h=h:
                     e.tensor_copy(Hbf[:, h * 64:(h + 1) * 64],
                                   Hb[:, h * 64:(h + 1) * 64]), inc=f"{tag}_hv")
                p.wait("tensor", f"{tag}_hv")
                p.wait("tensor", f"{tag}_ov")
            for dc in range(DC):
                pb = ps_b[dc % 2]
                p.wait("tensor", f"{tag}_we")
                for fc in range(DC):
                    p.do("tensor", lambda e, dc=dc, fc=fc, pb=pb:
                         e.matmul(pb[:, :], wo[:, fc, dc * 128:(dc + 1) * 128],
                                  hout[:, fc, :], start=(fc == 0), stop=(fc == DC - 1)),
                         inc=f"{tag}_wm")
                p.wait("vector", f"{tag}_wm")
                p.do("vector", lambda e, dc=dc, sl=sl, pb=pb:
                     e.tensor_tensor(wev[:, :], pb[:, :],
                                     lnx[dc][:, sl], ALU.add), inc=f"{tag}_we")
                p.wait("vector", f"{tag}_we")
                p.do("vector", lambda e, dc=dc, slg=slg:
                     e.tensor_tensor(st.resid[dc][:, slg], st.resid[dc][:, slg],
                                     wev[:, :], ALU.add), inc=f"{tag}_we")
        p.barrier()
        emit_store_out(p, c, st, y_ap, t0=b * S, t1=(b + 1) * S,
                       psums=[PB[i][:, 0:128] for i in range(2)])
        rsub.close()
    p.barrier()
    es.close()


def emit_l1attn(p: Prog, c: Consts, waps, x_ap, y_ap, tag, upto=9, dbga=None, dbgb=None):
    SCALE = 0.125
    NKT = S // 128
    NQG = S // 512
    es = ExitStack()
    lnx = [p.sbuf(f"{tag}_lnx{dc}", [128, S], BF16, stack=es) for dc in range(DC)]
    qf = p.sbuf(f"{tag}_qf", [128, S], BF16, stack=es)
    k1f = p.sbuf(f"{tag}_k1f", [128, S], BF16, stack=es)
    k2f = p.sbuf(f"{tag}_k2f", [128, S], BF16, stack=es)
    vtm = p.sbuf(f"{tag}_vtm", [128, NKT, 128], BF16, stack=es)
    es1 = p.sbuf(f"{tag}_es1", [128, NKT, 512], BF16, stack=es)
    es2_ = p.sbuf(f"{tag}_es2", [128, NKT, 512], BF16, stack=es)
    hout = p.sbuf(f"{tag}_ho", [128, S], BF16, stack=es)
    wpr = [p.sbuf(f"{tag}_wp{i}", [128, DC, 128], BF16, stack=es) for i in range(4)]
    wosl = p.sbuf(f"{tag}_wo", [128, D], BF16, stack=es)
    Ba = p.sbuf(f"{tag}_Ba", [128, 512], F32, stack=es)
    Bb = p.sbuf(f"{tag}_Bb", [128, 512], F32, stack=es)
    Bg = p.sbuf(f"{tag}_Bg", [64, 512], F32, stack=es)
    Nt = p.sbuf(f"{tag}_Nt", [128, 512], F32, stack=es)
    Nsb = p.sbuf(f"{tag}_Nsb", [128, 512], BF16, stack=es)
    Asb = p.sbuf(f"{tag}_Asb", [128, 512], BF16, stack=es)
    lam16 = p.sbuf(f"{tag}_lam", [128, 16], F32, stack=es)
    llrow = p.sbuf(f"{tag}_ll", [1, 16], F32, stack=es)
    PB = [p.psum(f"{tag}_PB{i}", [128, 512], F32, stack=es) for i in range(8)]
    psA, psZ1, psZ2, psD, psO = PB[5], PB[6], PB[7], PB[0], PB[4]
    psA2 = [PB[5], PB[1]]
    p.dma("sync", llrow[:, :], waps["dll"], inc=f"{tag}_ll")
    p.wait("tensor", f"{tag}_ll")
    p.do("tensor", lambda e: e.matmul(PB[4][:, 0:16], c.ones_f[0:1, :], llrow[:, :],
                                      start=True, stop=True), inc=f"{tag}_lm")
    p.wait("scalar", f"{tag}_lm")
    p.do("scalar", lambda e: e.activation(lam16[:, :], PB[4][:, 0:16], AF.Sigmoid),
         inc=f"{tag}_la")
    p.wait("tensor", f"{tag}_la")
    p.wait("vector", f"{tag}_la")
    for b in range(BLOC):
        rsub = ExitStack()
        st = Stream(p, c, cols=S, stack=rsub)
        emit_load_resid(p, c, st, x_ap, t0=b * S, t1=(b + 1) * S,
                        psums=[PB[i][:, 0:128] for i in range(4)])
        emit_ln(p, c, st, lnx, rms2=False, name=f"{tag}_ln{b}",
                tg0=b * (S // 512), tg1=(b + 1) * (S // 512), col0=b * S,
                psums=(PB[0], PB[1], PB[2], PB[3]))
        if dbga is not None and b == 1:
            p.wait("gpsimd", f"{tag}_woe")
            p.dma("gpsimd", dbgb[8, 0:128, 0:512], st.resid[0][:, 0:512],
                  inc=f"{tag}_dbg")
            p.wait("gpsimd", f"{tag}_dbg")
        for hp in range(DC):
            if dbga is not None and b == 0:
                p.wait("gpsimd", f"{tag}_woe")
                p.dma("gpsimd", dbgb[hp, 0:128, 0:512], st.resid[0][:, 0:512],
                      inc=f"{tag}_dbg")
                p.wait("gpsimd", f"{tag}_dbg")
            p.wait("gpsimd", f"{tag}_pm")
            p.wait("gpsimd", f"{tag}_wom")
            for i, nm in enumerate(("dq", "dk1", "dk2", "dv")):
                p.dma("gpsimd", wpr[i][:, :, :],
                      waps[nm][:, :, hp * 128:(hp + 1) * 128], inc=f"{tag}_wl")
            p.dma("gpsimd", wosl[:, :], waps["do"][:, hp, :], inc=f"{tag}_wl")
            p.wait("tensor", f"{tag}_wl")
            p.wait("tensor", f"{tag}_woe")
            for dst, wi in ((qf, 0), (k1f, 1), (k2f, 2)):
                for tg in range(NQG):
                    pb = PB[1 + tg % 2]
                    p.wait("tensor", f"{tag}_pe")
                    for kc in range(DC):
                        p.do("tensor", lambda e, wi=wi, kc=kc, tg=tg, b=b, pb=pb:
                             e.matmul(pb[:, :], wpr[wi][:, kc, :],
                                      lnx[kc][:, tg * 512:(tg + 1) * 512],
                                      start=(kc == 0), stop=(kc == DC - 1)),
                             inc=f"{tag}_pm")
                    p.wait("vector", f"{tag}_pm")
                    p.do("vector", lambda e, dst=dst, tg=tg, pb=pb:
                         e.tensor_copy(dst[:, tg * 512:(tg + 1) * 512], pb[:, :]),
                         inc=f"{tag}_pe")
            for kt in range(NKT):
                pb = PB[1 + kt % 2]
                p.wait("tensor", f"{tag}_pe")
                for kc in range(DC):
                    p.do("tensor", lambda e, kc=kc, kt=kt, pb=pb:
                         e.matmul(pb[:, 0:128], lnx[kc][:, kt * 128:(kt + 1) * 128],
                                  wpr[3][:, kc, :], start=(kc == 0),
                                  stop=(kc == DC - 1)), inc=f"{tag}_pm")
                p.wait("vector", f"{tag}_pm")
                p.do("vector", lambda e, kt=kt, pb=pb:
                     e.tensor_copy(vtm[:, kt, :], pb[:, 0:128]), inc=f"{tag}_pe")
            p.wait("tensor", f"{tag}_pe")
            for h2 in range(2 if upto >= 2 else 0):
                h = 2 * hp + h2
                hr = h2 * 64
                for qg in range(NQG):
                    nkt = 4 * (qg + 1)
                    qsl = slice(qg * 512, (qg + 1) * 512)
                    p.wait("scalar", f"{tag}_cv")
                    for mat, (kf, est, psZ) in enumerate(
                            ((k1f, es1, psZ1), (k2f, es2_, psZ2))):
                        for kt in range(nkt):
                            jj = kt - 4 * qg
                            p.wait("tensor", f"{tag}_ea")
                            p.do("tensor", lambda e, kf=kf, kt=kt, hr=hr, qsl=qsl:
                                 e.matmul(psA[:, :],
                                          kf[hr:hr + 64, kt * 128:(kt + 1) * 128],
                                          qf[hr:hr + 64, qsl],
                                          start=True, stop=(kt < 4 * qg)),
                                 inc=f"{tag}_sm")
                            if jj >= 0:
                                p.do("tensor", lambda e, jj=jj:
                                     e.matmul(psA[:, jj * 128:(jj + 1) * 128],
                                              c.ident_f[:, :], c.mask_sq[:, :],
                                              start=False, stop=True),
                                     inc=f"{tag}_sm")
                            p.wait("scalar", f"{tag}_sm")
                            p.do("scalar", lambda e, est=est, kt=kt:
                                 e.activation(est[:, kt, :], psA[:, :], AF.Exp,
                                              scale=SCALE, bias=c.col(-12.0)),
                                 inc=f"{tag}_ea")
                            if jj >= 1:
                                p.wait("vector", f"{tag}_ea")
                                p.do("vector", lambda e, est=est, kt=kt, jj=jj:
                                     e.memset(est[:, kt, 0:jj * 128], 0.0),
                                     inc=f"{tag}_ev")
                            p.wait("tensor", f"{tag}_ea")
                            p.wait("tensor", f"{tag}_ev")
                            p.do("tensor", lambda e, est=est, kt=kt, psZ=psZ:
                                 e.matmul(psZ[:, :], c.ones_bf[:, :], est[:, kt, :],
                                          start=(kt == 0), stop=(kt == nkt - 1)),
                                 inc=f"{tag}_zm")
                    if upto < 3:
                        continue
                    p.wait("vector", f"{tag}_zm")
                    p.do("vector", lambda e: e.reciprocal(Ba[:, :], psZ1[:, :]),
                         inc=f"{tag}_bv")
                    p.do("vector", lambda e: e.reciprocal(Bb[:, :], psZ2[:, :]),
                         inc=f"{tag}_bv")
                    p.wait("vector", f"{tag}_bv")
                    p.do("vector", lambda e, h=h:
                         e.tensor_scalar(Bb[:, :], Bb[:, :], lam16[:, h:h + 1],
                                         None, op0=ALU.mult), inc=f"{tag}_bv")
                    p.wait("vector", f"{tag}_bv")
                    if upto < 4:
                        continue
                    p.wait("tensor", f"{tag}_hv")
                    p.wait("tensor", f"{tag}_gv")
                    for kt in range(nkt):
                        p.wait("vector", f"{tag}_om")
                        p.wait("vector", f"{tag}_dm")
                        p.do("vector", lambda e, kt=kt:
                             e.tensor_tensor(Nt[:, :], es1[:, kt, :], Ba[:, :],
                                             ALU.mult), inc=f"{tag}_cv")
                        p.wait("vector", f"{tag}_cv")
                        p.do("vector", lambda e, kt=kt:
                             e.tensor_tensor(Asb[:, :], es2_[:, kt, :], Bb[:, :],
                                             ALU.mult), inc=f"{tag}_cv")
                        p.wait("vector", f"{tag}_cv")
                        p.do("vector", lambda e:
                             e.tensor_tensor(Nsb[:, :], Nt[:, :], Asb[:, :],
                                             ALU.subtract), inc=f"{tag}_cv")
                        p.wait("vector", f"{tag}_cv")
                        p.do("vector", lambda e:
                             e.tensor_scalar(Nt[:, :], Nsb[:, :], -1.0, None,
                                             op0=ALU.mult), inc=f"{tag}_cv")
                        p.wait("vector", f"{tag}_cv")
                        p.do("vector", lambda e:
                             e.tensor_tensor(Asb[:, :], Nsb[:, :], Nt[:, :],
                                             ALU.max), inc=f"{tag}_cv")
                        p.wait("tensor", f"{tag}_cv")
                        p.do("tensor", lambda e, kt=kt:
                             e.matmul(psD[:, :], c.ones_bf[:, :], Asb[:, :],
                                      start=(kt == 0), stop=(kt == nkt - 1)),
                             inc=f"{tag}_dm")
                        p.do("tensor", lambda e, kt=kt, hr=hr:
                             e.matmul(psO[0:64, :], vtm[:, kt, hr:hr + 64],
                                      Nsb[:, :], start=(kt == 0),
                                      stop=(kt == nkt - 1)), inc=f"{tag}_om")
                    if upto < 5:
                        continue
                    if dbga is not None and b == 0 and hp == 3 and h2 == 0 and qg == 0:
                        p.wait("gpsimd", f"{tag}_cv")
                        p.wait("gpsimd", f"{tag}_bv")
                        p.dma("gpsimd", dbga[0, 0:128, 0:512], es1[:, 0, :], inc=f"{tag}_dbg")
                        p.dma("gpsimd", dbga[1, 0:128, 0:512], es2_[:, 0, :], inc=f"{tag}_dbg")
                        p.dma("gpsimd", dbga[2, 0:128, 0:512], Ba[:, :], inc=f"{tag}_dbg")
                        p.dma("gpsimd", dbga[3, 0:128, 0:512], Bb[:, :], inc=f"{tag}_dbg")
                        p.dma("gpsimd", dbga[4, 0:128, 0:512], Nsb[:, :], inc=f"{tag}_dbg")
                        p.dma("gpsimd", dbga[5, 0:128, 0:512], Asb[:, :], inc=f"{tag}_dbg")
                        p.wait("gpsimd", f"{tag}_dbg")
                    p.wait("vector", f"{tag}_dm")
                    p.do("vector", lambda e: e.tensor_scalar(Bg[:, :], psD[0:64, :],
                         1e-6, None, op0=ALU.max), inc=f"{tag}_gv")
                    p.wait("vector", f"{tag}_gv")
                    p.do("vector", lambda e: e.reciprocal(Bg[:, :], Bg[:, :]),
                         inc=f"{tag}_gv")
                    p.wait("vector", f"{tag}_om")
                    p.wait("vector", f"{tag}_gv")
                    p.do("vector", lambda e, hr=hr, qsl=qsl:
                         e.tensor_tensor(hout[hr:hr + 64, qsl], psO[0:64, :],
                                         Bg[:, :], ALU.mult), inc=f"{tag}_hv")
                    p.wait("tensor", f"{tag}_hv")
                    if dbga is not None and b == 0 and hp == 3 and h2 == 1 and qg == 0:
                        p.wait("gpsimd", f"{tag}_hv")
                        p.wait("gpsimd", f"{tag}_gv")
                        p.dma("gpsimd", dbga[6, 0:64, 0:512], Bg[:, :], inc=f"{tag}_dbg")
                        p.dma("gpsimd", dbga[7, 0:128, 0:512], hout[:, 0:512], inc=f"{tag}_dbg")
                        p.wait("gpsimd", f"{tag}_dbg")
            for tg in range(NQG if upto >= 6 else 0):
                for dc in range(DC):
                    pb = PB[1 + dc % 2]
                    p.wait("tensor", f"{tag}_woe")
                    p.do("tensor", lambda e, dc=dc, tg=tg, pb=pb:
                         e.matmul(pb[:, :], wosl[:, dc * 128:(dc + 1) * 128],
                                  hout[:, tg * 512:(tg + 1) * 512],
                                  start=True, stop=True), inc=f"{tag}_wom")
                    p.wait("vector", f"{tag}_wom")
                    p.do("vector", lambda e, dc=dc, tg=tg, pb=pb:
                         e.tensor_tensor(st.resid[dc][:, tg * 512:(tg + 1) * 512],
                                         st.resid[dc][:, tg * 512:(tg + 1) * 512],
                                         pb[:, :], ALU.add), inc=f"{tag}_woe")
        p.barrier()
        emit_store_out(p, c, st, y_ap, t0=b * S, t1=(b + 1) * S,
                       psums=[PB[i][:, 0:128] for i in range(2)])
        rsub.close()
    p.barrier()
    es.close()


def _prep_common(inputs):
    d = {}
    d["l1_wfcT"] = np.ascontiguousarray(inputs["L1_ffn_W_fc"].T)
    d["l1_wprT"] = np.ascontiguousarray(inputs["L1_ffn_W_proj"].T)
    d["l1_lnw"] = np.ascontiguousarray(
        np.asarray(inputs["L1_ffn_ln_w"]).reshape(HC_FFN, 128).T)
    d["l1_lnb"] = np.ascontiguousarray(
        np.asarray(inputs["L1_ffn_ln_b"]).reshape(HC_FFN, 128).T)
    d["router"] = np.ascontiguousarray(
        np.asarray(inputs["L0_ffn_router_w"], dtype=np.float32)
        .reshape(DC, 128, E_MOE).transpose(1, 0, 2))
    d["g5"] = np.ascontiguousarray(
        np.asarray(inputs["L0_ffn_gate_w"], dtype=np.float32)
        .reshape(E_MOE, DC, 128, DC, 128).transpose(0, 2, 1, 3, 4))
    d["u5"] = np.ascontiguousarray(
        np.asarray(inputs["L0_ffn_up_w"], dtype=np.float32)
        .reshape(E_MOE, DC, 128, DC, 128).transpose(0, 2, 1, 3, 4))
    d["d5"] = np.ascontiguousarray(
        np.asarray(inputs["L0_ffn_down_w"], dtype=np.float32)
        .reshape(E_MOE, DC, 128, D).transpose(0, 2, 1, 3))
    def _wt(w):
        return np.ascontiguousarray(np.asarray(w, dtype=np.float32).T
                                    .reshape(DC, 128, -1).transpose(1, 0, 2))
    d["aq"] = _wt(inputs["L0_attn_Wq"])
    d["ak"] = _wt(inputs["L0_attn_Wk"])
    d["av"] = _wt(inputs["L0_attn_Wv"])
    d["aio"] = np.ascontiguousarray(np.concatenate(
        [np.asarray(inputs["L0_attn_Wi"], dtype=np.float32).T,
         np.asarray(inputs["L0_attn_Wf"], dtype=np.float32).T,
         np.asarray(inputs["L0_attn_Wo"], dtype=np.float32).T], axis=1)
        .reshape(DC, 128, 48).transpose(1, 0, 2))
    d["aout"] = _wt(inputs["L0_attn_Wout"])
    d["alnw"] = np.ascontiguousarray(
        np.asarray(inputs["L0_attn_ln_w"], dtype=np.float32).reshape(DC, 128).T)
    d["dq"] = _wt(inputs["L1_attn_Wq"])
    d["dk1"] = _wt(inputs["L1_attn_Wk1"])
    d["dk2"] = _wt(inputs["L1_attn_Wk2"])
    d["dv"] = _wt(inputs["L1_attn_Wv"])
    d["do"] = _wt(inputs["L1_attn_Wo"])
    d["dll"] = np.ascontiguousarray(
        np.asarray(inputs["L1_attn_lambda_logit"], dtype=np.float32).reshape(1, 16))
    return d


def _cmask(core):
    m = np.zeros((8, 1), np.float32)
    m[:core] = 1.0
    return m


def _x_shard(x, core):
    return np.ascontiguousarray(
        np.asarray(x).reshape(16, S, D)[core * BLOC:(core + 1) * BLOC]
        .reshape(T, D))


def build_l1ffn_launch(first, parts=("quant", "ln", "ffn")):
    nc = bass.Bass("TRN2", target_bir_lowering=False)
    x_in = nc.dram_tensor("x", [T, D], F32, kind="ExternalInput").ap()
    x_out = nc.dram_tensor("y", [T, D], F32, kind="ExternalOutput").ap()
    wfcT = nc.dram_tensor("l1_wfcT", [D, H_FFN], F32, kind="ExternalInput").ap()
    wprT = nc.dram_tensor("l1_wprT", [H_FFN, D], F32, kind="ExternalInput").ap()
    lnw = nc.dram_tensor("l1_lnw", [128, HC_FFN], F32, kind="ExternalInput").ap()
    lnb = nc.dram_tensor("l1_lnb", [128, HC_FFN], F32, kind="ExternalInput").ap()
    if first:
        qfc = nc.dram_tensor("qfc", [HC_FFN, DC, 128, 128], BF16,
                             kind="ExternalOutput").ap()
        qpr = nc.dram_tensor("qpr", [DC, HC_FFN, 128, 128], BF16,
                             kind="ExternalOutput").ap()
    else:
        qfc = nc.dram_tensor("qfc", [HC_FFN, DC, 128, 128], BF16,
                             kind="ExternalInput").ap()
        qpr = nc.dram_tensor("qpr", [DC, HC_FFN, 128, 128], BF16,
                             kind="ExternalInput").ap()
    p = Prog(nc)
    c = Consts(p)
    c.emit()
    if first and "quant" in parts:
        emit_quant(p, c, wfcT, qfc, D, H_FFN, "qf")
        emit_quant(p, c, wprT, qpr, H_FFN, D, "qp")
    emit_l1ffn_sub(p, c, x_in, x_out, qfc, qpr, lnw, lnb, "ff")
    return p.build()


def emit_l1ffn_sub(p, c, x_in, x_out, qfc, qpr, lnw, lnb, tag):
    for b in range(BLOC):
        sub = ExitStack()
        st = Stream(p, c, cols=S, stack=sub)
        lnxt = [p.sbuf(f"{tag}lnx{b}_{dc}", [128, S], BF16, stack=sub)
                for dc in range(DC)]
        emit_load_resid(p, c, st, x_in, t0=b * S, t1=(b + 1) * S)
        emit_ln(p, c, st, lnxt, rms2=False, name=f"{tag}ln{b}",
                tg0=b * (S // 512), tg1=(b + 1) * (S // 512), col0=b * S)
        emit_l1ffn(p, c, st, lnxt, qfc, qpr, lnw, lnb, f"{tag}{b}",
                   tg0=b * (S // 512), tg1=(b + 1) * (S // 512), col0=b * S)
        emit_store_out(p, c, st, x_out, t0=b * S, t1=(b + 1) * S)
        sub.close()
        p.epoch()


def emit_moe_sub(p, c, x_in, x_out, router, aps, tag):
    esk = ExitStack()
    rsb = p.sbuf(f"{tag}_rsb", [128, DC, E_MOE], F32, stack=esk)
    lgall = p.sbuf(f"{tag}_lg", [128, NTT * E_MOE], F32, stack=esk)
    p.dma("sync", rsb[:, :, :], router[:, :, :], inc=f"{tag}_rld")
    srcst = p.sbuf(f"{tag}_src", [128, NTT * 2], U32, stack=esk)
    wstash = p.sbuf(f"{tag}_wsh", [128, NTT * 2], F32, stack=esk)
    lnsub = ExitStack()
    lnxt = [p.sbuf(f"{tag}_lnx{dc}", [128, T], BF16, stack=lnsub)
            for dc in range(DC)]
    p.wait("tensor", f"{tag}_rld")
    for b in range(BLOC):
        rsub = ExitStack()
        rst = Stream(p, c, cols=S, stack=rsub)
        emit_load_resid(p, c, rst, x_in, t0=b * S, t1=(b + 1) * S)
        emit_ln(p, c, rst, lnxt, rms2=False, name=f"{tag}_ln{b}", router=rsb,
                lgall=lgall, tg0=b * (S // 512), tg1=(b + 1) * (S // 512),
                col0=0, rcol0=b * S)
        rsub.close()
    emit_l0ffn_route(p, c, lnxt, lgall, aps, tag, srcst, wstash)
    lnsub.close()
    emit_l0ffn_exec(p, c, aps, x_in, x_out, tag, srcst, wstash)
    esk.close()


def build_moe_launch():
    nc = bass.Bass("TRN2", target_bir_lowering=False)
    x_in = nc.dram_tensor("x", [T, D], F32, kind="ExternalInput").ap()
    x_out = nc.dram_tensor("y", [T, D], F32, kind="ExternalOutput").ap()
    router = nc.dram_tensor("router", [128, DC, E_MOE], F32, kind="ExternalInput").ap()
    g5 = nc.dram_tensor("g5", [E_MOE, 128, DC, DC, 128], F32, kind="ExternalInput").ap()
    u5 = nc.dram_tensor("u5", [E_MOE, 128, DC, DC, 128], F32, kind="ExternalInput").ap()
    d5 = nc.dram_tensor("d5", [E_MOE, 128, DC, D], F32, kind="ExternalInput").ap()
    cmask = nc.dram_tensor("cmask", [8, 1], F32, kind="ExternalInput").ap()
    aps = {
        "xe": nc.dram_tensor("moe_xe", [NSLOT, D], BF16, kind="Internal").ap(),
        "outsc": nc.dram_tensor("moe_out", [NSLOT + 1, D], BF16, kind="Internal").ap(),
        "cc_in": nc.dram_tensor("moe_ccin", [1, 32], F32, kind="Internal").ap(),
        "cc_out": nc.dram_tensor("moe_ccout", [8, 32], F32, kind="Internal").ap(),
        "g5": g5, "u5": u5, "d5": d5, "cmask": cmask,
    }
    p = Prog(nc)
    c = Consts(p)
    c.emit()
    emit_moe_sub(p, c, x_in, x_out, router, aps, "mo")
    return p.build()


def build_l0attn_launch():
    nc = bass.Bass("TRN2", target_bir_lowering=False)
    x_in = nc.dram_tensor("x", [T, D], F32, kind="ExternalInput").ap()
    x_out = nc.dram_tensor("y", [T, D], F32, kind="ExternalOutput").ap()
    waps = {}
    for nm, sh in (("aq", [128, DC, D]), ("ak", [128, DC, D]), ("av", [128, DC, D]),
                   ("aio", [128, DC, 48]), ("aout", [128, DC, D]),
                   ("alnw", [128, DC])):
        waps[nm] = nc.dram_tensor(nm, sh, F32, kind="ExternalInput").ap()
    p = Prog(nc)
    c = Consts(p)
    c.emit()
    emit_l0attn(p, c, waps, x_in, x_out, "at")
    return p.build()


def build_l1attn_launch(upto=9, dbg=False):
    nc = bass.Bass("TRN2", target_bir_lowering=False)
    x_in = nc.dram_tensor("x", [T, D], F32, kind="ExternalInput").ap()
    x_out = nc.dram_tensor("y", [T, D], F32, kind="ExternalOutput").ap()
    dbga = (nc.dram_tensor("dbga", [8, 128, 512], F32, kind="ExternalOutput").ap()
            if dbg else None)
    dbgb = (nc.dram_tensor("dbgb", [9, 128, 512], BF16, kind="ExternalOutput").ap()
            if dbg else None)
    waps = {}
    for nm in ("dq", "dk1", "dk2", "dv", "do"):
        waps[nm] = nc.dram_tensor(nm, [128, DC, D], F32, kind="ExternalInput").ap()
    waps["dll"] = nc.dram_tensor("dll", [1, 16], F32, kind="ExternalInput").ap()
    p = Prog(nc)
    c = Consts(p)
    c.emit()
    emit_l1attn(p, c, waps, x_in, x_out, "da", upto=upto, dbga=dbga, dbgb=dbgb)
    return p.build()


def build_fused(dbg=False):
    nc = bass.Bass("TRN2", target_bir_lowering=False)
    x_in = nc.dram_tensor("x", [T, D], F32, kind="ExternalInput").ap()
    y_out = nc.dram_tensor("y", [T, D], F32, kind="ExternalOutput").ap()
    xi = [nc.dram_tensor(f"xi{i}", [T, D], F32,
                         kind=("ExternalOutput" if dbg else "Internal")).ap()
          for i in range(5)]
    router = nc.dram_tensor("router", [128, DC, E_MOE], F32, kind="ExternalInput").ap()
    g5 = nc.dram_tensor("g5", [E_MOE, 128, DC, DC, 128], F32, kind="ExternalInput").ap()
    u5 = nc.dram_tensor("u5", [E_MOE, 128, DC, DC, 128], F32, kind="ExternalInput").ap()
    d5 = nc.dram_tensor("d5", [E_MOE, 128, DC, D], F32, kind="ExternalInput").ap()
    cmask = nc.dram_tensor("cmask", [8, 1], F32, kind="ExternalInput").ap()
    aps = {
        "xe": nc.dram_tensor("moe_xe", [NSLOT, D], BF16, kind="Internal").ap(),
        "outsc": nc.dram_tensor("moe_out", [NSLOT + 1, D], BF16, kind="Internal").ap(),
        "cc_in": nc.dram_tensor("moe_ccin", [1, 32], F32, kind="Internal").ap(),
        "cc_out": nc.dram_tensor("moe_ccout", [8, 32], F32, kind="Internal").ap(),
        "g5": g5, "u5": u5, "d5": d5, "cmask": cmask,
    }
    aps2 = dict(aps)
    aps2["cc_in"] = nc.dram_tensor("moe_ccin2", [1, 32], F32, kind="Internal").ap()
    aps2["cc_out"] = nc.dram_tensor("moe_ccout2", [8, 32], F32, kind="Internal").ap()
    awaps = {}
    for nm, sh in (("aq", [128, DC, D]), ("ak", [128, DC, D]), ("av", [128, DC, D]),
                   ("aio", [128, DC, 48]), ("aout", [128, DC, D]),
                   ("alnw", [128, DC])):
        awaps[nm] = nc.dram_tensor(nm, sh, F32, kind="ExternalInput").ap()
    dwaps = {}
    for nm in ("dq", "dk1", "dk2", "dv", "do"):
        dwaps[nm] = nc.dram_tensor(nm, [128, DC, D], F32, kind="ExternalInput").ap()
    dwaps["dll"] = nc.dram_tensor("dll", [1, 16], F32, kind="ExternalInput").ap()
    wfcT = nc.dram_tensor("l1_wfcT", [D, H_FFN], F32, kind="ExternalInput").ap()
    wprT = nc.dram_tensor("l1_wprT", [H_FFN, D], F32, kind="ExternalInput").ap()
    lnw = nc.dram_tensor("l1_lnw", [128, HC_FFN], F32, kind="ExternalInput").ap()
    lnb = nc.dram_tensor("l1_lnb", [128, HC_FFN], F32, kind="ExternalInput").ap()
    qfc = nc.dram_tensor("qfc", [HC_FFN, DC, 128, 128], BF16, kind="Internal").ap()
    qpr = nc.dram_tensor("qpr", [DC, HC_FFN, 128, 128], BF16, kind="Internal").ap()

    p = Prog(nc)
    c = Consts(p)
    c.emit()
    emit_quant(p, c, wfcT, qfc, D, H_FFN, "qf")
    p.epoch()
    emit_quant(p, c, wprT, qpr, H_FFN, D, "qp")
    p.epoch()
    emit_moe_sub(p, c, x_in, xi[0], router, aps, "m1")
    p.epoch()
    emit_l0attn(p, c, awaps, xi[0], xi[1], "at")
    p.epoch()
    emit_moe_sub(p, c, xi[1], xi[2], router, aps2, "m2")
    p.epoch()
    emit_l1ffn_sub(p, c, xi[2], xi[3], qfc, qpr, lnw, lnb, "fa")
    p.epoch()
    emit_l1attn(p, c, dwaps, xi[3], xi[4], "da")
    p.epoch()
    emit_l1ffn_sub(p, c, xi[4], y_out, qfc, qpr, lnw, lnb, "fb")
    return p.build()


def kernel(**inputs):
    com = _prep_common(inputs)
    x = np.asarray(inputs["L0_attn_x"], dtype=np.float32)
    nc = build_fused()
    in_maps = []
    for core in range(NCORES):
        m = {"x": _x_shard(x, core), "cmask": _cmask(core)}
        for k in ("router", "g5", "u5", "d5", "aq", "ak", "av", "aio", "aout",
                  "alnw", "dq", "dk1", "dk2", "dv", "do", "dll",
                  "l1_wfcT", "l1_wprT", "l1_lnw", "l1_lnb"):
            m[k] = com[k]
        in_maps.append(m)
    res = run_bass_kernel_spmd(nc, in_maps, core_ids=list(range(NCORES)))
    out = np.concatenate([r["y"].reshape(BLOC, S, D) for r in res.results], axis=0)
    return out


def kernel_unfused(**inputs):
    com = _prep_common(inputs)
    x = np.asarray(inputs["L0_attn_x"], dtype=np.float32)
    shards = [_x_shard(x, core) for core in range(NCORES)]

    nc_moe = build_moe_launch()
    moe_keys = ("router", "g5", "u5", "d5")
    def run_moe(shards):
        in_maps = [{"x": shards[core], **{k: com[k] for k in moe_keys},
                    "cmask": _cmask(core)} for core in range(NCORES)]
        res = run_bass_kernel_spmd(nc_moe, in_maps, core_ids=list(range(NCORES)))
        return [r["y"] for r in res.results]

    shards = run_moe(shards)

    nc_at = build_l0attn_launch()
    at_keys = ("aq", "ak", "av", "aio", "aout", "alnw")
    in_maps = [{"x": shards[core], **{k: com[k] for k in at_keys}}
               for core in range(NCORES)]
    res = run_bass_kernel_spmd(nc_at, in_maps, core_ids=list(range(NCORES)))
    shards = [r["y"] for r in res.results]

    shards = run_moe(shards)

    nc_f1 = build_l1ffn_launch(first=True)
    ff_keys = ("l1_wfcT", "l1_wprT", "l1_lnw", "l1_lnb")
    in_maps = [{"x": shards[core], **{k: com[k] for k in ff_keys}}
               for core in range(NCORES)]
    res = run_bass_kernel_spmd(nc_f1, in_maps, core_ids=list(range(NCORES)))
    shards = [r["y"] for r in res.results]
    qfc_s = [r["qfc"] for r in res.results]
    qpr_s = [r["qpr"] for r in res.results]

    nc_da = build_l1attn_launch()
    da_keys = ("dq", "dk1", "dk2", "dv", "do", "dll")
    in_maps = [{"x": shards[core], **{k: com[k] for k in da_keys}}
               for core in range(NCORES)]
    res = run_bass_kernel_spmd(nc_da, in_maps, core_ids=list(range(NCORES)))
    shards = [r["y"] for r in res.results]

    nc_f2 = build_l1ffn_launch(first=False)
    in_maps = [{"x": shards[core], **{k: com[k] for k in ff_keys},
                "qfc": qfc_s[core], "qpr": qpr_s[core]}
               for core in range(NCORES)]
    res = run_bass_kernel_spmd(nc_f2, in_maps, core_ids=list(range(NCORES)))
    out = np.concatenate([r["y"].reshape(BLOC, S, D) for r in res.results], axis=0)
    return out
